# version 11
# baseline (speedup 1.0000x reference)
import contextlib
import numpy as np

import concourse.bass as bass
import concourse.mybir as mybir
from concourse.bass_utils import run_bass_kernel_spmd

F32 = mybir.dt.float32
BF16 = mybir.dt.bfloat16
I8 = mybir.dt.int8
AF = mybir.ActivationFunctionType
ALU = mybir.AluOpType

D = 1024
DT = 8
SEQ = 2048
BC = 4
TOK = BC * SEQ
CH = 512
NCH = TOK // CH
NCORE = 8

FFN_H = 3072
HT = FFN_H // 128

LN_EPS = 1e-5
DEQ = 1.0 / 127.0



class SemC:
    def __init__(self, nc, es, name):
        self.h = es.enter_context(nc.semaphore(name))
        self.n = 0

    def inc(self, instr, k=1):
        instr.then_inc(self.h, k)
        self.n += k
        return instr

    def wait(self, eng, at=None):
        eng.wait_ge(self.h, self.n if at is None else at)


class Ctx:
    def __init__(self, nc, es, block):
        self.nc = nc
        self.es = es
        self.block = block
        self.LD = SemC(nc, es, "ld")
        self.GP = SemC(nc, es, "gp")
        self.SD = SemC(nc, es, "sd")
        self.TE = SemC(nc, es, "te")
        self.VE = SemC(nc, es, "ve")
        self.SE = SemC(nc, es, "se")
        self.pb = [es.enter_context(nc.psum_tensor(f"pb{i}", [128, 512], F32))
                   for i in range(8)]

    def uname(self, name):
        self._uid = getattr(self, "_uid", 0) + 1
        return f"{name}_u{self._uid}"

    def sb(self, name, shape, dtype):
        es = getattr(self, "cur_es", None) or self.es
        return es.enter_context(self.nc.sbuf_tensor(self.uname(name), shape, dtype))

    def vs(self, vector, instr):
        self.VE.inc(instr)
        self.VE.wait(vector)

    def sync(self, fn): self.block.sync(fn)
    def gps(self, fn): self.block.gpsimd(fn)
    def ten(self, fn): self.block.tensor(fn)
    def vec(self, fn): self.block.vector(fn)
    def sca(self, fn): self.block.scalar(fn)


def phase_barrier(ctx):
    sems = [ctx.LD, ctx.GP, ctx.SD, ctx.TE, ctx.VE, ctx.SE]

    def mk(engine):
        for s in sems:
            s.wait(engine)
    ctx.sync(mk)
    ctx.gps(mk)
    ctx.ten(mk)
    ctx.vec(mk)
    ctx.sca(mk)


def make_ctx(es, nc):
    block = es.enter_context(nc.Block())
    return Ctx(nc, es, block)


def x_dram_chunk(x_ap, c, cols=CH):
    v = x_ap.rearrange("(dt p) t -> p dt t", p=128)
    return v[:, :, c * cols:(c + 1) * cols]


class Shared:
    def __init__(self, ctx):
        self.ones_col16 = ctx.sb("ones_col16", [128, 1], BF16)
        self.ones_row16 = ctx.sb("ones_row16", [1, 128], BF16)
        self.ones_col_f32 = ctx.sb("ones_col_f32", [128, 1], F32)
        self.ones_row32 = ctx.sb("ones_row32", [1, 128], F32)
        self.tmpP = ctx.sb("sh_tmpP", [128, 256], F32)

        def _(vector):
            vector.memset(self.ones_col16[:], 1.0)
            vector.memset(self.ones_row16[:], 1.0)
            vector.memset(self.ones_row32[:], 1.0)
            ctx.VE.inc(vector.memset(self.ones_col_f32[:], 1.0))
        ctx.vec(_)


def make_ln_scratch(ctx, tag=""):
    return {
        "x16": ctx.sb(f"ln_x16{tag}", [128, DT, CH], BF16),
        "xsq": ctx.sb(f"ln_xsq{tag}", [128, DT, CH], BF16),
        "srow": ctx.sb(f"ln_srow{tag}", [1, CH], F32),
        "rstd": ctx.sb(f"ln_rstd{tag}", [1, CH], F32),
        "ab16": ctx.sb(f"ln_ab16{tag}", [1, 2 * CH], BF16),
    }


def emit_ln(ctx, sh, x_sb, ln16, scr, ncols=CH):
    x16, xsq = scr["x16"], scr["xsq"]
    srow, rstd, ab16 = scr["srow"], scr["rstd"], scr["ab16"]
    pa, pbk = ctx.pb[6], ctx.pb[7]

    def _(vector):
        ctx.TE.wait(vector)
        for j in range(DT):
            vector.tensor_copy(x16[:, j, :ncols], x_sb[:, j, :ncols])
        i = None
        for j in range(DT):
            i = vector.tensor_mul(xsq[:, j, :ncols], x16[:, j, :ncols],
                                  x16[:, j, :ncols])
        ctx.VE.inc(i)
    ctx.vec(_)

    def _(tensor):
        ctx.VE.wait(tensor)
        for j in range(DT):
            tensor.matmul(pa[0:1, 0:ncols], sh.ones_col16[:], x16[:, j, :ncols],
                          start=(j == 0), stop=(j == DT - 1))
        i = None
        for j in range(DT):
            i = tensor.matmul(pbk[0:1, 0:ncols], sh.ones_col16[:], xsq[:, j, :ncols],
                              start=(j == 0), stop=(j == DT - 1))
        ctx.TE.inc(i)
    ctx.ten(_)

    def _(vector):
        ctx.TE.wait(vector)
        vector.tensor_scalar_mul(srow[:, :ncols], pa[0:1, 0:ncols], 1.0 / D)
        vector.tensor_mul(rstd[:, :ncols], srow[:, :ncols], srow[:, :ncols])
        vector.scalar_tensor_tensor(
            rstd[:, :ncols], pbk[0:1, 0:ncols], 1.0 / D, rstd[:, :ncols],
            ALU.mult, ALU.subtract)
        ctx.VE.inc(vector.tensor_scalar_add(rstd[:, :ncols], rstd[:, :ncols],
                                            LN_EPS))
    ctx.vec(_)

    def _(scalar):
        ctx.VE.wait(scalar)
        ctx.SE.inc(scalar.activation(rstd[:, :ncols], rstd[:, :ncols], AF.Sqrt))
    ctx.sca(_)

    def _(vector):
        ctx.SE.wait(vector)
        ctx.VE.inc(vector.reciprocal(rstd[:, :ncols], rstd[:, :ncols]))
        ctx.VE.wait(vector)
        vector.tensor_copy(ab16[0:1, 0:ncols], rstd[:, :ncols])
        vector.tensor_mul(srow[:, :ncols], srow[:, :ncols], rstd[:, :ncols])
        ctx.VE.inc(vector.tensor_scalar_mul(ab16[0:1, CH:CH + ncols],
                                            srow[:, :ncols], -1.0))
    ctx.vec(_)

    def _(tensor):
        ctx.VE.wait(tensor)
        tensor.matmul(pa[:, 0:ncols], sh.ones_row16[:], ab16[0:1, 0:ncols],
                      start=True, stop=True)
        ctx.TE.inc(tensor.matmul(pbk[:, 0:ncols], sh.ones_row16[:],
                                 ab16[0:1, CH:CH + ncols], start=True, stop=True))
    ctx.ten(_)

    def _(vector):
        ctx.TE.wait(vector)
        i = None
        for j in range(DT):
            vector.tensor_tensor(ln16[:, j, :ncols], x16[:, j, :ncols],
                                 pa[:, 0:ncols], ALU.mult)
            i = vector.tensor_tensor(ln16[:, j, :ncols], ln16[:, j, :ncols],
                                     pbk[:, 0:ncols], ALU.add)
        ctx.VE.inc(i)
    ctx.vec(_)



def emit_l0ffn(ctx, sh, aps, scale, dbg=None):
    nc = ctx.nc
    x_in, x_out = aps["x_in"], aps["x_out"]

    wg16 = ctx.sb("wg16", [128, DT, FFN_H], BF16)
    wu16 = ctx.sb("wu16", [128, DT, FFN_H], BF16)
    wd16 = ctx.sb("wd16", [128, HT, D], BF16)
    with contextlib.ExitStack() as stage_es:
        st8 = stage_es.enter_context(nc.sbuf_tensor(ctx.uname("w_stage8"), [128, DT * FFN_H], I8))
        plan = [(aps["wg"], "(dt p) h -> p dt h", wg16, DT, FFN_H),
                (aps["wu"], "(dt p) h -> p dt h", wu16, DT, FFN_H),
                (aps["wd"], "(ht p) d -> p ht d", wd16, HT, D)]
        for wi, (w_ap, pat, w16, A, B) in enumerate(plan):
            stv = st8.ap().rearrange("p (a b) -> p a b", a=A)
            def _(sync, w_ap=w_ap, pat=pat, wi=wi, stv=stv):
                if wi > 0:
                    ctx.VE.wait(sync)
                ctx.LD.inc(sync.dma_start(out=stv, in_=w_ap.rearrange(pat, p=128)), 16)
            ctx.sync(_)

            def _(vector, w16=w16, stv=stv):
                ctx.LD.wait(vector)
                ctx.VE.inc(vector.tensor_scalar_mul(w16[:], stv, DEQ))
            ctx.vec(_)

    x_sb = [ctx.sb("ffn_x0", [128, DT, CH], F32)]
    ln16 = ctx.sb("ffn_ln16", [128, DT, CH], BF16)
    h16 = ctx.sb("ffn_h16", [128, HT, CH], BF16)
    silu_sb = ctx.sb("ffn_silu", [128, 4, CH], F32)
    scr = {
        "x16": h16[:, 8:16, :],
        "xsq": h16[:, 0:8, :],
        "srow": ctx.sb("ln_srow", [1, CH], F32),
        "rstd": ctx.sb("ln_rstd", [1, CH], F32),
        "ab16": ctx.sb("ln_ab16", [1, 2 * CH], BF16),
    }
    xb_release = {}

    for c in range(NCH):
        xb = x_sb[0]

        def _(sync, c=c, xb=xb):
            if c == 0:
                ctx.VE.wait(sync)
            else:
                ctx.VE.wait(sync, xb_release[c - 1])
                ctx.GP.wait(sync)
            ctx.LD.inc(sync.dma_start(out=xb[:], in_=x_dram_chunk(x_in, c)), 16)
        ctx.sync(_)
        ld_at = ctx.LD.n

        def _(vector, ld_at=ld_at):
            vector.wait_ge(ctx.LD.h, ld_at)
        ctx.vec(_)

        emit_ln(ctx, sh, xb, ln16, scr)

        se_rel = {}
        ve_rel = {}
        te_at = {}
        for j in range(HT):
            bg, bu = ctx.pb[2 * (j % 4)], ctx.pb[2 * (j % 4) + 1]

            def _(tensor, j=j, bg=bg, bu=bu):
                if j < 4:
                    ctx.VE.wait(tensor)
                    if j >= 2:
                        ctx.SE.wait(tensor)
                else:
                    ctx.SE.wait(tensor, se_rel[j - 4])
                    ctx.VE.wait(tensor, ve_rel[j - 4])
                for k in range(DT):
                    tensor.matmul(bg[:, 0:CH], wg16[:, k, j * 128:(j + 1) * 128],
                                  ln16[:, k, :], start=(k == 0), stop=(k == DT - 1))
                i = None
                for k in range(DT):
                    i = tensor.matmul(bu[:, 0:CH], wu16[:, k, j * 128:(j + 1) * 128],
                                      ln16[:, k, :], start=(k == 0), stop=(k == DT - 1))
                ctx.TE.inc(i)
            ctx.ten(_)
            te_at[j] = ctx.TE.n

            if j % 2 == 1:
                def _(scalar, j=j):
                    i = None
                    for jj in (j - 1, j):
                        scalar.wait_ge(ctx.TE.h, te_at[jj])
                        i = scalar.activation(silu_sb[:, jj % 4, :],
                                              ctx.pb[2 * (jj % 4)][:, 0:CH], AF.Silu)
                    ctx.SE.inc(i)
                ctx.sca(_)
                se_rel[j - 1] = se_rel[j] = ctx.SE.n

                def _(vector, j=j):
                    ctx.SE.wait(vector)
                    i = None
                    for jj in (j - 1, j):
                        i = vector.tensor_mul(h16[:, jj, :], silu_sb[:, jj % 4, :],
                                              ctx.pb[2 * (jj % 4) + 1][:, 0:CH])
                    ctx.VE.inc(i)
                ctx.vec(_)
                ve_rel[j - 1] = ve_rel[j] = ctx.VE.n


        dv_rel = {}
        for wave in range(4):
            bk = 2 + 2 * (wave % 2)

            def _(tensor, wave=wave, bk=bk):
                if wave == 0:
                    ctx.SE.wait(tensor, se_rel[HT - 1])
                    ctx.VE.wait(tensor)
                elif wave == 1:
                    pass
                else:
                    ctx.VE.wait(tensor, dv_rel[wave - 2])
                i = None
                for m in range(2):
                    mm = wave * 2 + m
                    for k in range(HT):
                        i = tensor.matmul(ctx.pb[bk + m][:, 0:CH],
                                          wd16[:, k, mm * 128:(mm + 1) * 128],
                                          h16[:, k, :],
                                          start=(k == 0), stop=(k == HT - 1))
                ctx.TE.inc(i)
            ctx.ten(_)

            def _(vector, wave=wave, bk=bk, xb=xb, c=c):
                ctx.TE.wait(vector)
                i = None
                for m in range(2):
                    mm = wave * 2 + m
                    i = vector.scalar_tensor_tensor(
                        xb[:, mm, :], ctx.pb[bk + m][:, 0:CH], scale,
                        xb[:, mm, :], ALU.mult, ALU.add)
                ctx.VE.inc(i)
            ctx.vec(_)
            dv_rel[wave] = ctx.VE.n
        xb_release[c] = ctx.VE.n

        def _(gpsimd, c=c, xb=xb):
            ctx.VE.wait(gpsimd)
            ctx.GP.inc(gpsimd.dma_start(out=x_dram_chunk(x_out, c), in_=xb[:]), 16)
        ctx.gps(_)

    def _(sync):
        ctx.GP.wait(sync)
    ctx.sync(_)



def _prep_int8(w_int32):
    return np.ascontiguousarray(w_int32.view(np.int8)[..., 0::4])


def build_l0ffn_program():
    nc = bass.Bass("TRN2", target_bir_lowering=False, debug=False)
    x_in = nc.dram_tensor("x", [D, TOK], F32, kind="ExternalInput").ap()
    wg = nc.dram_tensor("wg", [D, FFN_H], I8, kind="ExternalInput").ap()
    wu = nc.dram_tensor("wu", [D, FFN_H], I8, kind="ExternalInput").ap()
    wd = nc.dram_tensor("wd", [FFN_H, D], I8, kind="ExternalInput").ap()
    x_out = nc.dram_tensor("y", [D, TOK], F32, kind="ExternalOutput").ap()
    with contextlib.ExitStack() as es:
        ctx = make_ctx(es, nc)
        sh = Shared(ctx)
        emit_l0ffn(ctx, sh, {"x_in": x_in, "x_out": x_out,
                             "wg": wg, "wu": wu, "wd": wd}, 0.5)
    return nc



SCH = 512
NSC = SEQ // SCH
NF = 4


def _cmul_cols(ctx, v, outr, outi, ar, ai, br, bi, t1, t2):
    v.tensor_mul(t1, ar, br)
    ctx.vs(v, v.tensor_mul(t2, ai, bi))
    ctx.vs(v, v.tensor_tensor(outr, t1, t2, ALU.subtract))
    v.tensor_mul(t1, ar, bi)
    ctx.vs(v, v.tensor_mul(t2, ai, br))
    ctx.vs(v, v.tensor_tensor(outi, t1, t2, ALU.add))


def emit_l0attn(ctx, sh, aps, dbg=None):
    nc = ctx.nc
    x_in, x_out = aps["x_in"], aps["x_out"]

    bhat16 = ctx.sb("bhat16", [128, DT, 1025], BF16)
    chat16 = ctx.sb("chat16", [128, 9, D], BF16)
    Pr = ctx.sb("ssm_Pr", [128, NF, SCH], F32)
    Pi = ctx.sb("ssm_Pi", [128, NF, SCH], F32)
    Qc = ctx.sb("ssm_Qc", [128, 2, NF, NSC], F32)
    lamb = ctx.sb("ssm_lamb", [128, 2, NF], F32)
    Dcol = ctx.sb("ssm_D", [128, DT], F32)
    colt = ctx.sb("ssm_colt", [128, 8], F32)
    omdump = ctx.sb("ssm_omdump", [128, NF], F32)

    with contextlib.ExitStack() as pes:
        c_fret = pes.enter_context(nc.sbuf_tensor("c_fret", [128, 8, 513], F32))
        c_fimt = pes.enter_context(nc.sbuf_tensor("c_fimt", [128, 8, 513], F32))
        c_finr = pes.enter_context(nc.sbuf_tensor("c_finr", [128, 8, 513], F32))
        c_fini = pes.enter_context(nc.sbuf_tensor("c_fini", [128, 8, 513], F32))
        c_msin = pes.enter_context(nc.sbuf_tensor("c_msin", [128, 4, 513], F32))
        c_bw = pes.enter_context(nc.sbuf_tensor("c_bw", [128, 8, D], F32))
        c_cwt = pes.enter_context(nc.sbuf_tensor("c_cwt", [128, 8, D], F32))
        c_a = pes.enter_context(nc.sbuf_tensor("c_a", [128, 4], F32))

        def _(sync):
            for t, ap_, pat in [
                (c_fret, aps["fret"], "(nt p) f -> p nt f"),
                (c_fimt, aps["fimt"], "(nt p) f -> p nt f"),
                (c_finr, aps["finvre"], "(nt p) f -> p nt f"),
                (c_fini, aps["finvim"], "(nt p) f -> p nt f"),
                (c_msin, aps["msint"], "(kt p) f -> p kt f"),
                (c_bw, aps["bw"], "(nt p) d -> p nt d"),
                (c_cwt, aps["cwt"], "(nt p) d -> p nt d"),
            ]:
                ctx.LD.inc(sync.dma_start(out=t[:], in_=ap_.rearrange(pat, p=128)), 16)
            ctx.LD.inc(sync.dma_start(out=c_a[:], in_=aps["a512"]), 16)
            ctx.LD.inc(sync.dma_start(out=Dcol[:], in_=aps["dvec"]), 16)
        ctx.sync(_)

        def _(tensor):
            ctx.LD.wait(tensor)
            ctx.VE.wait(tensor)
            for fm in range(NF):
                i = None
                for kt in range(4):
                    i = tensor.matmul(ctx.pb[fm][:, 0:1],
                                      c_msin[:, kt, fm * 128:(fm + 1) * 128],
                                      c_a[:, kt:kt + 1],
                                      start=(kt == 0), stop=(kt == 3))
            ctx.TE.inc(i)
        ctx.ten(_)

        def _(vector):
            ctx.TE.wait(vector)
            t1 = colt[:, 0:1]
            t2 = colt[:, 1:2]
            t3 = colt[:, 2:3]
            for fm in range(NF):
                om = omdump[:, fm:fm + 1]
                ctx.vs(vector, vector.tensor_copy(om, ctx.pb[fm][:, 0:1]))
                ctx.vs(vector, vector.tensor_mul(t1, om, om))
                ctx.vs(vector, vector.tensor_scalar_add(t2, t1, 1.0))
                ctx.vs(vector, vector.reciprocal(t2, t2))
                ctx.vs(vector, vector.tensor_scalar_mul(t3, t1, -1.0))
                ctx.vs(vector, vector.tensor_scalar_add(t3, t3, 1.0))
                vector.tensor_mul(lamb[:, 0, fm:fm + 1], t3, t2)
                ctx.vs(vector, vector.tensor_scalar_mul(t3, om, 2.0))
                i = vector.tensor_mul(lamb[:, 1, fm:fm + 1], t3, t2)
            ctx.VE.inc(i)
        ctx.vec(_)

        def _(vector):
            t1 = colt[:, 0:1]
            t2 = colt[:, 1:2]
            qr = colt[:, 4:5]
            qi = colt[:, 5:6]
            rr = colt[:, 6:7]
            ri = colt[:, 7:8]
            i = None
            for fm in range(NF):
                lr = lamb[:, 0, fm:fm + 1]
                li = lamb[:, 1, fm:fm + 1]
                vector.memset(Pr[:, fm, 0:1], 1.0)
                vector.memset(Pi[:, fm, 0:1], 0.0)
                vector.tensor_copy(Pr[:, fm, 1:2], lr)
                ctx.vs(vector, vector.tensor_copy(Pi[:, fm, 1:2], li))
                n = 2
                while n < SCH:
                    _cmul_cols(ctx, vector, qr, qi, Pr[:, fm, n - 1:n],
                               Pi[:, fm, n - 1:n], lr, li, t1, t2)
                    blk = slice(0, n)
                    dst = slice(n, 2 * n)
                    tmp = sh.tmpP[:, 0:n]
                    ctx.vs(vector, vector.tensor_scalar_mul(tmp, Pi[:, fm, blk], qi))
                    vector.scalar_tensor_tensor(Pr[:, fm, dst], Pr[:, fm, blk], qr,
                                                tmp, ALU.mult, ALU.subtract)
                    ctx.vs(vector, vector.tensor_scalar_mul(tmp, Pr[:, fm, blk], qi))
                    ctx.vs(vector, vector.scalar_tensor_tensor(
                        Pi[:, fm, dst], Pi[:, fm, blk], qr, tmp, ALU.mult, ALU.add))
                    n *= 2
                _cmul_cols(ctx, vector, rr, ri, Pr[:, fm, SCH - 1:SCH],
                           Pi[:, fm, SCH - 1:SCH], lr, li, t1, t2)
                vector.memset(Qc[:, 0, fm, 0:1], 1.0)
                ctx.vs(vector, vector.memset(Qc[:, 1, fm, 0:1], 0.0))
                for cc in range(1, NSC):
                    _cmul_cols(ctx, vector, Qc[:, 0, fm, cc:cc + 1],
                               Qc[:, 1, fm, cc:cc + 1],
                               Qc[:, 0, fm, cc - 1:cc], Qc[:, 1, fm, cc - 1:cc],
                               rr, ri, t1, t2)
            ctx.VE.inc(vector.tensor_copy(colt[:, 0:1], omdump[:, 0:1]))
        ctx.vec(_)

        for dtm in range(DT):
            def _(tensor, dtm=dtm):
                if dtm == 0:
                    ctx.VE.wait(tensor)
                else:
                    ctx.VE.wait(tensor)
                i = None
                for nt in range(8):
                    tensor.matmul(ctx.pb[0][:, 0:512],
                                  c_bw[:, nt, dtm * 128:(dtm + 1) * 128],
                                  c_fret[:, nt, 0:512],
                                  start=(nt == 0), stop=(nt == 7))
                for nt in range(8):
                    tensor.matmul(ctx.pb[1][:, 0:512],
                                  c_bw[:, nt, dtm * 128:(dtm + 1) * 128],
                                  c_fimt[:, nt, 0:512],
                                  start=(nt == 0), stop=(nt == 7))
                for nt in range(8):
                    i = tensor.matmul(ctx.pb[2][:, 0:1],
                                      c_bw[:, nt, dtm * 128:(dtm + 1) * 128],
                                      c_fret[:, nt, 512:513],
                                      start=(nt == 0), stop=(nt == 7))
                ctx.TE.inc(i)
            ctx.ten(_)

            def _(vector, dtm=dtm):
                ctx.TE.wait(vector)
                vector.tensor_copy(bhat16[:, dtm, 0:512], ctx.pb[0][:, 0:512])
                vector.tensor_copy(bhat16[:, dtm, 512:1024], ctx.pb[1][:, 0:512])
                ctx.VE.inc(vector.tensor_copy(bhat16[:, dtm, 1024:1025],
                                              ctx.pb[2][:, 0:1]))
            ctx.vec(_)

        for chm in range(NF):
            for half in range(2):
                def _(tensor, chm=chm, half=half):
                    ctx.VE.wait(tensor)
                    i = None
                    for nt in range(8):
                        tensor.matmul(ctx.pb[0][:, 0:512],
                                      c_finr[:, nt, chm * 128:(chm + 1) * 128],
                                      c_cwt[:, nt, half * 512:(half + 1) * 512],
                                      start=(nt == 0), stop=(nt == 7))
                    for nt in range(8):
                        i = tensor.matmul(ctx.pb[1][:, 0:512],
                                          c_fini[:, nt, chm * 128:(chm + 1) * 128],
                                          c_cwt[:, nt, half * 512:(half + 1) * 512],
                                          start=(nt == 0), stop=(nt == 7))
                    ctx.TE.inc(i)
                ctx.ten(_)

                def _(vector, chm=chm, half=half):
                    ctx.TE.wait(vector)
                    hs = slice(half * 512, (half + 1) * 512)
                    vector.tensor_scalar_mul(chat16[:, chm, hs], ctx.pb[0][:, 0:512], 2.0)
                    i = vector.tensor_scalar_mul(chat16[:, 4 + chm, hs],
                                                 ctx.pb[1][:, 0:512], -2.0)
                    ctx.VE.inc(i)
                ctx.vec(_)

        def _(vector):
            vector.tensor_scalar_mul(chat16[0:1, 0, :], chat16[0:1, 0, :], 0.5)
            ctx.VE.inc(vector.tensor_scalar_mul(chat16[0:1, 4, :], chat16[0:1, 4, :], 0.5))
        ctx.vec(_)

        for half in range(2):
            def _(tensor, half=half):
                ctx.VE.wait(tensor)
                i = None
                for nt in range(8):
                    i = tensor.matmul(ctx.pb[2][0:1, 0:512],
                                      c_finr[:, nt, 512:513],
                                      c_cwt[:, nt, half * 512:(half + 1) * 512],
                                      start=(nt == 0), stop=(nt == 7))
                ctx.TE.inc(i)
            ctx.ten(_)

            def _(vector, half=half):
                ctx.TE.wait(vector)
                ctx.VE.inc(vector.tensor_copy(
                    chat16[0:1, 8, half * 512:(half + 1) * 512], ctx.pb[2][0:1, 0:512]))
            ctx.vec(_)
    if dbg is not None:
        def _(gpsimd):
            ctx.VE.wait(gpsimd)
            ctx.GP.inc(gpsimd.dma_start(out=dbg["lamb"], in_=lamb.ap().rearrange("p a b -> p (a b)")), 16)
            ctx.GP.inc(gpsimd.dma_start(out=dbg["omega"], in_=omdump[:]), 16)
            ctx.GP.inc(gpsimd.dma_start(out=dbg["P0"], in_=Pr[:, 0, :]), 16)
            ctx.GP.inc(gpsimd.dma_start(out=dbg["P0i"], in_=Pi[:, 0, :]), 16)
            ctx.GP.inc(gpsimd.dma_start(out=dbg["qc"], in_=Qc.ap().rearrange("p a b c -> p (a b c)")), 16)
            ctx.GP.inc(gpsimd.dma_start(out=dbg["bhat0"], in_=bhat16[:, 0, :]), 16)
            ctx.GP.inc(gpsimd.dma_start(out=dbg["chat0"], in_=chat16[:, 0, :]), 16)
            ctx.GP.inc(gpsimd.dma_start(out=dbg["chat4"], in_=chat16[:, 4, :]), 16)
            ctx.GP.inc(gpsimd.dma_start(out=dbg["chat8"], in_=chat16[0:1, 8, :]), 16)
        ctx.gps(_)

    xb = ctx.sb("ssm_x", [128, DT, SCH], F32)
    ln16 = ctx.sb("ssm_ln16", [128, DT, SCH], BF16)
    scr = make_ln_scratch(ctx, "ssm")
    hch16 = ctx.sb("ssm_h16", [128, 9, SCH], BF16)
    A2 = ctx.sb("ssm_A", [128, 2, SCH], F32)
    S2 = ctx.sb("ssm_S", [128, 2, SCH], F32)
    bp2 = ctx.sb("ssm_bp", [128, 2, SCH], F32)
    tmpw = ctx.sb("ssm_tmpw", [128, SCH], F32)
    carry = ctx.sb("ssm_carry", [128, 2, NF], F32)
    cw_rel = {}
    ssm_ln_rel = {0: 0}
    car512 = ctx.sb("ssm_car512", [1, 2], F32)
    h512 = ctx.sb("ssm_h512", [1, SCH], F32)
    xb_release = {}

    for b in range(BC):
        def _(vector, b=b):
            if b > 0:
                ctx.TE.wait(vector)
            vector.memset(carry[:], 0.0)
            ctx.VE.inc(vector.memset(car512[:], 0.0))
        ctx.vec(_)

        for cc in range(NSC):
            c = b * NSC + cc

            def _(sync, c=c):
                if c == 0:
                    ctx.VE.wait(sync)
                    ctx.TE.wait(sync)
                else:
                    ctx.VE.wait(sync, xb_release[c - 1])
                    ctx.GP.wait(sync)
                ctx.LD.inc(sync.dma_start(out=xb[:], in_=x_dram_chunk(x_in, c)), 16)
            ctx.sync(_)
            ld_at = ctx.LD.n

            def _(vector, ld_at=ld_at):
                vector.wait_ge(ctx.LD.h, ld_at)
            ctx.vec(_)

            emit_ln(ctx, sh, xb, ln16, scr, ncols=SCH)
            ssm_ln_rel[0] = ctx.VE.n

            def _(tensor):
                ctx.VE.wait(tensor)
                i = None
                for dt_ in range(DT):
                    i = tensor.matmul(ctx.pb[0][0:1, 0:SCH],
                                      bhat16[:, dt_, 1024:1025],
                                      ln16[:, dt_, :],
                                      start=(dt_ == 0), stop=(dt_ == DT - 1))
                ctx.TE.inc(i)
            ctx.ten(_)

            def _(vector, cc=cc):
                ctx.TE.wait(vector)
                ctx.vs(vector, vector.tensor_tensor_scan(
                    h512[:], sh.ones_col_f32[0:1, 0:1].broadcast_to((1, SCH)),
                    ctx.pb[0][0:1, 0:SCH], car512[0:1, 0:1], ALU.mult, ALU.add))
                vector.tensor_copy(car512[0:1, 0:1], h512[0:1, SCH - 1:SCH])
                ctx.VE.inc(vector.tensor_copy(hch16[0:1, 8, :], h512[:]))
            ctx.vec(_)
            f512_rel = ctx.VE.n

            scan_rel = {}
            bte_at = {}
            for fm in range(NF):
                def _(tensor, fm=fm):
                    if fm == 0:
                        ctx.VE.wait(tensor, f512_rel)
                    if cw_rel.get(fm) is not None:
                        ctx.VE.wait(tensor, max(cw_rel[fm], ssm_ln_rel[0]))
                    i = None
                    for dt_ in range(DT):
                        tensor.matmul(ctx.pb[2 * fm][:, 0:SCH],
                                      bhat16[:, dt_, fm * 128:(fm + 1) * 128],
                                      ln16[:, dt_, :],
                                      start=(dt_ == 0), stop=(dt_ == DT - 1))
                    for dt_ in range(DT):
                        i = tensor.matmul(ctx.pb[2 * fm + 1][:, 0:SCH],
                                          bhat16[:, dt_, 512 + fm * 128:512 + (fm + 1) * 128],
                                          ln16[:, dt_, :],
                                          start=(dt_ == 0), stop=(dt_ == DT - 1))
                    ctx.TE.inc(i)
                ctx.ten(_)
                bte_at[fm] = ctx.TE.n

                def _(vector, fm=fm, cc=cc):
                    vector.wait_ge(ctx.TE.h, bte_at[fm])
                    br = ctx.pb[2 * fm][:, 0:SCH]
                    bi = ctx.pb[2 * fm + 1][:, 0:SCH]
                    qr = Qc[:, 0, fm, cc:cc + 1]
                    qi = Qc[:, 1, fm, cc:cc + 1]
                    vector.tensor_scalar_mul(tmpw[:], Pi[:, fm, :], qi)
                    vector.scalar_tensor_tensor(A2[:, 0, :], Pr[:, fm, :], qr,
                                                tmpw[:], ALU.mult, ALU.subtract)
                    vector.tensor_scalar_mul(tmpw[:], Pr[:, fm, :], qi)
                    vector.scalar_tensor_tensor(A2[:, 1, :], Pi[:, fm, :], qr,
                                                tmpw[:], ALU.mult, ALU.add)
                    vector.tensor_mul(bp2[:, 0, :], A2[:, 0, :], br)
                    vector.tensor_mul(tmpw[:], A2[:, 1, :], bi)
                    vector.tensor_tensor(bp2[:, 0, :], bp2[:, 0, :], tmpw[:],
                                         ALU.subtract)
                    vector.tensor_mul(bp2[:, 1, :], A2[:, 0, :], bi)
                    vector.tensor_mul(tmpw[:], A2[:, 1, :], br)
                    vector.tensor_tensor(bp2[:, 1, :], bp2[:, 1, :], tmpw[:], ALU.add)
                    ones_b = sh.ones_col_f32[:, 0:1].broadcast_to((128, SCH))
                    vector.tensor_tensor_scan(S2[:, 0, :], ones_b, bp2[:, 0, :],
                                              carry[:, 0, fm:fm + 1], ALU.mult, ALU.add)
                    ctx.vs(vector, vector.tensor_tensor_scan(
                        S2[:, 1, :], ones_b, bp2[:, 1, :],
                        carry[:, 1, fm:fm + 1], ALU.mult, ALU.add))
                    vector.tensor_copy(carry[:, 0, fm:fm + 1], S2[:, 0, SCH - 1:SCH])
                    vector.tensor_copy(carry[:, 1, fm:fm + 1], S2[:, 1, SCH - 1:SCH])
                    vector.tensor_mul(tmpw[:], A2[:, 0, :], S2[:, 0, :])
                    vector.tensor_mul(bp2[:, 0, :], A2[:, 1, :], S2[:, 1, :])
                    vector.tensor_tensor(hch16[:, fm, :], tmpw[:], bp2[:, 0, :],
                                         ALU.add)
                    vector.tensor_mul(tmpw[:], A2[:, 0, :], S2[:, 1, :])
                    vector.tensor_mul(bp2[:, 1, :], A2[:, 1, :], S2[:, 0, :])
                    i = vector.tensor_tensor(hch16[:, 4 + fm, :], tmpw[:],
                                             bp2[:, 1, :], ALU.subtract)
                    ctx.VE.inc(i)
                ctx.vec(_)
                scan_rel[fm] = ctx.VE.n

            for w in range(NF):
                def _(tensor, w=w):
                    ctx.VE.wait(tensor, scan_rel[3])
                    i = None
                    for m in range(2):
                        md = w * 2 + m
                        for chm in range(8):
                            tensor.matmul(ctx.pb[2 * w + m][:, 0:SCH],
                                          chat16[:, chm, md * 128:(md + 1) * 128],
                                          hch16[:, chm, :],
                                          start=(chm == 0), stop=False)
                        i = tensor.matmul(ctx.pb[2 * w + m][:, 0:SCH],
                                          chat16[0:1, 8, md * 128:(md + 1) * 128],
                                          hch16[0:1, 8, :],
                                          start=False, stop=True)
                    ctx.TE.inc(i)
                ctx.ten(_)
                cte_at = ctx.TE.n

                def _(vector, w=w, c=c, cte_at=cte_at):
                    vector.wait_ge(ctx.TE.h, cte_at)
                    i = None
                    for m in range(2):
                        md = w * 2 + m
                        vector.scalar_tensor_tensor(
                            tmpw[:], ln16[:, md, :], Dcol[:, md:md + 1],
                            xb[:, md, :], ALU.mult, ALU.add)
                        i = vector.tensor_tensor(xb[:, md, :],
                                                 ctx.pb[2 * w + m][:, 0:SCH],
                                                 tmpw[:], ALU.add)
                    ctx.VE.inc(i)
                ctx.vec(_)
                cw_rel[w] = ctx.VE.n


            xb_release[c] = ctx.VE.n

            def _(gpsimd, c=c):
                ctx.VE.wait(gpsimd)
                ctx.GP.inc(gpsimd.dma_start(out=x_dram_chunk(x_out, c), in_=xb[:]), 16)
            ctx.gps(_)

    def _(sync):
        ctx.GP.wait(sync)
    ctx.sync(_)


def make_ssm_consts():
    N = 1024
    f = np.arange(513)
    n = np.arange(1024)
    ang = 2.0 * np.pi * np.outer(n, f) / N
    fret = np.cos(ang).astype(np.float32)
    fimt = (-np.sin(ang)).astype(np.float32)
    finvre = (np.cos(ang) / N).astype(np.float32)
    finvim = (np.sin(ang) / N).astype(np.float32)
    k = np.arange(1, 513)
    msint = (-2.0 * np.sin(2.0 * np.pi * np.outer(k, f) / N)).astype(np.float32)
    msint[511, :] = 0.0
    return dict(fret=fret, fimt=fimt, finvre=finvre, finvim=finvim, msint=msint)


def build_l0attn_program():
    nc = bass.Bass("TRN2", target_bir_lowering=False, debug=False)
    x_in = nc.dram_tensor("x", [D, TOK], F32, kind="ExternalInput").ap()
    bw = nc.dram_tensor("bw", [D, D], F32, kind="ExternalInput").ap()
    cwt = nc.dram_tensor("cwt", [D, D], F32, kind="ExternalInput").ap()
    a512 = nc.dram_tensor("a512", [128, 4], F32, kind="ExternalInput").ap()
    dvec = nc.dram_tensor("dvec", [128, 8], F32, kind="ExternalInput").ap()
    fret = nc.dram_tensor("fret", [D, 513], F32, kind="ExternalInput").ap()
    fimt = nc.dram_tensor("fimt", [D, 513], F32, kind="ExternalInput").ap()
    finvre = nc.dram_tensor("finvre", [D, 513], F32, kind="ExternalInput").ap()
    finvim = nc.dram_tensor("finvim", [D, 513], F32, kind="ExternalInput").ap()
    msint = nc.dram_tensor("msint", [512, 513], F32, kind="ExternalInput").ap()
    x_out = nc.dram_tensor("y", [D, TOK], F32, kind="ExternalOutput").ap()
    with contextlib.ExitStack() as es:
        ctx = make_ctx(es, nc)
        sh = Shared(ctx)
        emit_l0attn(ctx, sh, dict(x_in=x_in, x_out=x_out, bw=bw, cwt=cwt,
                                  a512=a512, dvec=dvec, fret=fret, fimt=fimt,
                                  finvre=finvre, finvim=finvim, msint=msint))
    return nc



def emit_l1attn(ctx, sh, aps, dbg=None):
    nc = ctx.nc
    x_in, x_out = aps["x_in"], aps["x_out"]

    ln16 = ctx.sb("wkv_ln16", [128, DT, SEQ], BF16)
    yf16 = ctx.sb("wkv_yf", [128, DT, SEQ], BF16)
    yb16 = ctx.sb("wkv_yb", [128, DT, SEQ], BF16)
    lam2 = ctx.sb("wkv_lam", [128, 2, DT], F32)
    ucol2 = ctx.sb("wkv_u", [128, 2, DT], F32)
    carN = ctx.sb("wkv_carN", [128, DT], F32)
    carD = ctx.sb("wkv_carD", [128, DT], F32)
    ek2 = ctx.sb("wkv_ek", [128, 2, CH], F32)
    eu2 = ctx.sb("wkv_eu", [128, 2, CH], F32)
    ekv = ctx.sb("wkv_ekv", [128, CH], F32)
    euv = ctx.sb("wkv_euv", [128, CH], F32)
    num = ctx.sb("wkv_num", [128, CH + 1], F32)
    den = ctx.sb("wkv_den", [128, CH + 1], F32)
    nsum2 = ctx.sb("wkv_nsum", [128, 2, CH], F32)
    dsum2 = ctx.sb("wkv_dsum", [128, 2, CH], F32)
    yv = ctx.sb("wkv_y", [128, CH], F32)
    sig = ctx.sb("wkv_sig", [128, 4, CH], F32)
    wkv_xc = ctx.sb("wkv_xc", [128, DT, CH], F32)
    minus1 = ctx.sb("wkv_m1", [128, 1], F32)
    pend = [None]
    rev0 = (slice(None), slice(None, None, -1))

    def emit_y(p):
        j, dr, tc, se_at = p["j"], p["dr"], p["tc"], p["se_at"]

        def _(vector):
            vector.wait_ge(ctx.SE.h, se_at)
            vector.tensor_mul(yv[:], nsum2[:, j % 2, :], dsum2[:, j % 2, :])
            ytile = (yf16 if dr == 0 else yb16)[:, j, tc:tc + CH]
            if dr == 0:
                i = vector.tensor_mul(ytile, yv[:], sig[:, j % 4, :])
            else:
                i = vector.tensor_mul(ytile, yv[rev0], sig[:, j % 4, :])
            ctx.VE.inc(i)
        ctx.vec(_)
    scr = {
        "x16": yb16[:, :, 0:CH],
        "xsq": yb16[:, :, CH:2 * CH],
        "srow": ctx.sb("wkv_srow", [1, CH], F32),
        "rstd": ctx.sb("wkv_rstd", [1, CH], F32),
        "ab16": ctx.sb("wkv_ab16", [1, 2 * CH], BF16),
    }

    def _(sync):
        ctx.LD.inc(sync.dma_start(out=ucol2[:, 0, :], in_=aps["tf"]), 16)
        ctx.LD.inc(sync.dma_start(out=ucol2[:, 1, :], in_=aps["tfr"]), 16)
        ctx.LD.inc(sync.dma_start(out=lam2[:, 0, :], in_=aps["td"]), 16)
        ctx.LD.inc(sync.dma_start(out=lam2[:, 1, :], in_=aps["tdr"]), 16)
    ctx.sync(_)

    def _(scalar):
        ctx.LD.wait(scalar)
        ctx.SE.inc(scalar.activation(lam2[:, :, :], lam2[:, :, :], AF.Exp))
    ctx.sca(_)

    def _(vector):
        ctx.SE.wait(vector)
        vector.memset(minus1[:], -1.0)
        ctx.VE.inc(vector.tensor_scalar_mul(lam2[:, :, :], lam2[:, :, :], -1.0))
        ctx.VE.wait(vector)
    ctx.vec(_)

    def _(scalar):
        ctx.VE.wait(scalar)
        ctx.SE.inc(scalar.activation(lam2[:, :, :], lam2[:, :, :], AF.Exp))
    ctx.sca(_)

    xb_release = {}
    for b in range(BC):
        for dr in range(2):
            with contextlib.ExitStack() as hes:
                rkvh = hes.enter_context(
                    nc.sbuf_tensor(f"rkvh_{b}_{dr}", [128, DT, 3072], BF16))
                stg = hes.enter_context(
                    nc.sbuf_tensor(f"rkvstg_{b}_{dr}", [128, DT, 128], F32))
                for sl in range(24):
                    def _(sync, sl=sl, b=b, dr=dr):
                        ctx.VE.wait(sync)
                        if sl == 0:
                            ctx.TE.wait(sync)
                            ctx.GP.wait(sync)
                        ctx.LD.inc(sync.dma_start(
                            out=stg[:],
                            in_=aps["rkvt"].rearrange("(dt p) h -> p dt h", p=128)[
                                :, :, dr * 3072 + sl * 128:dr * 3072 + (sl + 1) * 128]), 16)
                    ctx.sync(_)

                    def _(vector, sl=sl):
                        ctx.LD.wait(vector)
                        ctx.VE.inc(vector.tensor_copy(
                            rkvh[:, :, sl * 128:(sl + 1) * 128], stg[:]))
                    ctx.vec(_)

                def _(vector):
                    vector.memset(carN[:], 0.0)
                    ctx.VE.inc(vector.memset(carD[:], 0.0))
                ctx.vec(_)

                lam_j = lam2[:, dr, :]
                u_j = ucol2[:, dr, :]
                jse_rel = {}
                jve_rel = {}
                ln_rel = {0: 0}
                for ci in range(NSC):
                    c = (NSC - 1 - ci) if dr else ci
                    tc = c * CH
                    gc = b * NSC + c

                    if dr == 0:
                        def _(sync, gc=gc):
                            if gc > 0:
                                ctx.VE.wait(sync, xb_release.get(gc - 1, 0))
                            ctx.LD.inc(sync.dma_start(
                                out=wkv_xc[:], in_=x_dram_chunk(x_in, gc)), 16)
                        ctx.sync(_)
                        ld_at = ctx.LD.n

                        def _(vector, ld_at=ld_at):
                            vector.wait_ge(ctx.LD.h, ld_at)
                        ctx.vec(_)
                        emit_ln(ctx, sh, wkv_xc, ln16[:, :, tc:tc + CH], scr)
                        xb_release[gc] = ctx.VE.n
                        ln_rel[0] = ctx.VE.n

                    for j in range(DT):
                        b0 = 3 * (j % 2)
                        pr, pk, pv = ctx.pb[b0], ctx.pb[b0 + 1], ctx.pb[b0 + 2]
                        J = ci * DT + j

                        def _(tensor, j=j, pr=pr, pk=pk, pv=pv, tc=tc, J=J):
                            if J >= 2:
                                ctx.SE.wait(tensor, jse_rel[J - 2])
                                ctx.VE.wait(tensor, max(jve_rel[J - 2], ln_rel[0]))
                            else:
                                ctx.SE.wait(tensor)
                                ctx.VE.wait(tensor)
                            for dt_ in range(DT):
                                tensor.matmul(pr[:, 0:CH],
                                              rkvh[:, dt_, j * 128:(j + 1) * 128],
                                              ln16[:, dt_, tc:tc + CH],
                                              start=(dt_ == 0), stop=(dt_ == DT - 1))
                            for dt_ in range(DT):
                                tensor.matmul(pk[:, 0:CH],
                                              rkvh[:, dt_, 1024 + j * 128:1024 + (j + 1) * 128],
                                              ln16[:, dt_, tc:tc + CH],
                                              start=(dt_ == 0), stop=(dt_ == DT - 1))
                            i = None
                            for dt_ in range(DT):
                                i = tensor.matmul(pv[:, 0:CH],
                                                  rkvh[:, dt_, 2048 + j * 128:2048 + (j + 1) * 128],
                                                  ln16[:, dt_, tc:tc + CH],
                                                  start=(dt_ == 0), stop=(dt_ == DT - 1))
                            ctx.TE.inc(i)
                        ctx.ten(_)

                        def _(scalar, j=j, pr=pr, pk=pk):
                            ctx.TE.wait(scalar)
                            scalar.activation(sig[:, j % 4, :], pr[:, 0:CH], AF.Sigmoid)
                            scalar.activation(eu2[:, j % 2, :], pk[:, 0:CH], AF.Exp,
                                              bias=u_j[:, j:j + 1])
                            ctx.SE.inc(scalar.activation(ek2[:, j % 2, :],
                                                         pk[:, 0:CH], AF.Exp))
                        ctx.sca(_)
                        jse_rel[J] = ctx.SE.n

                        if pend[0] is not None:
                            emit_y(pend[0])
                            pend[0] = None

                        rev = (slice(None), slice(None, None, -1))

                        def _(vector, j=j, pv=pv, dr=dr):
                            ctx.SE.wait(vector)
                            eu = eu2[:, j % 2, :]
                            ek = ek2[:, j % 2, :]
                            nsum = nsum2[:, j % 2, :]
                            dsum = dsum2[:, j % 2, :]
                            vector.tensor_mul(ekv[:], ek, pv[:, 0:CH])
                            ctx.vs(vector, vector.tensor_mul(euv[:], eu, pv[:, 0:CH]))
                            vector.tensor_copy(num[:, 0:1], carN[:, j:j + 1])
                            vector.tensor_copy(den[:, 0:1], carD[:, j:j + 1])
                            lb = lam_j[:, j:j + 1].broadcast_to((128, CH))
                            if dr == 0:
                                vector.tensor_tensor_scan(
                                    num[:, 1:CH + 1], lb, ekv[:], carN[:, j:j + 1],
                                    ALU.mult, ALU.add)
                                ctx.vs(vector, vector.tensor_tensor_scan(
                                    den[:, 1:CH + 1], lb, ek, carD[:, j:j + 1],
                                    ALU.mult, ALU.add))
                                vector.tensor_tensor(nsum[:], num[:, 0:CH],
                                                     euv[:], ALU.add)
                                vector.tensor_tensor(dsum[:], den[:, 0:CH],
                                                     eu, ALU.add)
                                vector.tensor_copy(carN[:, j:j + 1], num[:, CH:CH + 1])
                                i = vector.tensor_copy(carD[:, j:j + 1], den[:, CH:CH + 1])
                            else:
                                vector.tensor_tensor_scan(
                                    num[:, 1:CH + 1], lb, ekv[rev], carN[:, j:j + 1],
                                    ALU.mult, ALU.add)
                                ctx.vs(vector, vector.tensor_tensor_scan(
                                    den[:, 1:CH + 1], lb, ek[rev], carD[:, j:j + 1],
                                    ALU.mult, ALU.add))
                                vector.tensor_tensor(nsum[:], num[:, 0:CH],
                                                     euv[:, CH - 1::-1], ALU.add)
                                vector.tensor_tensor(dsum[:], den[:, 0:CH],
                                                     eu[:, CH - 1::-1], ALU.add)
                                vector.tensor_copy(carN[:, j:j + 1], num[:, CH:CH + 1])
                                i = vector.tensor_copy(carD[:, j:j + 1], den[:, CH:CH + 1])
                            ctx.VE.inc(i)
                        ctx.vec(_)
                        jve_rel[J] = ctx.VE.n

                        def _(scalar, j=j):
                            ctx.VE.wait(scalar)
                            ds = dsum2[:, j % 2, :]
                            scalar.activation(ds, ds, AF.Ln)
                            ctx.SE.inc(scalar.activation(ds, ds, AF.Exp,
                                                         scale=minus1[:, 0:1]))
                        ctx.sca(_)
                        pend[0] = dict(j=j, dr=dr, tc=tc, se_at=ctx.SE.n)


                if pend[0] is not None:
                    emit_y(pend[0])
                    pend[0] = None

        with contextlib.ExitStack() as oes:
            ow16 = oes.enter_context(nc.sbuf_tensor(f"ow16_{b}", [128, 16, D], BF16))
            ostg = oes.enter_context(nc.sbuf_tensor(f"ostg_{b}", [128, 16, 128], F32))
            xo = wkv_xc
            for sl in range(8):
                def _(sync, sl=sl):
                    ctx.VE.wait(sync)
                    if sl == 0:
                        ctx.TE.wait(sync)
                        ctx.GP.wait(sync)
                    ctx.LD.inc(sync.dma_start(
                        out=ostg[:],
                        in_=aps["outwt"].rearrange("(kt p) d -> p kt d", p=128)[
                            :, :, sl * 128:(sl + 1) * 128]), 16)
                ctx.sync(_)

                def _(vector, sl=sl):
                    ctx.LD.wait(vector)
                    ctx.VE.inc(vector.tensor_copy(
                        ow16[:, :, sl * 128:(sl + 1) * 128], ostg[:]))
                ctx.vec(_)

            for cc in range(NSC):
                gc = b * NSC + cc
                tc = cc * CH

                def _(sync, gc=gc):
                    ctx.VE.wait(sync)
                    ctx.GP.wait(sync)
                    ctx.LD.inc(sync.dma_start(out=xo[:], in_=x_dram_chunk(x_in, gc)), 16)
                ctx.sync(_)
                ld_at = ctx.LD.n

                def _(tensor, tc=tc, ld_at=ld_at):
                    ctx.VE.wait(tensor)
                    i = None
                    for md in range(DT):
                        for kt in range(DT):
                            tensor.matmul(ctx.pb[md][:, 0:CH],
                                          ow16[:, kt, md * 128:(md + 1) * 128],
                                          yf16[:, kt, tc:tc + CH],
                                          start=(kt == 0), stop=False)
                        for kt in range(DT):
                            i = tensor.matmul(ctx.pb[md][:, 0:CH],
                                              ow16[:, 8 + kt, md * 128:(md + 1) * 128],
                                              yb16[:, kt, tc:tc + CH],
                                              start=False, stop=(kt == DT - 1))
                    ctx.TE.inc(i)
                ctx.ten(_)

                def _(vector, ld_at=ld_at):
                    vector.wait_ge(ctx.LD.h, ld_at)
                    ctx.TE.wait(vector)
                    i = None
                    for md in range(DT):
                        i = vector.tensor_tensor(xo[:, md, :], ctx.pb[md][:, 0:CH],
                                                 xo[:, md, :], ALU.add)
                    ctx.VE.inc(i)
                ctx.vec(_)

                def _(gpsimd, gc=gc):
                    ctx.VE.wait(gpsimd)
                    ctx.GP.inc(gpsimd.dma_start(out=x_dram_chunk(x_out, gc),
                                                in_=xo[:]), 16)
                ctx.gps(_)

    def _(sync):
        ctx.GP.wait(sync)
    ctx.sync(_)


def build_l1attn_program():
    nc = bass.Bass("TRN2", target_bir_lowering=False, debug=False)
    x_in = nc.dram_tensor("x", [D, TOK], F32, kind="ExternalInput").ap()
    rkvt = nc.dram_tensor("rkvt", [D, 6144], F32, kind="ExternalInput").ap()
    outwt = nc.dram_tensor("outwt", [2048, D], F32, kind="ExternalInput").ap()
    td = nc.dram_tensor("td", [128, 8], F32, kind="ExternalInput").ap()
    tf = nc.dram_tensor("tf", [128, 8], F32, kind="ExternalInput").ap()
    tdr = nc.dram_tensor("tdr", [128, 8], F32, kind="ExternalInput").ap()
    tfr = nc.dram_tensor("tfr", [128, 8], F32, kind="ExternalInput").ap()
    x_out = nc.dram_tensor("y", [D, TOK], F32, kind="ExternalOutput").ap()
    with contextlib.ExitStack() as es:
        ctx = make_ctx(es, nc)
        sh = Shared(ctx)
        emit_l1attn(ctx, sh, dict(x_in=x_in, x_out=x_out, rkvt=rkvt, outwt=outwt,
                                  td=td, tf=tf, tdr=tdr, tfr=tfr))
    return nc



MG = 1024
NMG = TOK // MG
INTER = 2048
NMP = 16


def emit_moe_preconvert(ctx, aps):
    nc = ctx.nc
    w1s, w2s = aps["w1s"], aps["w2s"]
    with contextlib.ExitStack() as pes:
        stg = [pes.enter_context(nc.sbuf_tensor(f"moepc_stg{i}", [128, 2, DT, 128], F32))
               for i in range(3)]
        cv = [pes.enter_context(nc.sbuf_tensor(f"moepc_cv{i}", [128, 2, DT, 128], BF16))
              for i in range(3)]
        stg2 = [pes.enter_context(nc.sbuf_tensor(f"moepc_stg2{i}", [128, 4, 1024], F32))
                for i in range(2)]
        cv2 = [pes.enter_context(nc.sbuf_tensor(f"moepc_cv2{i}", [128, 4, 1024], BF16))
               for i in range(2)]
        gp_rel = {}
        ve_relp = {}
        it = 0
        for e in range(8):
            for mp in range(NMP):
                s_, c_ = stg[it % 3], cv[it % 3]

                def _(sync, e=e, mp=mp, s_=s_, it=it):
                    if it >= 3:
                        ctx.GP.wait(sync, gp_rel[it - 3])
                        ctx.VE.wait(sync, ve_relp[it - 3])
                    src_ = aps["w1t"][e].rearrange("(dt p) c -> p dt c", p=128)
                    g = src_[:, :, mp * 128:(mp + 1) * 128].unsqueeze(1)
                    u = src_[:, :, INTER + mp * 128:INTER + (mp + 1) * 128].unsqueeze(1)
                    ctx.LD.inc(sync.dma_start(out=s_[:, 0:1, :, :], in_=g), 16)
                    ctx.LD.inc(sync.dma_start(out=s_[:, 1:2, :, :], in_=u), 16)
                ctx.sync(_)

                def _(vector, s_=s_, c_=c_):
                    ctx.LD.wait(vector)
                    ctx.VE.inc(vector.tensor_copy(c_[:], s_[:]))
                ctx.vec(_)
                ve_relp[it] = ctx.VE.n

                def _(gpsimd, e=e, mp=mp, c_=c_):
                    ctx.VE.wait(gpsimd)
                    ctx.GP.inc(gpsimd.dma_start(
                        out=w1s[e, mp],
                        in_=c_.ap().rearrange("p a b c -> p (a b c)")), 16)
                ctx.gps(_)
                gp_rel[it] = ctx.GP.n
                it += 1
        it2 = 0
        gp_rel2 = {}
        ve_rel2 = {}
        for e in range(8):
            for half in range(4):
                s_, c_ = stg2[it2 % 2], cv2[it2 % 2]

                def _(sync, e=e, half=half, s_=s_, it2=it2):
                    if it2 >= 2:
                        ctx.GP.wait(sync, gp_rel2[it2 - 2])
                        ctx.VE.wait(sync, ve_rel2[it2 - 2])
                    src_ = aps["w2t"][e].rearrange("(kt p) d -> p kt d", p=128)
                    ctx.LD.inc(sync.dma_start(
                        out=s_[:], in_=src_[:, half * 4:(half + 1) * 4, :]), 16)
                ctx.sync(_)

                def _(vector, s_=s_, c_=c_):
                    ctx.LD.wait(vector)
                    ctx.VE.inc(vector.tensor_copy(c_[:], s_[:]))
                ctx.vec(_)
                ve_rel2[it2] = ctx.VE.n

                def _(gpsimd, e=e, half=half, c_=c_):
                    ctx.VE.wait(gpsimd)
                    ctx.GP.inc(gpsimd.dma_start(
                        out=w2s[e, :, half * 4096:(half + 1) * 4096],
                        in_=c_.ap().rearrange("p a b -> p (a b)")), 16)
                ctx.gps(_)
                gp_rel2[it2] = ctx.GP.n
                it2 += 1


def emit_l1moe(ctx, sh, aps, scale):
    nc = ctx.nc
    x_in, x_out = aps["x_in"], aps["x_out"]

    acc = ctx.sb("moe_acc", [128, DT, MG], F32)
    Y16 = ctx.sb("moe_y", [128, NMP, MG], BF16)
    ln16 = ctx.sb("moe_ln", [128, DT, MG], BF16)
    w2sb2 = [ctx.sb(f"moe_w2_{i}", [128, 16, D], BF16) for i in range(2)]
    wsl = [ctx.sb(f"moe_w1sl{i}", [128, 2, DT, 128], BF16) for i in range(2)]
    xb = Y16.ap()[:, 0:8, :].bitcast(F32)
    cbrow = ctx.sb("moe_cbrow", [1, 8, MG], F32)
    cbbc = ctx.sb("moe_cbbc", [128, 2, CH], F32)
    comb = ctx.sb("moe_comb", [128, MG // 128, 8], F32)
    silu_sb = ctx.sb("moe_silu", [128, 4, CH], F32)
    tmpf = ctx.sb("moe_tmpf", [128, CH], F32)
    rw16 = ctx.sb("moe_rw16", [128, DT, 8], BF16)
    identf = ctx.sb("moe_ident", [128, 128], F32)
    rcol = ctx.sb("moe_rcol", [128, 4], F32)
    scr = {
        "x16": Y16[:, 8:16, 0:CH],
        "xsq": Y16[:, 8:16, CH:2 * CH],
        "srow": ctx.sb("moe_srow", [1, CH], F32),
        "rstd": ctx.sb("moe_rstd", [1, CH], F32),
        "ab16": ctx.sb("moe_ab16", [1, 2 * CH], BF16),
    }

    def _(sync):
        ctx.LD.inc(sync.dma_start(out=identf[:], in_=aps["ident"]), 16)
    ctx.sync(_)

    def _(sync):
        ctx.LD.inc(sync.dma_start(
            out=cbbc.ap().rearrange("p a b -> p (a b)")[:, 0:64].rearrange(
                "p (a b) -> p a b", a=DT),
            in_=aps["routert"].rearrange("(dt p) e -> p dt e", p=128)), 16)
    ctx.sync(_)

    def _(vector):
        ctx.LD.wait(vector)
        ctx.VE.inc(vector.tensor_copy(
            rw16[:], cbbc.ap().rearrange("p a b -> p (a b)")[:, 0:64].rearrange(
                "p (a b) -> p a b", a=DT)))
    ctx.vec(_)

    xb_release = {}
    ws_rel = {}
    w2_rel = {}
    qse_rel = {}
    qve_rel = {}
    moe_ln_rel = {0: 0}
    w2ve_rel = {}
    cb_rel = {0: 0}
    for g in range(NMG):
        t0 = g * MG
        for ch2 in range(2):
            gc = g * 2 + ch2

            def _(sync, gc=gc):
                if gc == 0:
                    ctx.VE.wait(sync)
                    ctx.TE.wait(sync)
                else:
                    ctx.VE.wait(sync, xb_release.get(gc - 1, 0))
                    ctx.GP.wait(sync)
                ctx.LD.inc(sync.dma_start(out=xb[:], in_=x_dram_chunk(x_in, gc)), 16)
            ctx.sync(_)
            ld_at = ctx.LD.n

            def _(vector, ld_at=ld_at):
                vector.wait_ge(ctx.LD.h, ld_at)
            ctx.vec(_)
            emit_ln(ctx, sh, xb, ln16[:, :, ch2 * CH:(ch2 + 1) * CH], scr)
            xb_release[gc] = ctx.VE.n

        def _(tensor):
            ctx.VE.wait(tensor)
            i = None
            for tt in range(MG // 128):
                for dt_ in range(DT):
                    i = tensor.matmul(ctx.pb[6][:, tt * 8:(tt + 1) * 8],
                                      ln16[:, dt_, tt * 128:(tt + 1) * 128],
                                      rw16[:, dt_, :],
                                      start=(dt_ == 0), stop=(dt_ == DT - 1))
            ctx.TE.inc(i)
        ctx.ten(_)

        r3 = lambda ap: ap.rearrange("p (a b) -> p a b", a=8)
        eg3 = r3(silu_sb[:, 0, 0:64])
        probs = r3(silu_sb[:, 0, 64:128])
        msk = r3(silu_sb[:, 0, 128:192])
        ge2 = r3(silu_sb[:, 0, 192:256])
        sr = r3(silu_sb[:, 1, 0:8])
        sr = silu_sb[:, 1, 0:8].rearrange("p (a b) -> p a b", b=1)
        m1r = silu_sb[:, 1, 8:16].rearrange("p (a b) -> p a b", b=1)
        m2r = silu_sb[:, 1, 16:24].rearrange("p (a b) -> p a b", b=1)

        def _(scalar):
            ctx.TE.wait(scalar)
            ctx.SE.inc(scalar.activation(
                silu_sb[:, 0, 0:64], ctx.pb[6][:, 0:64], AF.Exp))
        ctx.sca(_)

        def _(vector):
            ctx.SE.wait(vector)
            ctx.vs(vector, vector.reduce_sum(sr, eg3, mybir.AxisListType.X))
            ctx.vs(vector, vector.reciprocal(sr, sr))
            ctx.vs(vector, vector.tensor_tensor(
                probs, eg3, sr.broadcast_to((128, 8, 8)), ALU.mult))
            ctx.vs(vector, vector.reduce_max(m1r, probs, mybir.AxisListType.X))
            ctx.vs(vector, vector.tensor_tensor(
                msk, probs, m1r.broadcast_to((128, 8, 8)), ALU.is_ge))
            ctx.vs(vector, vector.scalar_tensor_tensor(
                msk, msk, -1e9, probs, ALU.mult, ALU.add))
            ctx.vs(vector, vector.reduce_max(m2r, msk, mybir.AxisListType.X))
            ctx.vs(vector, vector.tensor_tensor(
                ge2, probs, m2r.broadcast_to((128, 8, 8)), ALU.is_ge))
            ctx.VE.inc(vector.tensor_tensor(
                comb.ap().rearrange("p a b -> p a b"), probs, ge2, ALU.mult))
        ctx.vec(_)

        for e in range(8):
            for half in range(MG // 512):
                def _(tensor, e=e, half=half):
                    if e == 0 and half == 0:
                        ctx.VE.wait(tensor)
                    else:
                        ctx.VE.wait(tensor)
                    i = None
                    for q in range(4):
                        tt = half * 4 + q
                        i = tensor.transpose(ctx.pb[7][0:1, q * 128:(q + 1) * 128],
                                             comb[:, tt, e:e + 1], identf[:])
                    ctx.TE.inc(i)
                ctx.ten(_)

                def _(vector, e=e, half=half):
                    ctx.TE.wait(vector)
                    ctx.VE.inc(vector.tensor_copy(
                        cbrow[0:1, e, half * 512:(half + 1) * 512],
                        ctx.pb[7][0:1, 0:512]))
                ctx.vec(_)

        def _(vector):
            ctx.TE.wait(vector)
            ctx.VE.inc(vector.memset(acc[:], 0.0))
        ctx.vec(_)
        moe_ln_rel[0] = ctx.VE.n

        for e in range(8):
            w2sb = w2sb2[e % 2]
            E = g * 8 + e

            def _(scalar, e=e, g=g, w2sb=w2sb, E=E):
                if E - 2 in w2_rel:
                    scalar.wait_ge(ctx.TE.h, w2_rel[E - 2])
                else:
                    ctx.TE.wait(scalar)
                ctx.SD.inc(scalar.dma_start(
                    out=w2sb.ap().rearrange("p a b -> p (a b)"),
                    in_=aps["w2s"][e]), 16)
            ctx.sca(_)
            w2_at = ctx.SD.n

            def _(tensor, e=e, E=E):
                ve_t = moe_ln_rel[0]
                if E - 1 in w2ve_rel:
                    ve_t = max(ve_t, w2ve_rel[E - 1])
                    ctx.VE.wait(tensor, ve_t)
                else:
                    ctx.VE.wait(tensor)
                tensor.matmul(ctx.pb[6][:, 0:CH], sh.ones_row32[:],
                              cbrow[0:1, e, 0:CH], start=True, stop=True)
                ctx.TE.inc(tensor.matmul(ctx.pb[7][:, 0:CH], sh.ones_row32[:],
                                         cbrow[0:1, e, CH:2 * CH],
                                         start=True, stop=True))
            ctx.ten(_)

            def _(vector):
                ctx.TE.wait(vector)
                vector.tensor_copy(cbbc[:, 0, :], ctx.pb[6][:, 0:CH])
                ctx.VE.inc(vector.tensor_copy(cbbc[:, 1, :], ctx.pb[7][:, 0:CH]))
            ctx.vec(_)
            cb_rel[0] = ctx.VE.n

            te_at = {}
            for mp in range(NMP):
                ws = wsl[mp % 2]
                F = E * NMP + mp

                def _(sync, e=e, mp=mp, ws=ws, F=F):
                    if F - 2 in ws_rel:
                        ctx.TE.wait(sync, ws_rel[F - 2])
                    else:
                        ctx.TE.wait(sync)
                    ctx.LD.inc(sync.dma_start(
                        out=ws.ap().rearrange("p a b c -> p (a b c)"),
                        in_=aps["w1s"][e, mp]), 16)
                ctx.sync(_)
                sl_at = ctx.LD.n

                for ch2 in range(2):
                    q = mp * 2 + ch2
                    Q = E * 32 + q
                    bg, bu = ctx.pb[2 * (q % 4)], ctx.pb[2 * (q % 4) + 1]

                    def _(tensor, mp=mp, ch2=ch2, q=q, Q=Q, bg=bg, bu=bu, ws=ws,
                          sl_at=sl_at, e=e):
                        tensor.wait_ge(ctx.LD.h, sl_at)
                        if Q - 4 in qse_rel:
                            ctx.SE.wait(tensor, qse_rel[Q - 4])
                            ctx.VE.wait(tensor, max(qve_rel[Q - 4], moe_ln_rel[0],
                                                    cb_rel[0]))
                        else:
                            ctx.SE.wait(tensor)
                            ctx.VE.wait(tensor)
                        for k in range(DT):
                            tensor.matmul(bg[:, 0:CH], ws[:, 0, k, :],
                                          ln16[:, k, ch2 * CH:(ch2 + 1) * CH],
                                          start=(k == 0), stop=(k == DT - 1))
                        i = None
                        for k in range(DT):
                            i = tensor.matmul(bu[:, 0:CH], ws[:, 1, k, :],
                                              ln16[:, k, ch2 * CH:(ch2 + 1) * CH],
                                              start=(k == 0), stop=(k == DT - 1))
                        ctx.TE.inc(i)
                    ctx.ten(_)
                    te_at[q] = ctx.TE.n

                def _(scalar, mp=mp):
                    for ch2 in range(2):
                        q = mp * 2 + ch2
                        scalar.wait_ge(ctx.TE.h, te_at[q])
                        i = scalar.activation(silu_sb[:, q % 4, :],
                                              ctx.pb[2 * (q % 4)][:, 0:CH], AF.Silu)
                    ctx.SE.inc(i)
                ctx.sca(_)
                qse_rel[E * 32 + mp * 2] = qse_rel[E * 32 + mp * 2 + 1] = ctx.SE.n

                def _(vector, mp=mp):
                    ctx.SE.wait(vector)
                    i = None
                    for ch2 in range(2):
                        q = mp * 2 + ch2
                        bu = ctx.pb[2 * (q % 4) + 1]
                        vector.tensor_tensor(tmpf[:], bu[:, 0:CH],
                                             cbbc[:, ch2, :], ALU.mult)
                        i = vector.tensor_mul(
                            Y16[:, mp, ch2 * CH:(ch2 + 1) * CH],
                            tmpf[:], silu_sb[:, q % 4, :])
                    ctx.VE.inc(i)
                ctx.vec(_)
                qve_rel[E * 32 + mp * 2] = qve_rel[E * 32 + mp * 2 + 1] = ctx.VE.n
                ws_rel[F] = ctx.TE.n

            wv_rel = {}
            for ch2 in range(2):
                for wave in range(4):
                    r = ch2 * 4 + wave
                    bk = 4 + 2 * (r % 2)

                    def _(tensor, ch2=ch2, wave=wave, r=r, bk=bk, w2_at=w2_at,
                          w2sb=w2sb):
                        tensor.wait_ge(ctx.SD.h, w2_at)
                        if r >= 2:
                            ctx.VE.wait(tensor, wv_rel[r - 2])
                        else:
                            t = qve_rel[E * 32 + 31]
                            if E - 1 in w2ve_rel:
                                t = max(t, w2ve_rel[E - 1])
                                ctx.VE.wait(tensor, t)
                            else:
                                ctx.VE.wait(tensor)
                        i = None
                        for m in range(2):
                            md = wave * 2 + m
                            for kt in range(16):
                                i = tensor.matmul(
                                    ctx.pb[bk + m][:, 0:CH],
                                    w2sb[:, kt, md * 128:(md + 1) * 128],
                                    Y16[:, kt, ch2 * CH:(ch2 + 1) * CH],
                                    start=(kt == 0), stop=(kt == 15))
                        ctx.TE.inc(i)
                    ctx.ten(_)

                    def _(vector, ch2=ch2, wave=wave, bk=bk):
                        ctx.TE.wait(vector)
                        i = None
                        for m in range(2):
                            md = wave * 2 + m
                            sl_ = acc[:, md, ch2 * CH:(ch2 + 1) * CH]
                            i = vector.tensor_tensor(sl_, ctx.pb[bk + m][:, 0:CH],
                                                     sl_, ALU.add)
                        ctx.VE.inc(i)
                    ctx.vec(_)
                    wv_rel[r] = ctx.VE.n
            w2_rel[E] = ctx.TE.n
            w2ve_rel[E] = ctx.VE.n

        for ch2 in range(2):
            gc = g * 2 + ch2

            def _(sync, gc=gc):
                ctx.VE.wait(sync)
                ctx.GP.wait(sync)
                ctx.LD.inc(sync.dma_start(out=xb[:], in_=x_dram_chunk(x_in, gc)), 16)
            ctx.sync(_)
            ld_at = ctx.LD.n

            def _(vector, ch2=ch2, ld_at=ld_at):
                vector.wait_ge(ctx.LD.h, ld_at)
                i = None
                for md in range(DT):
                    i = vector.scalar_tensor_tensor(
                        xb[:, md, :], acc[:, md, ch2 * CH:(ch2 + 1) * CH], scale,
                        xb[:, md, :], ALU.mult, ALU.add)
                ctx.VE.inc(i)
            ctx.vec(_)

            def _(gpsimd, gc=gc):
                ctx.VE.wait(gpsimd)
                ctx.GP.inc(gpsimd.dma_start(out=x_dram_chunk(x_out, gc), in_=xb[:]), 16)
            ctx.gps(_)

    def _(sync):
        ctx.GP.wait(sync)
    ctx.sync(_)


def build_l1moe_program():
    nc = bass.Bass("TRN2", target_bir_lowering=False, debug=False)
    x_in = nc.dram_tensor("x", [D, TOK], F32, kind="ExternalInput").ap()
    w1t = nc.dram_tensor("w1t", [8, D, 2 * INTER], F32, kind="ExternalInput").ap()
    w2t = nc.dram_tensor("w2t", [8, INTER, D], F32, kind="ExternalInput").ap()
    routert = nc.dram_tensor("routert", [D, 8], F32, kind="ExternalInput").ap()
    ident = nc.dram_tensor("ident", [128, 128], F32, kind="ExternalInput").ap()
    x_out = nc.dram_tensor("y", [D, TOK], F32, kind="ExternalOutput").ap()
    w1s = nc.dram_tensor("w1s", [8, NMP, 128, 2048], BF16, kind="Internal").ap()
    w2s = nc.dram_tensor("w2s", [8, 128, 16384], BF16, kind="Internal").ap()
    with contextlib.ExitStack() as es:
        ctx = make_ctx(es, nc)
        sh = Shared(ctx)
        emit_moe_preconvert(ctx, dict(w1t=w1t, w2t=w2t, w1s=w1s, w2s=w2s))
        emit_l1moe(ctx, sh, dict(x_in=x_in, x_out=x_out, w1s=w1s, w2s=w2s,
                                 routert=routert, ident=ident), 0.5)
    return nc



def build_fused_program():
    nc = bass.Bass("TRN2", target_bir_lowering=False, debug=False)
    x_in = nc.dram_tensor("x", [D, TOK], F32, kind="ExternalInput").ap()
    wg = nc.dram_tensor("wg", [D, FFN_H], I8, kind="ExternalInput").ap()
    wu = nc.dram_tensor("wu", [D, FFN_H], I8, kind="ExternalInput").ap()
    wd = nc.dram_tensor("wd", [FFN_H, D], I8, kind="ExternalInput").ap()
    bw = nc.dram_tensor("bw", [D, D], F32, kind="ExternalInput").ap()
    cwt = nc.dram_tensor("cwt", [D, D], F32, kind="ExternalInput").ap()
    a512 = nc.dram_tensor("a512", [128, 4], F32, kind="ExternalInput").ap()
    dvec = nc.dram_tensor("dvec", [128, 8], F32, kind="ExternalInput").ap()
    fret = nc.dram_tensor("fret", [D, 513], F32, kind="ExternalInput").ap()
    fimt = nc.dram_tensor("fimt", [D, 513], F32, kind="ExternalInput").ap()
    finvre = nc.dram_tensor("finvre", [D, 513], F32, kind="ExternalInput").ap()
    finvim = nc.dram_tensor("finvim", [D, 513], F32, kind="ExternalInput").ap()
    msint = nc.dram_tensor("msint", [512, 513], F32, kind="ExternalInput").ap()
    rkvt = nc.dram_tensor("rkvt", [D, 6144], F32, kind="ExternalInput").ap()
    outwt = nc.dram_tensor("outwt", [2048, D], F32, kind="ExternalInput").ap()
    td = nc.dram_tensor("td", [128, 8], F32, kind="ExternalInput").ap()
    tf = nc.dram_tensor("tf", [128, 8], F32, kind="ExternalInput").ap()
    tdr = nc.dram_tensor("tdr", [128, 8], F32, kind="ExternalInput").ap()
    tfr = nc.dram_tensor("tfr", [128, 8], F32, kind="ExternalInput").ap()
    w1t = nc.dram_tensor("w1t", [8, D, 2 * INTER], F32, kind="ExternalInput").ap()
    w2t = nc.dram_tensor("w2t", [8, INTER, D], F32, kind="ExternalInput").ap()
    routert = nc.dram_tensor("routert", [D, 8], F32, kind="ExternalInput").ap()
    ident = nc.dram_tensor("ident", [128, 128], F32, kind="ExternalInput").ap()
    y_out = nc.dram_tensor("y", [D, TOK], F32, kind="ExternalOutput").ap()
    xa = nc.dram_tensor("xa", [D, TOK], F32, kind="Internal").ap()
    xb_ = nc.dram_tensor("xb", [D, TOK], F32, kind="Internal").ap()
    w1s = nc.dram_tensor("w1s", [8, NMP, 128, 2048], BF16, kind="Internal").ap()
    w2s = nc.dram_tensor("w2s", [8, 128, 16384], BF16, kind="Internal").ap()

    ssm = dict(bw=bw, cwt=cwt, a512=a512, dvec=dvec, fret=fret, fimt=fimt,
               finvre=finvre, finvim=finvim, msint=msint)
    wkv = dict(rkvt=rkvt, outwt=outwt, td=td, tf=tf, tdr=tdr, tfr=tfr)
    moe = dict(w1s=w1s, w2s=w2s, routert=routert, ident=ident)

    with contextlib.ExitStack() as es:
        ctx = make_ctx(es, nc)
        sh = Shared(ctx)

        def phase(emit, *args):
            with contextlib.ExitStack() as pes:
                ctx.cur_es = pes
                emit(*args)
            ctx.cur_es = None
            phase_barrier(ctx)

        phase(emit_moe_preconvert, ctx, dict(w1t=w1t, w2t=w2t, w1s=w1s, w2s=w2s))
        phase(emit_l0ffn, ctx, sh,
              dict(x_in=x_in, x_out=xa, wg=wg, wu=wu, wd=wd), 0.5)
        phase(emit_l0attn, ctx, sh, dict(x_in=xa, x_out=xb_, **ssm))
        phase(emit_l0ffn, ctx, sh,
              dict(x_in=xb_, x_out=xa, wg=wg, wu=wu, wd=wd), 0.5)
        phase(emit_l1moe, ctx, sh, dict(x_in=xa, x_out=xb_, **moe), 0.5)
        phase(emit_l1attn, ctx, sh, dict(x_in=xb_, x_out=xa, **wkv))
        phase(emit_l1moe, ctx, sh, dict(x_in=xa, x_out=y_out, **moe), 0.5)
    return nc



_PROG_CACHE = {}


def _get_prog(name):
    if name not in _PROG_CACHE:
        _PROG_CACHE[name] = {"l0ffn": build_l0ffn_program,
                             "l0attn": build_l0attn_program,
                             "l1attn": build_l1attn_program,
                             "l1moe": build_l1moe_program,
                             "fused": build_fused_program}[name]()
    return _PROG_CACHE[name]


def _col8(a):
    return np.ascontiguousarray(a.reshape(8, 128).T)


def kernel(L0_attn_u, L0_attn_a_params, L0_attn_B_w, L0_attn_C_w, L0_attn_D,
           L0_ffn_w_gate, L0_ffn_w_up, L0_ffn_w_down,
           L1_attn_rkv_w, L1_attn_out_w, L1_attn_time_decay, L1_attn_time_first,
           L1_attn_time_decay_rev, L1_attn_time_first_rev,
           L1_ffn_w1, L1_ffn_w2, L1_ffn_router_w):
    x = np.asarray(L0_attn_u)
    xT = np.ascontiguousarray(np.asarray(x).transpose(2, 0, 1))
    shards = [np.ascontiguousarray(xT[:, 4 * c:4 * (c + 1), :]).reshape(D, TOK)
              for c in range(NCORE)]

    wg = np.ascontiguousarray(_prep_int8(np.asarray(L0_ffn_w_gate)).T)
    wu = np.ascontiguousarray(_prep_int8(np.asarray(L0_ffn_w_up)).T)
    wd = np.ascontiguousarray(_prep_int8(np.asarray(L0_ffn_w_down)).T)
    cst = make_ssm_consts()
    ssm_w = dict(bw=np.ascontiguousarray(np.asarray(L0_attn_B_w)),
                 cwt=np.ascontiguousarray(np.asarray(L0_attn_C_w).T),
                 a512=_col8(np.asarray(L0_attn_a_params).reshape(4, 128).reshape(512))
                 if False else np.ascontiguousarray(
                     np.asarray(L0_attn_a_params).reshape(4, 128).T),
                 dvec=_col8(np.asarray(L0_attn_D)), **cst)
    wkv_w = dict(rkvt=np.ascontiguousarray(np.asarray(L1_attn_rkv_w).T),
                 outwt=np.ascontiguousarray(np.asarray(L1_attn_out_w).T),
                 td=_col8(np.asarray(L1_attn_time_decay)),
                 tf=_col8(np.asarray(L1_attn_time_first)),
                 tdr=_col8(np.asarray(L1_attn_time_decay_rev)),
                 tfr=_col8(np.asarray(L1_attn_time_first_rev)))
    moe_w = dict(w1t=np.ascontiguousarray(np.asarray(L1_ffn_w1).transpose(0, 2, 1)),
                 w2t=np.ascontiguousarray(np.asarray(L1_ffn_w2).transpose(0, 2, 1)),
                 routert=np.ascontiguousarray(np.asarray(L1_ffn_router_w).T),
                 ident=np.eye(128, dtype=np.float32))

    nc = _get_prog("fused")
    weights = {"wg": wg, "wu": wu, "wd": wd, **ssm_w, **wkv_w, **moe_w}
    in_maps = [{"x": shards[c], **weights} for c in range(NCORE)]
    res = run_bass_kernel_spmd(nc, in_maps, core_ids=list(range(NCORE)))
    ys = [res.results[c]["y"] for c in range(NCORE)]

    out = np.empty((32, SEQ, D), dtype=np.float32)
    for c in range(NCORE):
        out[4 * c:4 * (c + 1)] = ys[c].reshape(D, BC, SEQ).transpose(1, 2, 0)
    return out


# version 12 (final)
# speedup vs baseline: 1.0889x; 1.0889x over previous
import contextlib
import numpy as np

import concourse.bass as bass
import concourse.mybir as mybir
from concourse.bass_utils import run_bass_kernel_spmd

F32 = mybir.dt.float32
BF16 = mybir.dt.bfloat16
I8 = mybir.dt.int8
AF = mybir.ActivationFunctionType
ALU = mybir.AluOpType

D = 1024
DT = 8
SEQ = 2048
BC = 4
TOK = BC * SEQ
CH = 512
NCH = TOK // CH
NCORE = 8

FFN_H = 3072
HT = FFN_H // 128

LN_EPS = 1e-5
DEQ = 1.0 / 127.0



class SemC:
    def __init__(self, nc, es, name):
        self.h = es.enter_context(nc.semaphore(name))
        self.n = 0

    def inc(self, instr, k=1):
        instr.then_inc(self.h, k)
        self.n += k
        return instr

    def wait(self, eng, at=None):
        eng.wait_ge(self.h, self.n if at is None else at)


class Ctx:
    def __init__(self, nc, es, block):
        self.nc = nc
        self.es = es
        self.block = block
        self.LD = SemC(nc, es, "ld")
        self.GP = SemC(nc, es, "gp")
        self.SD = SemC(nc, es, "sd")
        self.TE = SemC(nc, es, "te")
        self.VE = SemC(nc, es, "ve")
        self.SE = SemC(nc, es, "se")
        self.pb = [es.enter_context(nc.psum_tensor(f"pb{i}", [128, 512], F32))
                   for i in range(8)]

    def uname(self, name):
        self._uid = getattr(self, "_uid", 0) + 1
        return f"{name}_u{self._uid}"

    def sb(self, name, shape, dtype):
        es = getattr(self, "cur_es", None) or self.es
        return es.enter_context(self.nc.sbuf_tensor(self.uname(name), shape, dtype))

    def vs(self, vector, instr):
        self.VE.inc(instr)
        self.VE.wait(vector)

    def sync(self, fn): self.block.sync(fn)
    def gps(self, fn): self.block.gpsimd(fn)
    def ten(self, fn): self.block.tensor(fn)
    def vec(self, fn): self.block.vector(fn)
    def sca(self, fn): self.block.scalar(fn)


def phase_barrier(ctx):
    sems = [ctx.LD, ctx.GP, ctx.SD, ctx.TE, ctx.VE, ctx.SE]

    def mk(engine):
        for s in sems:
            s.wait(engine)
    ctx.sync(mk)
    ctx.gps(mk)
    ctx.ten(mk)
    ctx.vec(mk)
    ctx.sca(mk)


def make_ctx(es, nc):
    block = es.enter_context(nc.Block())
    return Ctx(nc, es, block)


def x_dram_chunk(x_ap, c, cols=CH):
    v = x_ap.rearrange("(dt p) t -> p dt t", p=128)
    return v[:, :, c * cols:(c + 1) * cols]


class Shared:
    def __init__(self, ctx):
        self.ones_col16 = ctx.sb("ones_col16", [128, 1], BF16)
        self.ones_row16 = ctx.sb("ones_row16", [1, 128], BF16)
        self.ones_col_f32 = ctx.sb("ones_col_f32", [128, 1], F32)
        self.ones_row32 = ctx.sb("ones_row32", [1, 128], F32)
        self.tmpP = ctx.sb("sh_tmpP", [128, 256], F32)

        def _(vector):
            vector.memset(self.ones_col16[:], 1.0)
            vector.memset(self.ones_row16[:], 1.0)
            vector.memset(self.ones_row32[:], 1.0)
            ctx.VE.inc(vector.memset(self.ones_col_f32[:], 1.0))
        ctx.vec(_)


def make_ln_scratch(ctx, tag=""):
    return {
        "x16": ctx.sb(f"ln_x16{tag}", [128, DT, CH], BF16),
        "xsq": ctx.sb(f"ln_xsq{tag}", [128, DT, CH], BF16),
        "srow": ctx.sb(f"ln_srow{tag}", [1, CH], F32),
        "rstd": ctx.sb(f"ln_rstd{tag}", [1, CH], F32),
        "ab16": ctx.sb(f"ln_ab16{tag}", [1, 2 * CH], BF16),
    }


def emit_ln(ctx, sh, x_sb, ln16, scr, ncols=CH):
    x16, xsq = scr["x16"], scr["xsq"]
    srow, rstd, ab16 = scr["srow"], scr["rstd"], scr["ab16"]
    pa, pbk = ctx.pb[6], ctx.pb[7]

    def _(vector):
        ctx.TE.wait(vector)
        for j in range(DT):
            vector.tensor_copy(x16[:, j, :ncols], x_sb[:, j, :ncols])
        i = None
        for j in range(DT):
            i = vector.tensor_mul(xsq[:, j, :ncols], x16[:, j, :ncols],
                                  x16[:, j, :ncols])
        ctx.VE.inc(i)
    ctx.vec(_)

    def _(tensor):
        ctx.VE.wait(tensor)
        for j in range(DT):
            tensor.matmul(pa[0:1, 0:ncols], sh.ones_col16[:], x16[:, j, :ncols],
                          start=(j == 0), stop=(j == DT - 1))
        i = None
        for j in range(DT):
            i = tensor.matmul(pbk[0:1, 0:ncols], sh.ones_col16[:], xsq[:, j, :ncols],
                              start=(j == 0), stop=(j == DT - 1))
        ctx.TE.inc(i)
    ctx.ten(_)

    def _(vector):
        ctx.TE.wait(vector)
        vector.tensor_scalar_mul(srow[:, :ncols], pa[0:1, 0:ncols], 1.0 / D)
        vector.tensor_mul(rstd[:, :ncols], srow[:, :ncols], srow[:, :ncols])
        vector.scalar_tensor_tensor(
            rstd[:, :ncols], pbk[0:1, 0:ncols], 1.0 / D, rstd[:, :ncols],
            ALU.mult, ALU.subtract)
        ctx.VE.inc(vector.tensor_scalar_add(rstd[:, :ncols], rstd[:, :ncols],
                                            LN_EPS))
    ctx.vec(_)

    def _(scalar):
        ctx.VE.wait(scalar)
        ctx.SE.inc(scalar.activation(rstd[:, :ncols], rstd[:, :ncols], AF.Sqrt))
    ctx.sca(_)

    def _(vector):
        ctx.SE.wait(vector)
        ctx.VE.inc(vector.reciprocal(rstd[:, :ncols], rstd[:, :ncols]))
        ctx.VE.wait(vector)
        vector.tensor_copy(ab16[0:1, 0:ncols], rstd[:, :ncols])
        vector.tensor_mul(srow[:, :ncols], srow[:, :ncols], rstd[:, :ncols])
        ctx.VE.inc(vector.tensor_scalar_mul(ab16[0:1, CH:CH + ncols],
                                            srow[:, :ncols], -1.0))
    ctx.vec(_)

    def _(tensor):
        ctx.VE.wait(tensor)
        tensor.matmul(pa[:, 0:ncols], sh.ones_row16[:], ab16[0:1, 0:ncols],
                      start=True, stop=True)
        ctx.TE.inc(tensor.matmul(pbk[:, 0:ncols], sh.ones_row16[:],
                                 ab16[0:1, CH:CH + ncols], start=True, stop=True))
    ctx.ten(_)

    def _(vector):
        ctx.TE.wait(vector)
        i = None
        for j in range(DT):
            vector.tensor_tensor(ln16[:, j, :ncols], x16[:, j, :ncols],
                                 pa[:, 0:ncols], ALU.mult)
            i = vector.tensor_tensor(ln16[:, j, :ncols], ln16[:, j, :ncols],
                                     pbk[:, 0:ncols], ALU.add)
        ctx.VE.inc(i)
    ctx.vec(_)



def emit_l0ffn(ctx, sh, aps, scale, dbg=None):
    nc = ctx.nc
    x_in, x_out = aps["x_in"], aps["x_out"]

    wg16 = ctx.sb("wg16", [128, DT, FFN_H], BF16)
    wu16 = ctx.sb("wu16", [128, DT, FFN_H], BF16)
    wd16 = ctx.sb("wd16", [128, HT, D], BF16)
    with contextlib.ExitStack() as stage_es:
        st8 = stage_es.enter_context(nc.sbuf_tensor(ctx.uname("w_stage8"), [128, DT * FFN_H], I8))
        plan = [(aps["wg"], "(dt p) h -> p dt h", wg16, DT, FFN_H),
                (aps["wu"], "(dt p) h -> p dt h", wu16, DT, FFN_H),
                (aps["wd"], "(ht p) d -> p ht d", wd16, HT, D)]
        for wi, (w_ap, pat, w16, A, B) in enumerate(plan):
            stv = st8.ap().rearrange("p (a b) -> p a b", a=A)
            def _(sync, w_ap=w_ap, pat=pat, wi=wi, stv=stv):
                if wi > 0:
                    ctx.VE.wait(sync)
                ctx.LD.inc(sync.dma_start(out=stv, in_=w_ap.rearrange(pat, p=128)), 16)
            ctx.sync(_)

            def _(vector, w16=w16, stv=stv):
                ctx.LD.wait(vector)
                ctx.VE.inc(vector.tensor_scalar_mul(w16[:], stv, DEQ))
            ctx.vec(_)

    x_sb = [ctx.sb("ffn_x0", [128, DT, CH], F32)]
    ln16 = ctx.sb("ffn_ln16", [128, DT, CH], BF16)
    h16 = ctx.sb("ffn_h16", [128, HT, CH], BF16)
    silu_sb = ctx.sb("ffn_silu", [128, 4, CH], F32)
    scr = {
        "x16": h16[:, 8:16, :],
        "xsq": h16[:, 0:8, :],
        "srow": ctx.sb("ln_srow", [1, CH], F32),
        "rstd": ctx.sb("ln_rstd", [1, CH], F32),
        "ab16": ctx.sb("ln_ab16", [1, 2 * CH], BF16),
    }
    xb_release = {}

    for c in range(NCH):
        xb = x_sb[0]

        def _(sync, c=c, xb=xb):
            if c == 0:
                ctx.VE.wait(sync)
            else:
                ctx.VE.wait(sync, xb_release[c - 1])
                ctx.GP.wait(sync)
            ctx.LD.inc(sync.dma_start(out=xb[:], in_=x_dram_chunk(x_in, c)), 16)
        ctx.sync(_)
        ld_at = ctx.LD.n

        def _(vector, ld_at=ld_at):
            vector.wait_ge(ctx.LD.h, ld_at)
        ctx.vec(_)

        emit_ln(ctx, sh, xb, ln16, scr)

        se_rel = {}
        ve_rel = {}
        te_at = {}
        for j in range(HT):
            bg, bu = ctx.pb[2 * (j % 4)], ctx.pb[2 * (j % 4) + 1]

            def _(tensor, j=j, bg=bg, bu=bu):
                if j < 4:
                    ctx.VE.wait(tensor)
                    if j >= 2:
                        ctx.SE.wait(tensor)
                else:
                    ctx.SE.wait(tensor, se_rel[j - 4])
                    ctx.VE.wait(tensor, ve_rel[j - 4])
                for k in range(DT):
                    tensor.matmul(bg[:, 0:CH], wg16[:, k, j * 128:(j + 1) * 128],
                                  ln16[:, k, :], start=(k == 0), stop=(k == DT - 1))
                i = None
                for k in range(DT):
                    i = tensor.matmul(bu[:, 0:CH], wu16[:, k, j * 128:(j + 1) * 128],
                                      ln16[:, k, :], start=(k == 0), stop=(k == DT - 1))
                ctx.TE.inc(i)
            ctx.ten(_)
            te_at[j] = ctx.TE.n

            if j % 2 == 1:
                def _(scalar, j=j):
                    i = None
                    for jj in (j - 1, j):
                        scalar.wait_ge(ctx.TE.h, te_at[jj])
                        i = scalar.activation(silu_sb[:, jj % 4, :],
                                              ctx.pb[2 * (jj % 4)][:, 0:CH], AF.Silu)
                    ctx.SE.inc(i)
                ctx.sca(_)
                se_rel[j - 1] = se_rel[j] = ctx.SE.n

                def _(vector, j=j):
                    ctx.SE.wait(vector)
                    i = None
                    for jj in (j - 1, j):
                        i = vector.tensor_mul(h16[:, jj, :], silu_sb[:, jj % 4, :],
                                              ctx.pb[2 * (jj % 4) + 1][:, 0:CH])
                    ctx.VE.inc(i)
                ctx.vec(_)
                ve_rel[j - 1] = ve_rel[j] = ctx.VE.n


        dv_rel = {}
        for wave in range(4):
            bk = 2 + 2 * (wave % 2)

            def _(tensor, wave=wave, bk=bk):
                if wave == 0:
                    ctx.SE.wait(tensor, se_rel[HT - 1])
                    ctx.VE.wait(tensor)
                elif wave == 1:
                    pass
                else:
                    ctx.VE.wait(tensor, dv_rel[wave - 2])
                i = None
                for m in range(2):
                    mm = wave * 2 + m
                    for k in range(HT):
                        i = tensor.matmul(ctx.pb[bk + m][:, 0:CH],
                                          wd16[:, k, mm * 128:(mm + 1) * 128],
                                          h16[:, k, :],
                                          start=(k == 0), stop=(k == HT - 1))
                ctx.TE.inc(i)
            ctx.ten(_)

            def _(vector, wave=wave, bk=bk, xb=xb, c=c):
                ctx.TE.wait(vector)
                i = None
                for m in range(2):
                    mm = wave * 2 + m
                    i = vector.scalar_tensor_tensor(
                        xb[:, mm, :], ctx.pb[bk + m][:, 0:CH], scale,
                        xb[:, mm, :], ALU.mult, ALU.add)
                ctx.VE.inc(i)
            ctx.vec(_)
            dv_rel[wave] = ctx.VE.n
        xb_release[c] = ctx.VE.n

        def _(gpsimd, c=c, xb=xb):
            ctx.VE.wait(gpsimd)
            ctx.GP.inc(gpsimd.dma_start(out=x_dram_chunk(x_out, c), in_=xb[:]), 16)
        ctx.gps(_)

    def _(sync):
        ctx.GP.wait(sync)
    ctx.sync(_)



def _prep_int8(w_int32):
    return np.ascontiguousarray(w_int32.view(np.int8)[..., 0::4])


def build_l0ffn_program():
    nc = bass.Bass("TRN2", target_bir_lowering=False, debug=False)
    x_in = nc.dram_tensor("x", [D, TOK], F32, kind="ExternalInput").ap()
    wg = nc.dram_tensor("wg", [D, FFN_H], I8, kind="ExternalInput").ap()
    wu = nc.dram_tensor("wu", [D, FFN_H], I8, kind="ExternalInput").ap()
    wd = nc.dram_tensor("wd", [FFN_H, D], I8, kind="ExternalInput").ap()
    x_out = nc.dram_tensor("y", [D, TOK], F32, kind="ExternalOutput").ap()
    with contextlib.ExitStack() as es:
        ctx = make_ctx(es, nc)
        sh = Shared(ctx)
        emit_l0ffn(ctx, sh, {"x_in": x_in, "x_out": x_out,
                             "wg": wg, "wu": wu, "wd": wd}, 0.5)
    return nc



SCH = 512
NSC = SEQ // SCH
NF = 4


def _cmul_cols(ctx, v, outr, outi, ar, ai, br, bi, t1, t2):
    v.tensor_mul(t1, ar, br)
    ctx.vs(v, v.tensor_mul(t2, ai, bi))
    ctx.vs(v, v.tensor_tensor(outr, t1, t2, ALU.subtract))
    v.tensor_mul(t1, ar, bi)
    ctx.vs(v, v.tensor_mul(t2, ai, br))
    ctx.vs(v, v.tensor_tensor(outi, t1, t2, ALU.add))


def emit_l0attn(ctx, sh, aps, dbg=None):
    nc = ctx.nc
    x_in, x_out = aps["x_in"], aps["x_out"]

    bhat16 = ctx.sb("bhat16", [128, DT, 1025], BF16)
    chat16 = ctx.sb("chat16", [128, 9, D], BF16)
    Pr = ctx.sb("ssm_Pr", [128, NF, SCH], F32)
    Pi = ctx.sb("ssm_Pi", [128, NF, SCH], F32)
    Qc = ctx.sb("ssm_Qc", [128, 2, NF, NSC], F32)
    lamb = ctx.sb("ssm_lamb", [128, 2, NF], F32)
    Dcol = ctx.sb("ssm_D", [128, DT], F32)
    colt = ctx.sb("ssm_colt", [128, 8], F32)
    omdump = ctx.sb("ssm_omdump", [128, NF], F32)

    with contextlib.ExitStack() as pes:
        c_fret = pes.enter_context(nc.sbuf_tensor("c_fret", [128, 8, 513], F32))
        c_fimt = pes.enter_context(nc.sbuf_tensor("c_fimt", [128, 8, 513], F32))
        c_finr = pes.enter_context(nc.sbuf_tensor("c_finr", [128, 8, 513], F32))
        c_fini = pes.enter_context(nc.sbuf_tensor("c_fini", [128, 8, 513], F32))
        c_msin = pes.enter_context(nc.sbuf_tensor("c_msin", [128, 4, 513], F32))
        c_bw = pes.enter_context(nc.sbuf_tensor("c_bw", [128, 8, D], F32))
        c_cwt = pes.enter_context(nc.sbuf_tensor("c_cwt", [128, 8, D], F32))
        c_a = pes.enter_context(nc.sbuf_tensor("c_a", [128, 4], F32))

        def _(sync):
            for t, ap_, pat in [
                (c_fret, aps["fret"], "(nt p) f -> p nt f"),
                (c_fimt, aps["fimt"], "(nt p) f -> p nt f"),
                (c_finr, aps["finvre"], "(nt p) f -> p nt f"),
                (c_fini, aps["finvim"], "(nt p) f -> p nt f"),
                (c_msin, aps["msint"], "(kt p) f -> p kt f"),
                (c_bw, aps["bw"], "(nt p) d -> p nt d"),
                (c_cwt, aps["cwt"], "(nt p) d -> p nt d"),
            ]:
                ctx.LD.inc(sync.dma_start(out=t[:], in_=ap_.rearrange(pat, p=128)), 16)
            ctx.LD.inc(sync.dma_start(out=c_a[:], in_=aps["a512"]), 16)
            ctx.LD.inc(sync.dma_start(out=Dcol[:], in_=aps["dvec"]), 16)
        ctx.sync(_)

        def _(tensor):
            ctx.LD.wait(tensor)
            ctx.VE.wait(tensor)
            for fm in range(NF):
                i = None
                for kt in range(4):
                    i = tensor.matmul(ctx.pb[fm][:, 0:1],
                                      c_msin[:, kt, fm * 128:(fm + 1) * 128],
                                      c_a[:, kt:kt + 1],
                                      start=(kt == 0), stop=(kt == 3))
            ctx.TE.inc(i)
        ctx.ten(_)

        def _(vector):
            ctx.TE.wait(vector)
            t1 = colt[:, 0:1]
            t2 = colt[:, 1:2]
            t3 = colt[:, 2:3]
            for fm in range(NF):
                om = omdump[:, fm:fm + 1]
                ctx.vs(vector, vector.tensor_copy(om, ctx.pb[fm][:, 0:1]))
                ctx.vs(vector, vector.tensor_mul(t1, om, om))
                ctx.vs(vector, vector.tensor_scalar_add(t2, t1, 1.0))
                ctx.vs(vector, vector.reciprocal(t2, t2))
                ctx.vs(vector, vector.tensor_scalar_mul(t3, t1, -1.0))
                ctx.vs(vector, vector.tensor_scalar_add(t3, t3, 1.0))
                vector.tensor_mul(lamb[:, 0, fm:fm + 1], t3, t2)
                ctx.vs(vector, vector.tensor_scalar_mul(t3, om, 2.0))
                i = vector.tensor_mul(lamb[:, 1, fm:fm + 1], t3, t2)
            ctx.VE.inc(i)
        ctx.vec(_)

        def _(vector):
            t1 = colt[:, 0:1]
            t2 = colt[:, 1:2]
            qr = colt[:, 4:5]
            qi = colt[:, 5:6]
            rr = colt[:, 6:7]
            ri = colt[:, 7:8]
            i = None
            for fm in range(NF):
                lr = lamb[:, 0, fm:fm + 1]
                li = lamb[:, 1, fm:fm + 1]
                vector.memset(Pr[:, fm, 0:1], 1.0)
                vector.memset(Pi[:, fm, 0:1], 0.0)
                vector.tensor_copy(Pr[:, fm, 1:2], lr)
                ctx.vs(vector, vector.tensor_copy(Pi[:, fm, 1:2], li))
                n = 2
                while n < SCH:
                    _cmul_cols(ctx, vector, qr, qi, Pr[:, fm, n - 1:n],
                               Pi[:, fm, n - 1:n], lr, li, t1, t2)
                    blk = slice(0, n)
                    dst = slice(n, 2 * n)
                    tmp = sh.tmpP[:, 0:n]
                    ctx.vs(vector, vector.tensor_scalar_mul(tmp, Pi[:, fm, blk], qi))
                    vector.scalar_tensor_tensor(Pr[:, fm, dst], Pr[:, fm, blk], qr,
                                                tmp, ALU.mult, ALU.subtract)
                    ctx.vs(vector, vector.tensor_scalar_mul(tmp, Pr[:, fm, blk], qi))
                    ctx.vs(vector, vector.scalar_tensor_tensor(
                        Pi[:, fm, dst], Pi[:, fm, blk], qr, tmp, ALU.mult, ALU.add))
                    n *= 2
                _cmul_cols(ctx, vector, rr, ri, Pr[:, fm, SCH - 1:SCH],
                           Pi[:, fm, SCH - 1:SCH], lr, li, t1, t2)
                vector.memset(Qc[:, 0, fm, 0:1], 1.0)
                ctx.vs(vector, vector.memset(Qc[:, 1, fm, 0:1], 0.0))
                for cc in range(1, NSC):
                    _cmul_cols(ctx, vector, Qc[:, 0, fm, cc:cc + 1],
                               Qc[:, 1, fm, cc:cc + 1],
                               Qc[:, 0, fm, cc - 1:cc], Qc[:, 1, fm, cc - 1:cc],
                               rr, ri, t1, t2)
            ctx.VE.inc(vector.tensor_copy(colt[:, 0:1], omdump[:, 0:1]))
        ctx.vec(_)

        for dtm in range(DT):
            def _(tensor, dtm=dtm):
                if dtm == 0:
                    ctx.VE.wait(tensor)
                else:
                    ctx.VE.wait(tensor)
                i = None
                for nt in range(8):
                    tensor.matmul(ctx.pb[0][:, 0:512],
                                  c_bw[:, nt, dtm * 128:(dtm + 1) * 128],
                                  c_fret[:, nt, 0:512],
                                  start=(nt == 0), stop=(nt == 7))
                for nt in range(8):
                    tensor.matmul(ctx.pb[1][:, 0:512],
                                  c_bw[:, nt, dtm * 128:(dtm + 1) * 128],
                                  c_fimt[:, nt, 0:512],
                                  start=(nt == 0), stop=(nt == 7))
                for nt in range(8):
                    i = tensor.matmul(ctx.pb[2][:, 0:1],
                                      c_bw[:, nt, dtm * 128:(dtm + 1) * 128],
                                      c_fret[:, nt, 512:513],
                                      start=(nt == 0), stop=(nt == 7))
                ctx.TE.inc(i)
            ctx.ten(_)

            def _(vector, dtm=dtm):
                ctx.TE.wait(vector)
                vector.tensor_copy(bhat16[:, dtm, 0:512], ctx.pb[0][:, 0:512])
                vector.tensor_copy(bhat16[:, dtm, 512:1024], ctx.pb[1][:, 0:512])
                ctx.VE.inc(vector.tensor_copy(bhat16[:, dtm, 1024:1025],
                                              ctx.pb[2][:, 0:1]))
            ctx.vec(_)

        for chm in range(NF):
            for half in range(2):
                def _(tensor, chm=chm, half=half):
                    ctx.VE.wait(tensor)
                    i = None
                    for nt in range(8):
                        tensor.matmul(ctx.pb[0][:, 0:512],
                                      c_finr[:, nt, chm * 128:(chm + 1) * 128],
                                      c_cwt[:, nt, half * 512:(half + 1) * 512],
                                      start=(nt == 0), stop=(nt == 7))
                    for nt in range(8):
                        i = tensor.matmul(ctx.pb[1][:, 0:512],
                                          c_fini[:, nt, chm * 128:(chm + 1) * 128],
                                          c_cwt[:, nt, half * 512:(half + 1) * 512],
                                          start=(nt == 0), stop=(nt == 7))
                    ctx.TE.inc(i)
                ctx.ten(_)

                def _(vector, chm=chm, half=half):
                    ctx.TE.wait(vector)
                    hs = slice(half * 512, (half + 1) * 512)
                    vector.tensor_scalar_mul(chat16[:, chm, hs], ctx.pb[0][:, 0:512], 2.0)
                    i = vector.tensor_scalar_mul(chat16[:, 4 + chm, hs],
                                                 ctx.pb[1][:, 0:512], -2.0)
                    ctx.VE.inc(i)
                ctx.vec(_)

        def _(vector):
            vector.tensor_scalar_mul(chat16[0:1, 0, :], chat16[0:1, 0, :], 0.5)
            ctx.VE.inc(vector.tensor_scalar_mul(chat16[0:1, 4, :], chat16[0:1, 4, :], 0.5))
        ctx.vec(_)

        for half in range(2):
            def _(tensor, half=half):
                ctx.VE.wait(tensor)
                i = None
                for nt in range(8):
                    i = tensor.matmul(ctx.pb[2][0:1, 0:512],
                                      c_finr[:, nt, 512:513],
                                      c_cwt[:, nt, half * 512:(half + 1) * 512],
                                      start=(nt == 0), stop=(nt == 7))
                ctx.TE.inc(i)
            ctx.ten(_)

            def _(vector, half=half):
                ctx.TE.wait(vector)
                ctx.VE.inc(vector.tensor_copy(
                    chat16[0:1, 8, half * 512:(half + 1) * 512], ctx.pb[2][0:1, 0:512]))
            ctx.vec(_)
    if dbg is not None:
        def _(gpsimd):
            ctx.VE.wait(gpsimd)
            ctx.GP.inc(gpsimd.dma_start(out=dbg["lamb"], in_=lamb.ap().rearrange("p a b -> p (a b)")), 16)
            ctx.GP.inc(gpsimd.dma_start(out=dbg["omega"], in_=omdump[:]), 16)
            ctx.GP.inc(gpsimd.dma_start(out=dbg["P0"], in_=Pr[:, 0, :]), 16)
            ctx.GP.inc(gpsimd.dma_start(out=dbg["P0i"], in_=Pi[:, 0, :]), 16)
            ctx.GP.inc(gpsimd.dma_start(out=dbg["qc"], in_=Qc.ap().rearrange("p a b c -> p (a b c)")), 16)
            ctx.GP.inc(gpsimd.dma_start(out=dbg["bhat0"], in_=bhat16[:, 0, :]), 16)
            ctx.GP.inc(gpsimd.dma_start(out=dbg["chat0"], in_=chat16[:, 0, :]), 16)
            ctx.GP.inc(gpsimd.dma_start(out=dbg["chat4"], in_=chat16[:, 4, :]), 16)
            ctx.GP.inc(gpsimd.dma_start(out=dbg["chat8"], in_=chat16[0:1, 8, :]), 16)
        ctx.gps(_)

    xb = ctx.sb("ssm_x", [128, DT, SCH], F32)
    ln16 = ctx.sb("ssm_ln16", [128, DT, SCH], BF16)
    scr = make_ln_scratch(ctx, "ssm")
    hch16 = ctx.sb("ssm_h16", [128, 9, SCH], BF16)
    A2 = ctx.sb("ssm_A", [128, 2, SCH], F32)
    S2 = ctx.sb("ssm_S", [128, 2, SCH], F32)
    bp2 = ctx.sb("ssm_bp", [128, 2, SCH], F32)
    tmpw = ctx.sb("ssm_tmpw", [128, SCH], F32)
    carry = ctx.sb("ssm_carry", [128, 2, NF], F32)
    cw_rel = {}
    ssm_ln_rel = {0: 0}
    car512 = ctx.sb("ssm_car512", [1, 2], F32)
    h512 = ctx.sb("ssm_h512", [1, SCH], F32)
    xb_release = {}

    for b in range(BC):
        def _(vector, b=b):
            if b > 0:
                ctx.TE.wait(vector)
            vector.memset(carry[:], 0.0)
            ctx.VE.inc(vector.memset(car512[:], 0.0))
        ctx.vec(_)

        for cc in range(NSC):
            c = b * NSC + cc

            def _(sync, c=c):
                if c == 0:
                    ctx.VE.wait(sync)
                    ctx.TE.wait(sync)
                else:
                    ctx.VE.wait(sync, xb_release[c - 1])
                    ctx.GP.wait(sync)
                ctx.LD.inc(sync.dma_start(out=xb[:], in_=x_dram_chunk(x_in, c)), 16)
            ctx.sync(_)
            ld_at = ctx.LD.n

            def _(vector, ld_at=ld_at):
                vector.wait_ge(ctx.LD.h, ld_at)
            ctx.vec(_)

            emit_ln(ctx, sh, xb, ln16, scr, ncols=SCH)
            ssm_ln_rel[0] = ctx.VE.n

            def _(tensor):
                ctx.VE.wait(tensor)
                i = None
                for dt_ in range(DT):
                    i = tensor.matmul(ctx.pb[0][0:1, 0:SCH],
                                      bhat16[:, dt_, 1024:1025],
                                      ln16[:, dt_, :],
                                      start=(dt_ == 0), stop=(dt_ == DT - 1))
                ctx.TE.inc(i)
            ctx.ten(_)

            def _(vector, cc=cc):
                ctx.TE.wait(vector)
                ctx.vs(vector, vector.tensor_tensor_scan(
                    h512[:], sh.ones_col_f32[0:1, 0:1].broadcast_to((1, SCH)),
                    ctx.pb[0][0:1, 0:SCH], car512[0:1, 0:1], ALU.mult, ALU.add))
                vector.tensor_copy(car512[0:1, 0:1], h512[0:1, SCH - 1:SCH])
                ctx.VE.inc(vector.tensor_copy(hch16[0:1, 8, :], h512[:]))
            ctx.vec(_)
            f512_rel = ctx.VE.n

            scan_rel = {}
            bte_at = {}
            for fm in range(NF):
                def _(tensor, fm=fm):
                    if fm == 0:
                        ctx.VE.wait(tensor, f512_rel)
                    if cw_rel.get(fm) is not None:
                        ctx.VE.wait(tensor, max(cw_rel[fm], ssm_ln_rel[0]))
                    i = None
                    for dt_ in range(DT):
                        tensor.matmul(ctx.pb[2 * fm][:, 0:SCH],
                                      bhat16[:, dt_, fm * 128:(fm + 1) * 128],
                                      ln16[:, dt_, :],
                                      start=(dt_ == 0), stop=(dt_ == DT - 1))
                    for dt_ in range(DT):
                        i = tensor.matmul(ctx.pb[2 * fm + 1][:, 0:SCH],
                                          bhat16[:, dt_, 512 + fm * 128:512 + (fm + 1) * 128],
                                          ln16[:, dt_, :],
                                          start=(dt_ == 0), stop=(dt_ == DT - 1))
                    ctx.TE.inc(i)
                ctx.ten(_)
                bte_at[fm] = ctx.TE.n

                def _(vector, fm=fm, cc=cc):
                    vector.wait_ge(ctx.TE.h, bte_at[fm])
                    br = ctx.pb[2 * fm][:, 0:SCH]
                    bi = ctx.pb[2 * fm + 1][:, 0:SCH]
                    qr = Qc[:, 0, fm, cc:cc + 1]
                    qi = Qc[:, 1, fm, cc:cc + 1]
                    vector.tensor_scalar_mul(tmpw[:], Pi[:, fm, :], qi)
                    vector.scalar_tensor_tensor(A2[:, 0, :], Pr[:, fm, :], qr,
                                                tmpw[:], ALU.mult, ALU.subtract)
                    vector.tensor_scalar_mul(tmpw[:], Pr[:, fm, :], qi)
                    vector.scalar_tensor_tensor(A2[:, 1, :], Pi[:, fm, :], qr,
                                                tmpw[:], ALU.mult, ALU.add)
                    vector.tensor_mul(bp2[:, 0, :], A2[:, 0, :], br)
                    vector.tensor_mul(tmpw[:], A2[:, 1, :], bi)
                    vector.tensor_tensor(bp2[:, 0, :], bp2[:, 0, :], tmpw[:],
                                         ALU.subtract)
                    vector.tensor_mul(bp2[:, 1, :], A2[:, 0, :], bi)
                    vector.tensor_mul(tmpw[:], A2[:, 1, :], br)
                    vector.tensor_tensor(bp2[:, 1, :], bp2[:, 1, :], tmpw[:], ALU.add)
                    ones_b = sh.ones_col_f32[:, 0:1].broadcast_to((128, SCH))
                    vector.tensor_tensor_scan(S2[:, 0, :], ones_b, bp2[:, 0, :],
                                              carry[:, 0, fm:fm + 1], ALU.mult, ALU.add)
                    ctx.vs(vector, vector.tensor_tensor_scan(
                        S2[:, 1, :], ones_b, bp2[:, 1, :],
                        carry[:, 1, fm:fm + 1], ALU.mult, ALU.add))
                    vector.tensor_copy(carry[:, 0, fm:fm + 1], S2[:, 0, SCH - 1:SCH])
                    vector.tensor_copy(carry[:, 1, fm:fm + 1], S2[:, 1, SCH - 1:SCH])
                    vector.tensor_mul(tmpw[:], A2[:, 0, :], S2[:, 0, :])
                    vector.tensor_mul(bp2[:, 0, :], A2[:, 1, :], S2[:, 1, :])
                    vector.tensor_tensor(hch16[:, fm, :], tmpw[:], bp2[:, 0, :],
                                         ALU.add)
                    vector.tensor_mul(tmpw[:], A2[:, 0, :], S2[:, 1, :])
                    vector.tensor_mul(bp2[:, 1, :], A2[:, 1, :], S2[:, 0, :])
                    i = vector.tensor_tensor(hch16[:, 4 + fm, :], tmpw[:],
                                             bp2[:, 1, :], ALU.subtract)
                    ctx.VE.inc(i)
                ctx.vec(_)
                scan_rel[fm] = ctx.VE.n

            for w in range(NF):
                def _(tensor, w=w):
                    ctx.VE.wait(tensor, scan_rel[3])
                    i = None
                    for m in range(2):
                        md = w * 2 + m
                        for chm in range(8):
                            tensor.matmul(ctx.pb[2 * w + m][:, 0:SCH],
                                          chat16[:, chm, md * 128:(md + 1) * 128],
                                          hch16[:, chm, :],
                                          start=(chm == 0), stop=False)
                        i = tensor.matmul(ctx.pb[2 * w + m][:, 0:SCH],
                                          chat16[0:1, 8, md * 128:(md + 1) * 128],
                                          hch16[0:1, 8, :],
                                          start=False, stop=True)
                    ctx.TE.inc(i)
                ctx.ten(_)
                cte_at = ctx.TE.n

                def _(vector, w=w, c=c, cte_at=cte_at):
                    vector.wait_ge(ctx.TE.h, cte_at)
                    i = None
                    for m in range(2):
                        md = w * 2 + m
                        vector.scalar_tensor_tensor(
                            tmpw[:], ln16[:, md, :], Dcol[:, md:md + 1],
                            xb[:, md, :], ALU.mult, ALU.add)
                        i = vector.tensor_tensor(xb[:, md, :],
                                                 ctx.pb[2 * w + m][:, 0:SCH],
                                                 tmpw[:], ALU.add)
                    ctx.VE.inc(i)
                ctx.vec(_)
                cw_rel[w] = ctx.VE.n


            xb_release[c] = ctx.VE.n

            def _(gpsimd, c=c):
                ctx.VE.wait(gpsimd)
                ctx.GP.inc(gpsimd.dma_start(out=x_dram_chunk(x_out, c), in_=xb[:]), 16)
            ctx.gps(_)

    def _(sync):
        ctx.GP.wait(sync)
    ctx.sync(_)


def make_ssm_consts():
    N = 1024
    f = np.arange(513)
    n = np.arange(1024)
    ang = 2.0 * np.pi * np.outer(n, f) / N
    fret = np.cos(ang).astype(np.float32)
    fimt = (-np.sin(ang)).astype(np.float32)
    finvre = (np.cos(ang) / N).astype(np.float32)
    finvim = (np.sin(ang) / N).astype(np.float32)
    k = np.arange(1, 513)
    msint = (-2.0 * np.sin(2.0 * np.pi * np.outer(k, f) / N)).astype(np.float32)
    msint[511, :] = 0.0
    return dict(fret=fret, fimt=fimt, finvre=finvre, finvim=finvim, msint=msint)


def build_l0attn_program():
    nc = bass.Bass("TRN2", target_bir_lowering=False, debug=False)
    x_in = nc.dram_tensor("x", [D, TOK], F32, kind="ExternalInput").ap()
    bw = nc.dram_tensor("bw", [D, D], F32, kind="ExternalInput").ap()
    cwt = nc.dram_tensor("cwt", [D, D], F32, kind="ExternalInput").ap()
    a512 = nc.dram_tensor("a512", [128, 4], F32, kind="ExternalInput").ap()
    dvec = nc.dram_tensor("dvec", [128, 8], F32, kind="ExternalInput").ap()
    fret = nc.dram_tensor("fret", [D, 513], F32, kind="ExternalInput").ap()
    fimt = nc.dram_tensor("fimt", [D, 513], F32, kind="ExternalInput").ap()
    finvre = nc.dram_tensor("finvre", [D, 513], F32, kind="ExternalInput").ap()
    finvim = nc.dram_tensor("finvim", [D, 513], F32, kind="ExternalInput").ap()
    msint = nc.dram_tensor("msint", [512, 513], F32, kind="ExternalInput").ap()
    x_out = nc.dram_tensor("y", [D, TOK], F32, kind="ExternalOutput").ap()
    with contextlib.ExitStack() as es:
        ctx = make_ctx(es, nc)
        sh = Shared(ctx)
        emit_l0attn(ctx, sh, dict(x_in=x_in, x_out=x_out, bw=bw, cwt=cwt,
                                  a512=a512, dvec=dvec, fret=fret, fimt=fimt,
                                  finvre=finvre, finvim=finvim, msint=msint))
    return nc



def emit_l1attn(ctx, sh, aps, dbg=None):
    nc = ctx.nc
    x_in, x_out = aps["x_in"], aps["x_out"]

    ln16 = ctx.sb("wkv_ln16", [128, DT, SEQ], BF16)
    yf16 = ctx.sb("wkv_yf", [128, DT, SEQ], BF16)
    yb16 = ctx.sb("wkv_yb", [128, DT, SEQ], BF16)
    lam2 = ctx.sb("wkv_lam", [128, 2, DT], F32)
    ucol2 = ctx.sb("wkv_u", [128, 2, DT], F32)
    carN = ctx.sb("wkv_carN", [128, DT], F32)
    carD = ctx.sb("wkv_carD", [128, DT], F32)
    ek2 = ctx.sb("wkv_ek", [128, 2, CH], F32)
    eu2 = ctx.sb("wkv_eu", [128, 2, CH], F32)
    ekv = ctx.sb("wkv_ekv", [128, CH], F32)
    euv = ctx.sb("wkv_euv", [128, CH], F32)
    num = ctx.sb("wkv_num", [128, CH + 1], F32)
    den = ctx.sb("wkv_den", [128, CH + 1], F32)
    nsum2 = ctx.sb("wkv_nsum", [128, 2, CH], F32)
    dsum2 = ctx.sb("wkv_dsum", [128, 2, CH], F32)
    yv = ctx.sb("wkv_y", [128, CH], F32)
    sig = ctx.sb("wkv_sig", [128, 4, CH], F32)
    wkv_xc = ctx.sb("wkv_xc", [128, DT, CH], F32)
    minus1 = ctx.sb("wkv_m1", [128, 1], F32)
    pend = [None]
    rev0 = (slice(None), slice(None, None, -1))

    jgp_rel = {}

    def emit_y(p):
        j, dr, tc, se_at, J = p["j"], p["dr"], p["tc"], p["se_at"], p["J"]

        def _(gpsimd):
            gpsimd.wait_ge(ctx.SE.h, se_at)
            gpsimd.tensor_mul(yv[:], nsum2[:, j % 2, :], dsum2[:, j % 2, :])
            ytile = (yf16 if dr == 0 else yb16)[:, j, tc:tc + CH]
            if dr == 0:
                i = gpsimd.tensor_mul(ytile, yv[:], sig[:, j % 4, :])
            else:
                i = gpsimd.tensor_mul(ytile, yv[rev0], sig[:, j % 4, :])
            ctx.GP.inc(i, 1)
        ctx.gps(_)
        jgp_rel[J] = ctx.GP.n
    scr = {
        "x16": yb16[:, :, 0:CH],
        "xsq": yb16[:, :, CH:2 * CH],
        "srow": ctx.sb("wkv_srow", [1, CH], F32),
        "rstd": ctx.sb("wkv_rstd", [1, CH], F32),
        "ab16": ctx.sb("wkv_ab16", [1, 2 * CH], BF16),
    }

    def _(sync):
        ctx.LD.inc(sync.dma_start(out=ucol2[:, 0, :], in_=aps["tf"]), 16)
        ctx.LD.inc(sync.dma_start(out=ucol2[:, 1, :], in_=aps["tfr"]), 16)
        ctx.LD.inc(sync.dma_start(out=lam2[:, 0, :], in_=aps["td"]), 16)
        ctx.LD.inc(sync.dma_start(out=lam2[:, 1, :], in_=aps["tdr"]), 16)
    ctx.sync(_)

    def _(scalar):
        ctx.LD.wait(scalar)
        ctx.SE.inc(scalar.activation(lam2[:, :, :], lam2[:, :, :], AF.Exp))
    ctx.sca(_)

    def _(vector):
        ctx.SE.wait(vector)
        vector.memset(minus1[:], -1.0)
        ctx.VE.inc(vector.tensor_scalar_mul(lam2[:, :, :], lam2[:, :, :], -1.0))
        ctx.VE.wait(vector)
    ctx.vec(_)

    def _(scalar):
        ctx.VE.wait(scalar)
        ctx.SE.inc(scalar.activation(lam2[:, :, :], lam2[:, :, :], AF.Exp))
    ctx.sca(_)

    xb_release = {}
    for b in range(BC):
        for dr in range(2):
            with contextlib.ExitStack() as hes:
                rkvh = hes.enter_context(
                    nc.sbuf_tensor(f"rkvh_{b}_{dr}", [128, DT, 3072], BF16))
                stg = hes.enter_context(
                    nc.sbuf_tensor(f"rkvstg_{b}_{dr}", [128, DT, 128], F32))
                for sl in range(24):
                    def _(sync, sl=sl, b=b, dr=dr):
                        ctx.VE.wait(sync)
                        if sl == 0:
                            ctx.TE.wait(sync)
                            ctx.GP.wait(sync)
                        ctx.LD.inc(sync.dma_start(
                            out=stg[:],
                            in_=aps["rkvt"].rearrange("(dt p) h -> p dt h", p=128)[
                                :, :, dr * 3072 + sl * 128:dr * 3072 + (sl + 1) * 128]), 16)
                    ctx.sync(_)

                    def _(vector, sl=sl):
                        ctx.LD.wait(vector)
                        ctx.VE.inc(vector.tensor_copy(
                            rkvh[:, :, sl * 128:(sl + 1) * 128], stg[:]))
                    ctx.vec(_)

                def _(vector):
                    vector.memset(carN[:], 0.0)
                    ctx.VE.inc(vector.memset(carD[:], 0.0))
                ctx.vec(_)

                lam_j = lam2[:, dr, :]
                u_j = ucol2[:, dr, :]
                jse_rel = {}
                jve_rel = {}
                ln_rel = {0: 0}
                for ci in range(NSC):
                    c = (NSC - 1 - ci) if dr else ci
                    tc = c * CH
                    gc = b * NSC + c

                    if dr == 0:
                        def _(sync, gc=gc):
                            if gc > 0:
                                ctx.VE.wait(sync, xb_release.get(gc - 1, 0))
                            ctx.LD.inc(sync.dma_start(
                                out=wkv_xc[:], in_=x_dram_chunk(x_in, gc)), 16)
                        ctx.sync(_)
                        ld_at = ctx.LD.n

                        def _(vector, ld_at=ld_at):
                            vector.wait_ge(ctx.LD.h, ld_at)
                        ctx.vec(_)
                        emit_ln(ctx, sh, wkv_xc, ln16[:, :, tc:tc + CH], scr)
                        xb_release[gc] = ctx.VE.n
                        ln_rel[0] = ctx.VE.n

                    for j in range(DT):
                        b0 = 3 * (j % 2)
                        pr, pk, pv = ctx.pb[b0], ctx.pb[b0 + 1], ctx.pb[b0 + 2]
                        J = ci * DT + j

                        def _(tensor, j=j, pr=pr, pk=pk, pv=pv, tc=tc, J=J):
                            if J >= 2:
                                ctx.SE.wait(tensor, jse_rel[J - 2])
                                ctx.VE.wait(tensor, max(jve_rel[J - 2], ln_rel[0]))
                            else:
                                ctx.SE.wait(tensor)
                                ctx.VE.wait(tensor)
                            for dt_ in range(DT):
                                tensor.matmul(pr[:, 0:CH],
                                              rkvh[:, dt_, j * 128:(j + 1) * 128],
                                              ln16[:, dt_, tc:tc + CH],
                                              start=(dt_ == 0), stop=(dt_ == DT - 1))
                            for dt_ in range(DT):
                                tensor.matmul(pk[:, 0:CH],
                                              rkvh[:, dt_, 1024 + j * 128:1024 + (j + 1) * 128],
                                              ln16[:, dt_, tc:tc + CH],
                                              start=(dt_ == 0), stop=(dt_ == DT - 1))
                            i = None
                            for dt_ in range(DT):
                                i = tensor.matmul(pv[:, 0:CH],
                                                  rkvh[:, dt_, 2048 + j * 128:2048 + (j + 1) * 128],
                                                  ln16[:, dt_, tc:tc + CH],
                                                  start=(dt_ == 0), stop=(dt_ == DT - 1))
                            ctx.TE.inc(i)
                        ctx.ten(_)

                        def _(scalar, j=j, pr=pr, pk=pk, J=J):
                            ctx.TE.wait(scalar)
                            if J - 4 in jgp_rel:
                                scalar.wait_ge(ctx.GP.h, jgp_rel[J - 4])
                            scalar.activation(sig[:, j % 4, :], pr[:, 0:CH], AF.Sigmoid)
                            scalar.activation(eu2[:, j % 2, :], pk[:, 0:CH], AF.Exp,
                                              bias=u_j[:, j:j + 1])
                            ctx.SE.inc(scalar.activation(ek2[:, j % 2, :],
                                                         pk[:, 0:CH], AF.Exp))
                        ctx.sca(_)
                        jse_rel[J] = ctx.SE.n

                        if pend[0] is not None:
                            emit_y(pend[0])
                            pend[0] = None

                        rev = (slice(None), slice(None, None, -1))

                        def _(vector, j=j, pv=pv, dr=dr, J=J):
                            ctx.SE.wait(vector)
                            if J - 2 in jgp_rel:
                                vector.wait_ge(ctx.GP.h, jgp_rel[J - 2])
                            eu = eu2[:, j % 2, :]
                            ek = ek2[:, j % 2, :]
                            nsum = nsum2[:, j % 2, :]
                            dsum = dsum2[:, j % 2, :]
                            vector.tensor_mul(ekv[:], ek, pv[:, 0:CH])
                            ctx.vs(vector, vector.tensor_mul(euv[:], eu, pv[:, 0:CH]))
                            vector.tensor_copy(num[:, 0:1], carN[:, j:j + 1])
                            vector.tensor_copy(den[:, 0:1], carD[:, j:j + 1])
                            lb = lam_j[:, j:j + 1].broadcast_to((128, CH))
                            if dr == 0:
                                vector.tensor_tensor_scan(
                                    num[:, 1:CH + 1], lb, ekv[:], carN[:, j:j + 1],
                                    ALU.mult, ALU.add)
                                ctx.vs(vector, vector.tensor_tensor_scan(
                                    den[:, 1:CH + 1], lb, ek, carD[:, j:j + 1],
                                    ALU.mult, ALU.add))
                                vector.tensor_tensor(nsum[:], num[:, 0:CH],
                                                     euv[:], ALU.add)
                                vector.tensor_tensor(dsum[:], den[:, 0:CH],
                                                     eu, ALU.add)
                                vector.tensor_copy(carN[:, j:j + 1], num[:, CH:CH + 1])
                                i = vector.tensor_copy(carD[:, j:j + 1], den[:, CH:CH + 1])
                            else:
                                vector.tensor_tensor_scan(
                                    num[:, 1:CH + 1], lb, ekv[rev], carN[:, j:j + 1],
                                    ALU.mult, ALU.add)
                                ctx.vs(vector, vector.tensor_tensor_scan(
                                    den[:, 1:CH + 1], lb, ek[rev], carD[:, j:j + 1],
                                    ALU.mult, ALU.add))
                                vector.tensor_tensor(nsum[:], num[:, 0:CH],
                                                     euv[:, CH - 1::-1], ALU.add)
                                vector.tensor_tensor(dsum[:], den[:, 0:CH],
                                                     eu[:, CH - 1::-1], ALU.add)
                                vector.tensor_copy(carN[:, j:j + 1], num[:, CH:CH + 1])
                                i = vector.tensor_copy(carD[:, j:j + 1], den[:, CH:CH + 1])
                            ctx.VE.inc(i)
                        ctx.vec(_)
                        jve_rel[J] = ctx.VE.n

                        def _(scalar, j=j, J=J):
                            ctx.VE.wait(scalar)
                            if J - 2 in jgp_rel:
                                scalar.wait_ge(ctx.GP.h, jgp_rel[J - 2])
                            ds = dsum2[:, j % 2, :]
                            scalar.activation(ds, ds, AF.Ln)
                            ctx.SE.inc(scalar.activation(ds, ds, AF.Exp,
                                                         scale=minus1[:, 0:1]))
                        ctx.sca(_)
                        pend[0] = dict(j=j, dr=dr, tc=tc, se_at=ctx.SE.n, J=J)


                if pend[0] is not None:
                    emit_y(pend[0])
                    pend[0] = None

        with contextlib.ExitStack() as oes:
            ow16 = oes.enter_context(nc.sbuf_tensor(f"ow16_{b}", [128, 16, D], BF16))
            ostg = oes.enter_context(nc.sbuf_tensor(f"ostg_{b}", [128, 16, 128], F32))
            xo = wkv_xc
            for sl in range(8):
                def _(sync, sl=sl):
                    ctx.VE.wait(sync)
                    if sl == 0:
                        ctx.TE.wait(sync)
                        ctx.GP.wait(sync)
                    ctx.LD.inc(sync.dma_start(
                        out=ostg[:],
                        in_=aps["outwt"].rearrange("(kt p) d -> p kt d", p=128)[
                            :, :, sl * 128:(sl + 1) * 128]), 16)
                ctx.sync(_)

                def _(vector, sl=sl):
                    ctx.LD.wait(vector)
                    ctx.VE.inc(vector.tensor_copy(
                        ow16[:, :, sl * 128:(sl + 1) * 128], ostg[:]))
                ctx.vec(_)

            for cc in range(NSC):
                gc = b * NSC + cc
                tc = cc * CH

                def _(sync, gc=gc):
                    ctx.VE.wait(sync)
                    ctx.GP.wait(sync)
                    ctx.LD.inc(sync.dma_start(out=xo[:], in_=x_dram_chunk(x_in, gc)), 16)
                ctx.sync(_)
                ld_at = ctx.LD.n

                def _(tensor, tc=tc, ld_at=ld_at):
                    ctx.VE.wait(tensor)
                    ctx.GP.wait(tensor)
                    i = None
                    for md in range(DT):
                        for kt in range(DT):
                            tensor.matmul(ctx.pb[md][:, 0:CH],
                                          ow16[:, kt, md * 128:(md + 1) * 128],
                                          yf16[:, kt, tc:tc + CH],
                                          start=(kt == 0), stop=False)
                        for kt in range(DT):
                            i = tensor.matmul(ctx.pb[md][:, 0:CH],
                                              ow16[:, 8 + kt, md * 128:(md + 1) * 128],
                                              yb16[:, kt, tc:tc + CH],
                                              start=False, stop=(kt == DT - 1))
                    ctx.TE.inc(i)
                ctx.ten(_)

                def _(vector, ld_at=ld_at):
                    vector.wait_ge(ctx.LD.h, ld_at)
                    ctx.TE.wait(vector)
                    i = None
                    for md in range(DT):
                        i = vector.tensor_tensor(xo[:, md, :], ctx.pb[md][:, 0:CH],
                                                 xo[:, md, :], ALU.add)
                    ctx.VE.inc(i)
                ctx.vec(_)

                def _(gpsimd, gc=gc):
                    ctx.VE.wait(gpsimd)
                    ctx.GP.inc(gpsimd.dma_start(out=x_dram_chunk(x_out, gc),
                                                in_=xo[:]), 16)
                ctx.gps(_)

    def _(sync):
        ctx.GP.wait(sync)
    ctx.sync(_)


def build_l1attn_program():
    nc = bass.Bass("TRN2", target_bir_lowering=False, debug=False)
    x_in = nc.dram_tensor("x", [D, TOK], F32, kind="ExternalInput").ap()
    rkvt = nc.dram_tensor("rkvt", [D, 6144], F32, kind="ExternalInput").ap()
    outwt = nc.dram_tensor("outwt", [2048, D], F32, kind="ExternalInput").ap()
    td = nc.dram_tensor("td", [128, 8], F32, kind="ExternalInput").ap()
    tf = nc.dram_tensor("tf", [128, 8], F32, kind="ExternalInput").ap()
    tdr = nc.dram_tensor("tdr", [128, 8], F32, kind="ExternalInput").ap()
    tfr = nc.dram_tensor("tfr", [128, 8], F32, kind="ExternalInput").ap()
    x_out = nc.dram_tensor("y", [D, TOK], F32, kind="ExternalOutput").ap()
    with contextlib.ExitStack() as es:
        ctx = make_ctx(es, nc)
        sh = Shared(ctx)
        emit_l1attn(ctx, sh, dict(x_in=x_in, x_out=x_out, rkvt=rkvt, outwt=outwt,
                                  td=td, tf=tf, tdr=tdr, tfr=tfr))
    return nc



MG = 1024
NMG = TOK // MG
INTER = 2048
NMP = 16


def emit_moe_preconvert(ctx, aps):
    nc = ctx.nc
    w1s, w2s = aps["w1s"], aps["w2s"]
    with contextlib.ExitStack() as pes:
        stg = [pes.enter_context(nc.sbuf_tensor(f"moepc_stg{i}", [128, 2, DT, 128], F32))
               for i in range(3)]
        cv = [pes.enter_context(nc.sbuf_tensor(f"moepc_cv{i}", [128, 2, DT, 128], BF16))
              for i in range(3)]
        stg2 = [pes.enter_context(nc.sbuf_tensor(f"moepc_stg2{i}", [128, 4, 1024], F32))
                for i in range(2)]
        cv2 = [pes.enter_context(nc.sbuf_tensor(f"moepc_cv2{i}", [128, 4, 1024], BF16))
               for i in range(2)]
        gp_rel = {}
        ve_relp = {}
        it = 0
        for e in range(8):
            for mp in range(NMP):
                s_, c_ = stg[it % 3], cv[it % 3]

                def _(sync, e=e, mp=mp, s_=s_, it=it):
                    if it >= 3:
                        ctx.GP.wait(sync, gp_rel[it - 3])
                        ctx.VE.wait(sync, ve_relp[it - 3])
                    src_ = aps["w1t"][e].rearrange("(dt p) c -> p dt c", p=128)
                    g = src_[:, :, mp * 128:(mp + 1) * 128].unsqueeze(1)
                    u = src_[:, :, INTER + mp * 128:INTER + (mp + 1) * 128].unsqueeze(1)
                    ctx.LD.inc(sync.dma_start(out=s_[:, 0:1, :, :], in_=g), 16)
                    ctx.LD.inc(sync.dma_start(out=s_[:, 1:2, :, :], in_=u), 16)
                ctx.sync(_)

                def _(vector, s_=s_, c_=c_):
                    ctx.LD.wait(vector)
                    ctx.VE.inc(vector.tensor_copy(c_[:], s_[:]))
                ctx.vec(_)
                ve_relp[it] = ctx.VE.n

                def _(gpsimd, e=e, mp=mp, c_=c_):
                    ctx.VE.wait(gpsimd)
                    ctx.GP.inc(gpsimd.dma_start(
                        out=w1s[e, mp],
                        in_=c_.ap().rearrange("p a b c -> p (a b c)")), 16)
                ctx.gps(_)
                gp_rel[it] = ctx.GP.n
                it += 1
        it2 = 0
        gp_rel2 = {}
        ve_rel2 = {}
        for e in range(8):
            for half in range(4):
                s_, c_ = stg2[it2 % 2], cv2[it2 % 2]

                def _(sync, e=e, half=half, s_=s_, it2=it2):
                    if it2 >= 2:
                        ctx.GP.wait(sync, gp_rel2[it2 - 2])
                        ctx.VE.wait(sync, ve_rel2[it2 - 2])
                    src_ = aps["w2t"][e].rearrange("(kt p) d -> p kt d", p=128)
                    ctx.LD.inc(sync.dma_start(
                        out=s_[:], in_=src_[:, half * 4:(half + 1) * 4, :]), 16)
                ctx.sync(_)

                def _(vector, s_=s_, c_=c_):
                    ctx.LD.wait(vector)
                    ctx.VE.inc(vector.tensor_copy(c_[:], s_[:]))
                ctx.vec(_)
                ve_rel2[it2] = ctx.VE.n

                def _(gpsimd, e=e, half=half, c_=c_):
                    ctx.VE.wait(gpsimd)
                    ctx.GP.inc(gpsimd.dma_start(
                        out=w2s[e, :, half * 4096:(half + 1) * 4096],
                        in_=c_.ap().rearrange("p a b -> p (a b)")), 16)
                ctx.gps(_)
                gp_rel2[it2] = ctx.GP.n
                it2 += 1


def emit_l1moe(ctx, sh, aps, scale):
    nc = ctx.nc
    x_in, x_out = aps["x_in"], aps["x_out"]

    acc = ctx.sb("moe_acc", [128, DT, MG], F32)
    Y16 = ctx.sb("moe_y", [128, NMP, MG], BF16)
    ln16 = ctx.sb("moe_ln", [128, DT, MG], BF16)
    w2sb2 = [ctx.sb(f"moe_w2_{i}", [128, 16, D], BF16) for i in range(2)]
    wsl = [ctx.sb(f"moe_w1sl{i}", [128, 2, DT, 128], BF16) for i in range(2)]
    xb = Y16.ap()[:, 0:8, :].bitcast(F32)
    cbrow = ctx.sb("moe_cbrow", [1, 8, MG], F32)
    cbbc = ctx.sb("moe_cbbc", [128, 2, CH], F32)
    comb = ctx.sb("moe_comb", [128, MG // 128, 8], F32)
    silu_sb = ctx.sb("moe_silu", [128, 4, CH], F32)
    tmpf = ctx.sb("moe_tmpf", [128, CH], F32)
    rw16 = ctx.sb("moe_rw16", [128, DT, 8], BF16)
    identf = ctx.sb("moe_ident", [128, 128], F32)
    rcol = ctx.sb("moe_rcol", [128, 4], F32)
    scr = {
        "x16": Y16[:, 8:16, 0:CH],
        "xsq": Y16[:, 8:16, CH:2 * CH],
        "srow": ctx.sb("moe_srow", [1, CH], F32),
        "rstd": ctx.sb("moe_rstd", [1, CH], F32),
        "ab16": ctx.sb("moe_ab16", [1, 2 * CH], BF16),
    }

    def _(sync):
        ctx.LD.inc(sync.dma_start(out=identf[:], in_=aps["ident"]), 16)
    ctx.sync(_)

    def _(sync):
        ctx.LD.inc(sync.dma_start(
            out=cbbc.ap().rearrange("p a b -> p (a b)")[:, 0:64].rearrange(
                "p (a b) -> p a b", a=DT),
            in_=aps["routert"].rearrange("(dt p) e -> p dt e", p=128)), 16)
    ctx.sync(_)

    def _(vector):
        ctx.LD.wait(vector)
        ctx.VE.inc(vector.tensor_copy(
            rw16[:], cbbc.ap().rearrange("p a b -> p (a b)")[:, 0:64].rearrange(
                "p (a b) -> p a b", a=DT)))
    ctx.vec(_)

    xb_release = {}
    ws_rel = {}
    w2_rel = {}
    qse_rel = {}
    qve_rel = {}
    moe_ln_rel = {0: 0}
    w2ve_rel = {}
    cb_rel = {0: 0}
    for g in range(NMG):
        t0 = g * MG
        for ch2 in range(2):
            gc = g * 2 + ch2

            def _(sync, gc=gc):
                if gc == 0:
                    ctx.VE.wait(sync)
                    ctx.TE.wait(sync)
                else:
                    ctx.VE.wait(sync, xb_release.get(gc - 1, 0))
                    ctx.GP.wait(sync)
                ctx.LD.inc(sync.dma_start(out=xb[:], in_=x_dram_chunk(x_in, gc)), 16)
            ctx.sync(_)
            ld_at = ctx.LD.n

            def _(vector, ld_at=ld_at):
                vector.wait_ge(ctx.LD.h, ld_at)
            ctx.vec(_)
            emit_ln(ctx, sh, xb, ln16[:, :, ch2 * CH:(ch2 + 1) * CH], scr)
            xb_release[gc] = ctx.VE.n

        def _(tensor):
            ctx.VE.wait(tensor)
            i = None
            for tt in range(MG // 128):
                for dt_ in range(DT):
                    i = tensor.matmul(ctx.pb[6][:, tt * 8:(tt + 1) * 8],
                                      ln16[:, dt_, tt * 128:(tt + 1) * 128],
                                      rw16[:, dt_, :],
                                      start=(dt_ == 0), stop=(dt_ == DT - 1))
            ctx.TE.inc(i)
        ctx.ten(_)

        r3 = lambda ap: ap.rearrange("p (a b) -> p a b", a=8)
        eg3 = r3(silu_sb[:, 0, 0:64])
        probs = r3(silu_sb[:, 0, 64:128])
        msk = r3(silu_sb[:, 0, 128:192])
        ge2 = r3(silu_sb[:, 0, 192:256])
        sr = r3(silu_sb[:, 1, 0:8])
        sr = silu_sb[:, 1, 0:8].rearrange("p (a b) -> p a b", b=1)
        m1r = silu_sb[:, 1, 8:16].rearrange("p (a b) -> p a b", b=1)
        m2r = silu_sb[:, 1, 16:24].rearrange("p (a b) -> p a b", b=1)

        def _(scalar):
            ctx.TE.wait(scalar)
            ctx.SE.inc(scalar.activation(
                silu_sb[:, 0, 0:64], ctx.pb[6][:, 0:64], AF.Exp))
        ctx.sca(_)

        def _(vector):
            ctx.SE.wait(vector)
            ctx.vs(vector, vector.reduce_sum(sr, eg3, mybir.AxisListType.X))
            ctx.vs(vector, vector.reciprocal(sr, sr))
            ctx.vs(vector, vector.tensor_tensor(
                probs, eg3, sr.broadcast_to((128, 8, 8)), ALU.mult))
            ctx.vs(vector, vector.reduce_max(m1r, probs, mybir.AxisListType.X))
            ctx.vs(vector, vector.tensor_tensor(
                msk, probs, m1r.broadcast_to((128, 8, 8)), ALU.is_ge))
            ctx.vs(vector, vector.scalar_tensor_tensor(
                msk, msk, -1e9, probs, ALU.mult, ALU.add))
            ctx.vs(vector, vector.reduce_max(m2r, msk, mybir.AxisListType.X))
            ctx.vs(vector, vector.tensor_tensor(
                ge2, probs, m2r.broadcast_to((128, 8, 8)), ALU.is_ge))
            ctx.VE.inc(vector.tensor_tensor(
                comb.ap().rearrange("p a b -> p a b"), probs, ge2, ALU.mult))
        ctx.vec(_)

        for e in range(8):
            for half in range(MG // 512):
                def _(tensor, e=e, half=half):
                    if e == 0 and half == 0:
                        ctx.VE.wait(tensor)
                    else:
                        ctx.VE.wait(tensor)
                    i = None
                    for q in range(4):
                        tt = half * 4 + q
                        i = tensor.transpose(ctx.pb[7][0:1, q * 128:(q + 1) * 128],
                                             comb[:, tt, e:e + 1], identf[:])
                    ctx.TE.inc(i)
                ctx.ten(_)

                def _(vector, e=e, half=half):
                    ctx.TE.wait(vector)
                    ctx.VE.inc(vector.tensor_copy(
                        cbrow[0:1, e, half * 512:(half + 1) * 512],
                        ctx.pb[7][0:1, 0:512]))
                ctx.vec(_)

        def _(vector):
            ctx.TE.wait(vector)
            ctx.VE.inc(vector.memset(acc[:], 0.0))
        ctx.vec(_)
        moe_ln_rel[0] = ctx.VE.n

        for e in range(8):
            w2sb = w2sb2[e % 2]
            E = g * 8 + e

            def _(scalar, e=e, g=g, w2sb=w2sb, E=E):
                if E - 2 in w2_rel:
                    scalar.wait_ge(ctx.TE.h, w2_rel[E - 2])
                else:
                    ctx.TE.wait(scalar)
                ctx.SD.inc(scalar.dma_start(
                    out=w2sb.ap().rearrange("p a b -> p (a b)"),
                    in_=aps["w2s"][e]), 16)
            ctx.sca(_)
            w2_at = ctx.SD.n

            def _(tensor, e=e, E=E):
                ve_t = moe_ln_rel[0]
                if E - 1 in w2ve_rel:
                    ve_t = max(ve_t, w2ve_rel[E - 1])
                    ctx.VE.wait(tensor, ve_t)
                else:
                    ctx.VE.wait(tensor)
                tensor.matmul(ctx.pb[6][:, 0:CH], sh.ones_row32[:],
                              cbrow[0:1, e, 0:CH], start=True, stop=True)
                ctx.TE.inc(tensor.matmul(ctx.pb[7][:, 0:CH], sh.ones_row32[:],
                                         cbrow[0:1, e, CH:2 * CH],
                                         start=True, stop=True))
            ctx.ten(_)

            def _(vector):
                ctx.TE.wait(vector)
                vector.tensor_copy(cbbc[:, 0, :], ctx.pb[6][:, 0:CH])
                ctx.VE.inc(vector.tensor_copy(cbbc[:, 1, :], ctx.pb[7][:, 0:CH]))
            ctx.vec(_)
            cb_rel[0] = ctx.VE.n

            te_at = {}
            for mp in range(NMP):
                ws = wsl[mp % 2]
                F = E * NMP + mp

                def _(sync, e=e, mp=mp, ws=ws, F=F):
                    if F - 2 in ws_rel:
                        ctx.TE.wait(sync, ws_rel[F - 2])
                    else:
                        ctx.TE.wait(sync)
                    ctx.LD.inc(sync.dma_start(
                        out=ws.ap().rearrange("p a b c -> p (a b c)"),
                        in_=aps["w1s"][e, mp]), 16)
                ctx.sync(_)
                sl_at = ctx.LD.n

                for ch2 in range(2):
                    q = mp * 2 + ch2
                    Q = E * 32 + q
                    bg, bu = ctx.pb[2 * (q % 4)], ctx.pb[2 * (q % 4) + 1]

                    def _(tensor, mp=mp, ch2=ch2, q=q, Q=Q, bg=bg, bu=bu, ws=ws,
                          sl_at=sl_at, e=e):
                        tensor.wait_ge(ctx.LD.h, sl_at)
                        if Q - 4 in qse_rel:
                            ctx.SE.wait(tensor, qse_rel[Q - 4])
                            ctx.VE.wait(tensor, max(qve_rel[Q - 4], moe_ln_rel[0],
                                                    cb_rel[0]))
                        else:
                            ctx.SE.wait(tensor)
                            ctx.VE.wait(tensor)
                        for k in range(DT):
                            tensor.matmul(bg[:, 0:CH], ws[:, 0, k, :],
                                          ln16[:, k, ch2 * CH:(ch2 + 1) * CH],
                                          start=(k == 0), stop=(k == DT - 1))
                        i = None
                        for k in range(DT):
                            i = tensor.matmul(bu[:, 0:CH], ws[:, 1, k, :],
                                              ln16[:, k, ch2 * CH:(ch2 + 1) * CH],
                                              start=(k == 0), stop=(k == DT - 1))
                        ctx.TE.inc(i)
                    ctx.ten(_)
                    te_at[q] = ctx.TE.n

                def _(scalar, mp=mp):
                    for ch2 in range(2):
                        q = mp * 2 + ch2
                        scalar.wait_ge(ctx.TE.h, te_at[q])
                        i = scalar.activation(silu_sb[:, q % 4, :],
                                              ctx.pb[2 * (q % 4)][:, 0:CH], AF.Silu)
                    ctx.SE.inc(i)
                ctx.sca(_)
                qse_rel[E * 32 + mp * 2] = qse_rel[E * 32 + mp * 2 + 1] = ctx.SE.n

                def _(vector, mp=mp):
                    ctx.SE.wait(vector)
                    i = None
                    for ch2 in range(2):
                        q = mp * 2 + ch2
                        bu = ctx.pb[2 * (q % 4) + 1]
                        vector.tensor_tensor(tmpf[:], bu[:, 0:CH],
                                             cbbc[:, ch2, :], ALU.mult)
                        i = vector.tensor_mul(
                            Y16[:, mp, ch2 * CH:(ch2 + 1) * CH],
                            tmpf[:], silu_sb[:, q % 4, :])
                    ctx.VE.inc(i)
                ctx.vec(_)
                qve_rel[E * 32 + mp * 2] = qve_rel[E * 32 + mp * 2 + 1] = ctx.VE.n
                ws_rel[F] = ctx.TE.n

            wv_rel = {}
            for ch2 in range(2):
                for wave in range(4):
                    r = ch2 * 4 + wave
                    bk = 4 + 2 * (r % 2)

                    def _(tensor, ch2=ch2, wave=wave, r=r, bk=bk, w2_at=w2_at,
                          w2sb=w2sb):
                        tensor.wait_ge(ctx.SD.h, w2_at)
                        if r >= 2:
                            ctx.VE.wait(tensor, wv_rel[r - 2])
                        else:
                            t = qve_rel[E * 32 + 31]
                            if E - 1 in w2ve_rel:
                                t = max(t, w2ve_rel[E - 1])
                                ctx.VE.wait(tensor, t)
                            else:
                                ctx.VE.wait(tensor)
                        i = None
                        for m in range(2):
                            md = wave * 2 + m
                            for kt in range(16):
                                i = tensor.matmul(
                                    ctx.pb[bk + m][:, 0:CH],
                                    w2sb[:, kt, md * 128:(md + 1) * 128],
                                    Y16[:, kt, ch2 * CH:(ch2 + 1) * CH],
                                    start=(kt == 0), stop=(kt == 15))
                        ctx.TE.inc(i)
                    ctx.ten(_)

                    def _(vector, ch2=ch2, wave=wave, bk=bk):
                        ctx.TE.wait(vector)
                        i = None
                        for m in range(2):
                            md = wave * 2 + m
                            sl_ = acc[:, md, ch2 * CH:(ch2 + 1) * CH]
                            i = vector.tensor_tensor(sl_, ctx.pb[bk + m][:, 0:CH],
                                                     sl_, ALU.add)
                        ctx.VE.inc(i)
                    ctx.vec(_)
                    wv_rel[r] = ctx.VE.n
            w2_rel[E] = ctx.TE.n
            w2ve_rel[E] = ctx.VE.n

        for ch2 in range(2):
            gc = g * 2 + ch2

            def _(sync, gc=gc):
                ctx.VE.wait(sync)
                ctx.GP.wait(sync)
                ctx.LD.inc(sync.dma_start(out=xb[:], in_=x_dram_chunk(x_in, gc)), 16)
            ctx.sync(_)
            ld_at = ctx.LD.n

            def _(vector, ch2=ch2, ld_at=ld_at):
                vector.wait_ge(ctx.LD.h, ld_at)
                i = None
                for md in range(DT):
                    i = vector.scalar_tensor_tensor(
                        xb[:, md, :], acc[:, md, ch2 * CH:(ch2 + 1) * CH], scale,
                        xb[:, md, :], ALU.mult, ALU.add)
                ctx.VE.inc(i)
            ctx.vec(_)

            def _(gpsimd, gc=gc):
                ctx.VE.wait(gpsimd)
                ctx.GP.inc(gpsimd.dma_start(out=x_dram_chunk(x_out, gc), in_=xb[:]), 16)
            ctx.gps(_)

    def _(sync):
        ctx.GP.wait(sync)
    ctx.sync(_)


def build_l1moe_program():
    nc = bass.Bass("TRN2", target_bir_lowering=False, debug=False)
    x_in = nc.dram_tensor("x", [D, TOK], F32, kind="ExternalInput").ap()
    w1t = nc.dram_tensor("w1t", [8, D, 2 * INTER], F32, kind="ExternalInput").ap()
    w2t = nc.dram_tensor("w2t", [8, INTER, D], F32, kind="ExternalInput").ap()
    routert = nc.dram_tensor("routert", [D, 8], F32, kind="ExternalInput").ap()
    ident = nc.dram_tensor("ident", [128, 128], F32, kind="ExternalInput").ap()
    x_out = nc.dram_tensor("y", [D, TOK], F32, kind="ExternalOutput").ap()
    w1s = nc.dram_tensor("w1s", [8, NMP, 128, 2048], BF16, kind="Internal").ap()
    w2s = nc.dram_tensor("w2s", [8, 128, 16384], BF16, kind="Internal").ap()
    with contextlib.ExitStack() as es:
        ctx = make_ctx(es, nc)
        sh = Shared(ctx)
        emit_moe_preconvert(ctx, dict(w1t=w1t, w2t=w2t, w1s=w1s, w2s=w2s))
        emit_l1moe(ctx, sh, dict(x_in=x_in, x_out=x_out, w1s=w1s, w2s=w2s,
                                 routert=routert, ident=ident), 0.5)
    return nc



def build_fused_program():
    nc = bass.Bass("TRN2", target_bir_lowering=False, debug=False)
    x_in = nc.dram_tensor("x", [D, TOK], F32, kind="ExternalInput").ap()
    wg = nc.dram_tensor("wg", [D, FFN_H], I8, kind="ExternalInput").ap()
    wu = nc.dram_tensor("wu", [D, FFN_H], I8, kind="ExternalInput").ap()
    wd = nc.dram_tensor("wd", [FFN_H, D], I8, kind="ExternalInput").ap()
    bw = nc.dram_tensor("bw", [D, D], F32, kind="ExternalInput").ap()
    cwt = nc.dram_tensor("cwt", [D, D], F32, kind="ExternalInput").ap()
    a512 = nc.dram_tensor("a512", [128, 4], F32, kind="ExternalInput").ap()
    dvec = nc.dram_tensor("dvec", [128, 8], F32, kind="ExternalInput").ap()
    fret = nc.dram_tensor("fret", [D, 513], F32, kind="ExternalInput").ap()
    fimt = nc.dram_tensor("fimt", [D, 513], F32, kind="ExternalInput").ap()
    finvre = nc.dram_tensor("finvre", [D, 513], F32, kind="ExternalInput").ap()
    finvim = nc.dram_tensor("finvim", [D, 513], F32, kind="ExternalInput").ap()
    msint = nc.dram_tensor("msint", [512, 513], F32, kind="ExternalInput").ap()
    rkvt = nc.dram_tensor("rkvt", [D, 6144], F32, kind="ExternalInput").ap()
    outwt = nc.dram_tensor("outwt", [2048, D], F32, kind="ExternalInput").ap()
    td = nc.dram_tensor("td", [128, 8], F32, kind="ExternalInput").ap()
    tf = nc.dram_tensor("tf", [128, 8], F32, kind="ExternalInput").ap()
    tdr = nc.dram_tensor("tdr", [128, 8], F32, kind="ExternalInput").ap()
    tfr = nc.dram_tensor("tfr", [128, 8], F32, kind="ExternalInput").ap()
    w1t = nc.dram_tensor("w1t", [8, D, 2 * INTER], F32, kind="ExternalInput").ap()
    w2t = nc.dram_tensor("w2t", [8, INTER, D], F32, kind="ExternalInput").ap()
    routert = nc.dram_tensor("routert", [D, 8], F32, kind="ExternalInput").ap()
    ident = nc.dram_tensor("ident", [128, 128], F32, kind="ExternalInput").ap()
    y_out = nc.dram_tensor("y", [D, TOK], F32, kind="ExternalOutput").ap()
    xa = nc.dram_tensor("xa", [D, TOK], F32, kind="Internal").ap()
    xb_ = nc.dram_tensor("xb", [D, TOK], F32, kind="Internal").ap()
    w1s = nc.dram_tensor("w1s", [8, NMP, 128, 2048], BF16, kind="Internal").ap()
    w2s = nc.dram_tensor("w2s", [8, 128, 16384], BF16, kind="Internal").ap()

    ssm = dict(bw=bw, cwt=cwt, a512=a512, dvec=dvec, fret=fret, fimt=fimt,
               finvre=finvre, finvim=finvim, msint=msint)
    wkv = dict(rkvt=rkvt, outwt=outwt, td=td, tf=tf, tdr=tdr, tfr=tfr)
    moe = dict(w1s=w1s, w2s=w2s, routert=routert, ident=ident)

    with contextlib.ExitStack() as es:
        ctx = make_ctx(es, nc)
        sh = Shared(ctx)

        def phase(emit, *args):
            with contextlib.ExitStack() as pes:
                ctx.cur_es = pes
                emit(*args)
            ctx.cur_es = None
            phase_barrier(ctx)

        phase(emit_moe_preconvert, ctx, dict(w1t=w1t, w2t=w2t, w1s=w1s, w2s=w2s))
        phase(emit_l0ffn, ctx, sh,
              dict(x_in=x_in, x_out=xa, wg=wg, wu=wu, wd=wd), 0.5)
        phase(emit_l0attn, ctx, sh, dict(x_in=xa, x_out=xb_, **ssm))
        phase(emit_l0ffn, ctx, sh,
              dict(x_in=xb_, x_out=xa, wg=wg, wu=wu, wd=wd), 0.5)
        phase(emit_l1moe, ctx, sh, dict(x_in=xa, x_out=xb_, **moe), 0.5)
        phase(emit_l1attn, ctx, sh, dict(x_in=xb_, x_out=xa, **wkv))
        phase(emit_l1moe, ctx, sh, dict(x_in=xa, x_out=y_out, **moe), 0.5)
    return nc



_PROG_CACHE = {}


def _get_prog(name):
    if name not in _PROG_CACHE:
        _PROG_CACHE[name] = {"l0ffn": build_l0ffn_program,
                             "l0attn": build_l0attn_program,
                             "l1attn": build_l1attn_program,
                             "l1moe": build_l1moe_program,
                             "fused": build_fused_program}[name]()
    return _PROG_CACHE[name]


def _col8(a):
    return np.ascontiguousarray(a.reshape(8, 128).T)


def kernel(L0_attn_u, L0_attn_a_params, L0_attn_B_w, L0_attn_C_w, L0_attn_D,
           L0_ffn_w_gate, L0_ffn_w_up, L0_ffn_w_down,
           L1_attn_rkv_w, L1_attn_out_w, L1_attn_time_decay, L1_attn_time_first,
           L1_attn_time_decay_rev, L1_attn_time_first_rev,
           L1_ffn_w1, L1_ffn_w2, L1_ffn_router_w):
    x = np.asarray(L0_attn_u)
    xT = np.ascontiguousarray(np.asarray(x).transpose(2, 0, 1))
    shards = [np.ascontiguousarray(xT[:, 4 * c:4 * (c + 1), :]).reshape(D, TOK)
              for c in range(NCORE)]

    wg = np.ascontiguousarray(_prep_int8(np.asarray(L0_ffn_w_gate)).T)
    wu = np.ascontiguousarray(_prep_int8(np.asarray(L0_ffn_w_up)).T)
    wd = np.ascontiguousarray(_prep_int8(np.asarray(L0_ffn_w_down)).T)
    cst = make_ssm_consts()
    ssm_w = dict(bw=np.ascontiguousarray(np.asarray(L0_attn_B_w)),
                 cwt=np.ascontiguousarray(np.asarray(L0_attn_C_w).T),
                 a512=_col8(np.asarray(L0_attn_a_params).reshape(4, 128).reshape(512))
                 if False else np.ascontiguousarray(
                     np.asarray(L0_attn_a_params).reshape(4, 128).T),
                 dvec=_col8(np.asarray(L0_attn_D)), **cst)
    wkv_w = dict(rkvt=np.ascontiguousarray(np.asarray(L1_attn_rkv_w).T),
                 outwt=np.ascontiguousarray(np.asarray(L1_attn_out_w).T),
                 td=_col8(np.asarray(L1_attn_time_decay)),
                 tf=_col8(np.asarray(L1_attn_time_first)),
                 tdr=_col8(np.asarray(L1_attn_time_decay_rev)),
                 tfr=_col8(np.asarray(L1_attn_time_first_rev)))
    moe_w = dict(w1t=np.ascontiguousarray(np.asarray(L1_ffn_w1).transpose(0, 2, 1)),
                 w2t=np.ascontiguousarray(np.asarray(L1_ffn_w2).transpose(0, 2, 1)),
                 routert=np.ascontiguousarray(np.asarray(L1_ffn_router_w).T),
                 ident=np.eye(128, dtype=np.float32))

    nc = _get_prog("fused")
    weights = {"wg": wg, "wu": wu, "wd": wd, **ssm_w, **wkv_w, **moe_w}
    in_maps = [{"x": shards[c], **weights} for c in range(NCORE)]
    res = run_bass_kernel_spmd(nc, in_maps, core_ids=list(range(NCORE)))
    ys = [res.results[c]["y"] for c in range(NCORE)]

    out = np.empty((32, SEQ, D), dtype=np.float32)
    for c in range(NCORE):
        out[4 * c:4 * (c + 1)] = ys[c].reshape(D, BC, SEQ).transpose(1, 2, 0)
    return out


# version 13
# speedup vs baseline: 1.1450x; 1.0515x over previous
import contextlib
import numpy as np

import concourse.bass as bass
import concourse.mybir as mybir
from concourse.bass_utils import run_bass_kernel_spmd

F32 = mybir.dt.float32
BF16 = mybir.dt.bfloat16
I8 = mybir.dt.int8
AF = mybir.ActivationFunctionType
ALU = mybir.AluOpType

D = 1024
DT = 8
SEQ = 2048
BC = 4
TOK = BC * SEQ
CH = 512
NCH = TOK // CH
NCORE = 8

FFN_H = 3072
HT = FFN_H // 128

LN_EPS = 1e-5
DEQ = 1.0 / 127.0



class SemC:
    def __init__(self, nc, es, name):
        self.h = es.enter_context(nc.semaphore(name))
        self.n = 0

    def inc(self, instr, k=1):
        instr.then_inc(self.h, k)
        self.n += k
        return instr

    def wait(self, eng, at=None):
        eng.wait_ge(self.h, self.n if at is None else at)


class Ctx:
    def __init__(self, nc, es, block):
        self.nc = nc
        self.es = es
        self.block = block
        self.LD = SemC(nc, es, "ld")
        self.GP = SemC(nc, es, "gp")
        self.SD = SemC(nc, es, "sd")
        self.TE = SemC(nc, es, "te")
        self.VE = SemC(nc, es, "ve")
        self.SE = SemC(nc, es, "se")
        self.pb = [es.enter_context(nc.psum_tensor(f"pb{i}", [128, 512], F32))
                   for i in range(8)]

    def uname(self, name):
        self._uid = getattr(self, "_uid", 0) + 1
        return f"{name}_u{self._uid}"

    def sb(self, name, shape, dtype):
        es = getattr(self, "cur_es", None) or self.es
        return es.enter_context(self.nc.sbuf_tensor(self.uname(name), shape, dtype))

    def vs(self, vector, instr):
        self.VE.inc(instr)
        self.VE.wait(vector)

    def sync(self, fn): self.block.sync(fn)
    def gps(self, fn): self.block.gpsimd(fn)
    def ten(self, fn): self.block.tensor(fn)
    def vec(self, fn): self.block.vector(fn)
    def sca(self, fn): self.block.scalar(fn)


def phase_barrier(ctx):
    sems = [ctx.LD, ctx.GP, ctx.SD, ctx.TE, ctx.VE, ctx.SE]

    def mk(engine):
        for s in sems:
            s.wait(engine)
    ctx.sync(mk)
    ctx.gps(mk)
    ctx.ten(mk)
    ctx.vec(mk)
    ctx.sca(mk)


def make_ctx(es, nc):
    block = es.enter_context(nc.Block())
    return Ctx(nc, es, block)


def x_dram_chunk(x_ap, c, cols=CH):
    v = x_ap.rearrange("(dt p) t -> p dt t", p=128)
    return v[:, :, c * cols:(c + 1) * cols]


class Shared:
    def __init__(self, ctx):
        self.ones_col16 = ctx.sb("ones_col16", [128, 1], BF16)
        self.ones_row16 = ctx.sb("ones_row16", [1, 128], BF16)
        self.ones_col_f32 = ctx.sb("ones_col_f32", [128, 1], F32)
        self.ones_row32 = ctx.sb("ones_row32", [1, 128], F32)
        self.tmpP = ctx.sb("sh_tmpP", [128, 256], F32)

        def _(vector):
            vector.memset(self.ones_col16[:], 1.0)
            vector.memset(self.ones_row16[:], 1.0)
            vector.memset(self.ones_row32[:], 1.0)
            ctx.VE.inc(vector.memset(self.ones_col_f32[:], 1.0))
        ctx.vec(_)


def make_ln_scratch(ctx, tag=""):
    return {
        "x16": ctx.sb(f"ln_x16{tag}", [128, DT, CH], BF16),
        "xsq": ctx.sb(f"ln_xsq{tag}", [128, DT, CH], BF16),
        "srow": ctx.sb(f"ln_srow{tag}", [1, CH], F32),
        "rstd": ctx.sb(f"ln_rstd{tag}", [1, CH], F32),
        "ab16": ctx.sb(f"ln_ab16{tag}", [1, 2 * CH], BF16),
    }


def emit_ln(ctx, sh, x_sb, ln16, scr, ncols=CH):
    x16, xsq = scr["x16"], scr["xsq"]
    srow, rstd, ab16 = scr["srow"], scr["rstd"], scr["ab16"]
    pa, pbk = ctx.pb[6], ctx.pb[7]

    def _(vector):
        ctx.TE.wait(vector)
        for j in range(DT):
            vector.tensor_copy(x16[:, j, :ncols], x_sb[:, j, :ncols])
        i = None
        for j in range(DT):
            i = vector.tensor_mul(xsq[:, j, :ncols], x16[:, j, :ncols],
                                  x16[:, j, :ncols])
        ctx.VE.inc(i)
    ctx.vec(_)

    def _(tensor):
        ctx.VE.wait(tensor)
        for j in range(DT):
            tensor.matmul(pa[0:1, 0:ncols], sh.ones_col16[:], x16[:, j, :ncols],
                          start=(j == 0), stop=(j == DT - 1))
        i = None
        for j in range(DT):
            i = tensor.matmul(pbk[0:1, 0:ncols], sh.ones_col16[:], xsq[:, j, :ncols],
                              start=(j == 0), stop=(j == DT - 1))
        ctx.TE.inc(i)
    ctx.ten(_)

    def _(vector):
        ctx.TE.wait(vector)
        vector.tensor_scalar_mul(srow[:, :ncols], pa[0:1, 0:ncols], 1.0 / D)
        vector.tensor_mul(rstd[:, :ncols], srow[:, :ncols], srow[:, :ncols])
        vector.scalar_tensor_tensor(
            rstd[:, :ncols], pbk[0:1, 0:ncols], 1.0 / D, rstd[:, :ncols],
            ALU.mult, ALU.subtract)
        ctx.VE.inc(vector.tensor_scalar_add(rstd[:, :ncols], rstd[:, :ncols],
                                            LN_EPS))
    ctx.vec(_)

    def _(scalar):
        ctx.VE.wait(scalar)
        ctx.SE.inc(scalar.activation(rstd[:, :ncols], rstd[:, :ncols], AF.Sqrt))
    ctx.sca(_)

    def _(vector):
        ctx.SE.wait(vector)
        ctx.VE.inc(vector.reciprocal(rstd[:, :ncols], rstd[:, :ncols]))
        ctx.VE.wait(vector)
        vector.tensor_copy(ab16[0:1, 0:ncols], rstd[:, :ncols])
        vector.tensor_mul(srow[:, :ncols], srow[:, :ncols], rstd[:, :ncols])
        ctx.VE.inc(vector.tensor_scalar_mul(ab16[0:1, CH:CH + ncols],
                                            srow[:, :ncols], -1.0))
    ctx.vec(_)

    def _(tensor):
        ctx.VE.wait(tensor)
        tensor.matmul(pa[:, 0:ncols], sh.ones_row16[:], ab16[0:1, 0:ncols],
                      start=True, stop=True)
        ctx.TE.inc(tensor.matmul(pbk[:, 0:ncols], sh.ones_row16[:],
                                 ab16[0:1, CH:CH + ncols], start=True, stop=True))
    ctx.ten(_)

    def _(vector):
        ctx.TE.wait(vector)
        i = None
        for j in range(DT):
            vector.tensor_tensor(ln16[:, j, :ncols], x16[:, j, :ncols],
                                 pa[:, 0:ncols], ALU.mult)
            i = vector.tensor_tensor(ln16[:, j, :ncols], ln16[:, j, :ncols],
                                     pbk[:, 0:ncols], ALU.add)
        ctx.VE.inc(i)
    ctx.vec(_)



def emit_l0ffn(ctx, sh, aps, scale, dbg=None):
    nc = ctx.nc
    x_in, x_out = aps["x_in"], aps["x_out"]

    wg16 = ctx.sb("wg16", [128, DT, FFN_H], BF16)
    wu16 = ctx.sb("wu16", [128, DT, FFN_H], BF16)
    wd16 = ctx.sb("wd16", [128, HT, D], BF16)
    with contextlib.ExitStack() as stage_es:
        st8 = stage_es.enter_context(nc.sbuf_tensor(ctx.uname("w_stage8"), [128, DT * FFN_H], I8))
        plan = [(aps["wg"], "(dt p) h -> p dt h", wg16, DT, FFN_H),
                (aps["wu"], "(dt p) h -> p dt h", wu16, DT, FFN_H),
                (aps["wd"], "(ht p) d -> p ht d", wd16, HT, D)]
        for wi, (w_ap, pat, w16, A, B) in enumerate(plan):
            stv = st8.ap().rearrange("p (a b) -> p a b", a=A)
            def _(sync, w_ap=w_ap, pat=pat, wi=wi, stv=stv):
                if wi > 0:
                    ctx.VE.wait(sync)
                ctx.LD.inc(sync.dma_start(out=stv, in_=w_ap.rearrange(pat, p=128)), 16)
            ctx.sync(_)

            def _(vector, w16=w16, stv=stv):
                ctx.LD.wait(vector)
                ctx.VE.inc(vector.tensor_scalar_mul(w16[:], stv, DEQ))
            ctx.vec(_)

    x_sb = [ctx.sb("ffn_x0", [128, DT, CH], F32)]
    ln16 = ctx.sb("ffn_ln16", [128, DT, CH], BF16)
    h16 = ctx.sb("ffn_h16", [128, HT, CH], BF16)
    silu_sb = ctx.sb("ffn_silu", [128, 4, CH], F32)
    scr = {
        "x16": h16[:, 8:16, :],
        "xsq": h16[:, 0:8, :],
        "srow": ctx.sb("ln_srow", [1, CH], F32),
        "rstd": ctx.sb("ln_rstd", [1, CH], F32),
        "ab16": ctx.sb("ln_ab16", [1, 2 * CH], BF16),
    }
    xb_release = {}

    for c in range(NCH):
        xb = x_sb[0]

        def _(sync, c=c, xb=xb):
            if c == 0:
                ctx.VE.wait(sync)
            else:
                ctx.VE.wait(sync, xb_release[c - 1])
                ctx.GP.wait(sync)
            ctx.LD.inc(sync.dma_start(out=xb[:], in_=x_dram_chunk(x_in, c)), 16)
        ctx.sync(_)
        ld_at = ctx.LD.n

        def _(vector, ld_at=ld_at):
            vector.wait_ge(ctx.LD.h, ld_at)
        ctx.vec(_)

        emit_ln(ctx, sh, xb, ln16, scr)

        se_rel = {}
        ve_rel = {}
        te_at = {}
        for j in range(HT):
            bg, bu = ctx.pb[2 * (j % 4)], ctx.pb[2 * (j % 4) + 1]

            def _(tensor, j=j, bg=bg, bu=bu):
                if j < 4:
                    ctx.VE.wait(tensor)
                    if j >= 2:
                        ctx.SE.wait(tensor)
                else:
                    ctx.SE.wait(tensor, se_rel[j - 4])
                    ctx.VE.wait(tensor, ve_rel[j - 4])
                for k in range(DT):
                    tensor.matmul(bg[:, 0:CH], wg16[:, k, j * 128:(j + 1) * 128],
                                  ln16[:, k, :], start=(k == 0), stop=(k == DT - 1))
                i = None
                for k in range(DT):
                    i = tensor.matmul(bu[:, 0:CH], wu16[:, k, j * 128:(j + 1) * 128],
                                      ln16[:, k, :], start=(k == 0), stop=(k == DT - 1))
                ctx.TE.inc(i)
            ctx.ten(_)
            te_at[j] = ctx.TE.n

            if j % 2 == 1:
                def _(scalar, j=j):
                    i = None
                    for jj in (j - 1, j):
                        scalar.wait_ge(ctx.TE.h, te_at[jj])
                        i = scalar.activation(silu_sb[:, jj % 4, :],
                                              ctx.pb[2 * (jj % 4)][:, 0:CH], AF.Silu)
                    ctx.SE.inc(i)
                ctx.sca(_)
                se_rel[j - 1] = se_rel[j] = ctx.SE.n

                def _(vector, j=j):
                    ctx.SE.wait(vector)
                    i = None
                    for jj in (j - 1, j):
                        i = vector.tensor_mul(h16[:, jj, :], silu_sb[:, jj % 4, :],
                                              ctx.pb[2 * (jj % 4) + 1][:, 0:CH])
                    ctx.VE.inc(i)
                ctx.vec(_)
                ve_rel[j - 1] = ve_rel[j] = ctx.VE.n


        dv_rel = {}
        for wave in range(4):
            bk = 2 + 2 * (wave % 2)

            def _(tensor, wave=wave, bk=bk):
                if wave == 0:
                    ctx.SE.wait(tensor, se_rel[HT - 1])
                    ctx.VE.wait(tensor)
                elif wave == 1:
                    pass
                else:
                    ctx.VE.wait(tensor, dv_rel[wave - 2])
                i = None
                for m in range(2):
                    mm = wave * 2 + m
                    for k in range(HT):
                        i = tensor.matmul(ctx.pb[bk + m][:, 0:CH],
                                          wd16[:, k, mm * 128:(mm + 1) * 128],
                                          h16[:, k, :],
                                          start=(k == 0), stop=(k == HT - 1))
                ctx.TE.inc(i)
            ctx.ten(_)

            def _(vector, wave=wave, bk=bk, xb=xb, c=c):
                ctx.TE.wait(vector)
                i = None
                for m in range(2):
                    mm = wave * 2 + m
                    i = vector.scalar_tensor_tensor(
                        xb[:, mm, :], ctx.pb[bk + m][:, 0:CH], scale,
                        xb[:, mm, :], ALU.mult, ALU.add)
                ctx.VE.inc(i)
            ctx.vec(_)
            dv_rel[wave] = ctx.VE.n
        xb_release[c] = ctx.VE.n

        def _(gpsimd, c=c, xb=xb):
            ctx.VE.wait(gpsimd)
            ctx.GP.inc(gpsimd.dma_start(out=x_dram_chunk(x_out, c), in_=xb[:]), 16)
        ctx.gps(_)

    def _(sync):
        ctx.GP.wait(sync)
    ctx.sync(_)



def _prep_int8(w_int32):
    return np.ascontiguousarray(w_int32.view(np.int8)[..., 0::4])


def build_l0ffn_program():
    nc = bass.Bass("TRN2", target_bir_lowering=False, debug=False)
    x_in = nc.dram_tensor("x", [D, TOK], F32, kind="ExternalInput").ap()
    wg = nc.dram_tensor("wg", [D, FFN_H], I8, kind="ExternalInput").ap()
    wu = nc.dram_tensor("wu", [D, FFN_H], I8, kind="ExternalInput").ap()
    wd = nc.dram_tensor("wd", [FFN_H, D], I8, kind="ExternalInput").ap()
    x_out = nc.dram_tensor("y", [D, TOK], F32, kind="ExternalOutput").ap()
    with contextlib.ExitStack() as es:
        ctx = make_ctx(es, nc)
        sh = Shared(ctx)
        emit_l0ffn(ctx, sh, {"x_in": x_in, "x_out": x_out,
                             "wg": wg, "wu": wu, "wd": wd}, 0.5)
    return nc



SCH = 512
NSC = SEQ // SCH
NF = 4


def _cmul_cols(ctx, v, outr, outi, ar, ai, br, bi, t1, t2):
    v.tensor_mul(t1, ar, br)
    ctx.vs(v, v.tensor_mul(t2, ai, bi))
    ctx.vs(v, v.tensor_tensor(outr, t1, t2, ALU.subtract))
    v.tensor_mul(t1, ar, bi)
    ctx.vs(v, v.tensor_mul(t2, ai, br))
    ctx.vs(v, v.tensor_tensor(outi, t1, t2, ALU.add))


def emit_l0attn(ctx, sh, aps, dbg=None):
    nc = ctx.nc
    x_in, x_out = aps["x_in"], aps["x_out"]

    bhat16 = ctx.sb("bhat16", [128, DT, 1025], BF16)
    chat16 = ctx.sb("chat16", [128, 9, D], BF16)
    Pr = ctx.sb("ssm_Pr", [128, NF, SCH], F32)
    Pi = ctx.sb("ssm_Pi", [128, NF, SCH], F32)
    Qc = ctx.sb("ssm_Qc", [128, 2, NF, NSC], F32)
    lamb = ctx.sb("ssm_lamb", [128, 2, NF], F32)
    Dcol = ctx.sb("ssm_D", [128, DT], F32)
    colt = ctx.sb("ssm_colt", [128, 8], F32)
    omdump = ctx.sb("ssm_omdump", [128, NF], F32)

    with contextlib.ExitStack() as pes:
        c_fret = pes.enter_context(nc.sbuf_tensor("c_fret", [128, 8, 513], F32))
        c_fimt = pes.enter_context(nc.sbuf_tensor("c_fimt", [128, 8, 513], F32))
        c_finr = pes.enter_context(nc.sbuf_tensor("c_finr", [128, 8, 513], F32))
        c_fini = pes.enter_context(nc.sbuf_tensor("c_fini", [128, 8, 513], F32))
        c_msin = pes.enter_context(nc.sbuf_tensor("c_msin", [128, 4, 513], F32))
        c_bw = pes.enter_context(nc.sbuf_tensor("c_bw", [128, 8, D], F32))
        c_cwt = pes.enter_context(nc.sbuf_tensor("c_cwt", [128, 8, D], F32))
        c_a = pes.enter_context(nc.sbuf_tensor("c_a", [128, 4], F32))

        def _(sync):
            for t, ap_, pat in [
                (c_fret, aps["fret"], "(nt p) f -> p nt f"),
                (c_fimt, aps["fimt"], "(nt p) f -> p nt f"),
                (c_finr, aps["finvre"], "(nt p) f -> p nt f"),
                (c_fini, aps["finvim"], "(nt p) f -> p nt f"),
                (c_msin, aps["msint"], "(kt p) f -> p kt f"),
                (c_bw, aps["bw"], "(nt p) d -> p nt d"),
                (c_cwt, aps["cwt"], "(nt p) d -> p nt d"),
            ]:
                ctx.LD.inc(sync.dma_start(out=t[:], in_=ap_.rearrange(pat, p=128)), 16)
            ctx.LD.inc(sync.dma_start(out=c_a[:], in_=aps["a512"]), 16)
            ctx.LD.inc(sync.dma_start(out=Dcol[:], in_=aps["dvec"]), 16)
        ctx.sync(_)

        def _(tensor):
            ctx.LD.wait(tensor)
            ctx.VE.wait(tensor)
            for fm in range(NF):
                i = None
                for kt in range(4):
                    i = tensor.matmul(ctx.pb[fm][:, 0:1],
                                      c_msin[:, kt, fm * 128:(fm + 1) * 128],
                                      c_a[:, kt:kt + 1],
                                      start=(kt == 0), stop=(kt == 3))
            ctx.TE.inc(i)
        ctx.ten(_)

        def _(vector):
            ctx.TE.wait(vector)
            t1 = colt[:, 0:1]
            t2 = colt[:, 1:2]
            t3 = colt[:, 2:3]
            for fm in range(NF):
                om = omdump[:, fm:fm + 1]
                ctx.vs(vector, vector.tensor_copy(om, ctx.pb[fm][:, 0:1]))
                ctx.vs(vector, vector.tensor_mul(t1, om, om))
                ctx.vs(vector, vector.tensor_scalar_add(t2, t1, 1.0))
                ctx.vs(vector, vector.reciprocal(t2, t2))
                ctx.vs(vector, vector.tensor_scalar_mul(t3, t1, -1.0))
                ctx.vs(vector, vector.tensor_scalar_add(t3, t3, 1.0))
                vector.tensor_mul(lamb[:, 0, fm:fm + 1], t3, t2)
                ctx.vs(vector, vector.tensor_scalar_mul(t3, om, 2.0))
                i = vector.tensor_mul(lamb[:, 1, fm:fm + 1], t3, t2)
            ctx.VE.inc(i)
        ctx.vec(_)

        def _(vector):
            t1 = colt[:, 0:1]
            t2 = colt[:, 1:2]
            qr = colt[:, 4:5]
            qi = colt[:, 5:6]
            rr = colt[:, 6:7]
            ri = colt[:, 7:8]
            i = None
            for fm in range(NF):
                lr = lamb[:, 0, fm:fm + 1]
                li = lamb[:, 1, fm:fm + 1]
                vector.memset(Pr[:, fm, 0:1], 1.0)
                vector.memset(Pi[:, fm, 0:1], 0.0)
                vector.tensor_copy(Pr[:, fm, 1:2], lr)
                ctx.vs(vector, vector.tensor_copy(Pi[:, fm, 1:2], li))
                n = 2
                while n < SCH:
                    _cmul_cols(ctx, vector, qr, qi, Pr[:, fm, n - 1:n],
                               Pi[:, fm, n - 1:n], lr, li, t1, t2)
                    blk = slice(0, n)
                    dst = slice(n, 2 * n)
                    tmp = sh.tmpP[:, 0:n]
                    ctx.vs(vector, vector.tensor_scalar_mul(tmp, Pi[:, fm, blk], qi))
                    vector.scalar_tensor_tensor(Pr[:, fm, dst], Pr[:, fm, blk], qr,
                                                tmp, ALU.mult, ALU.subtract)
                    ctx.vs(vector, vector.tensor_scalar_mul(tmp, Pr[:, fm, blk], qi))
                    ctx.vs(vector, vector.scalar_tensor_tensor(
                        Pi[:, fm, dst], Pi[:, fm, blk], qr, tmp, ALU.mult, ALU.add))
                    n *= 2
                _cmul_cols(ctx, vector, rr, ri, Pr[:, fm, SCH - 1:SCH],
                           Pi[:, fm, SCH - 1:SCH], lr, li, t1, t2)
                vector.memset(Qc[:, 0, fm, 0:1], 1.0)
                ctx.vs(vector, vector.memset(Qc[:, 1, fm, 0:1], 0.0))
                for cc in range(1, NSC):
                    _cmul_cols(ctx, vector, Qc[:, 0, fm, cc:cc + 1],
                               Qc[:, 1, fm, cc:cc + 1],
                               Qc[:, 0, fm, cc - 1:cc], Qc[:, 1, fm, cc - 1:cc],
                               rr, ri, t1, t2)
            ctx.VE.inc(vector.tensor_copy(colt[:, 0:1], omdump[:, 0:1]))
        ctx.vec(_)

        for dtm in range(DT):
            def _(tensor, dtm=dtm):
                if dtm == 0:
                    ctx.VE.wait(tensor)
                else:
                    ctx.VE.wait(tensor)
                i = None
                for nt in range(8):
                    tensor.matmul(ctx.pb[0][:, 0:512],
                                  c_bw[:, nt, dtm * 128:(dtm + 1) * 128],
                                  c_fret[:, nt, 0:512],
                                  start=(nt == 0), stop=(nt == 7))
                for nt in range(8):
                    tensor.matmul(ctx.pb[1][:, 0:512],
                                  c_bw[:, nt, dtm * 128:(dtm + 1) * 128],
                                  c_fimt[:, nt, 0:512],
                                  start=(nt == 0), stop=(nt == 7))
                for nt in range(8):
                    i = tensor.matmul(ctx.pb[2][:, 0:1],
                                      c_bw[:, nt, dtm * 128:(dtm + 1) * 128],
                                      c_fret[:, nt, 512:513],
                                      start=(nt == 0), stop=(nt == 7))
                ctx.TE.inc(i)
            ctx.ten(_)

            def _(vector, dtm=dtm):
                ctx.TE.wait(vector)
                vector.tensor_copy(bhat16[:, dtm, 0:512], ctx.pb[0][:, 0:512])
                vector.tensor_copy(bhat16[:, dtm, 512:1024], ctx.pb[1][:, 0:512])
                ctx.VE.inc(vector.tensor_copy(bhat16[:, dtm, 1024:1025],
                                              ctx.pb[2][:, 0:1]))
            ctx.vec(_)

        for chm in range(NF):
            for half in range(2):
                def _(tensor, chm=chm, half=half):
                    ctx.VE.wait(tensor)
                    i = None
                    for nt in range(8):
                        tensor.matmul(ctx.pb[0][:, 0:512],
                                      c_finr[:, nt, chm * 128:(chm + 1) * 128],
                                      c_cwt[:, nt, half * 512:(half + 1) * 512],
                                      start=(nt == 0), stop=(nt == 7))
                    for nt in range(8):
                        i = tensor.matmul(ctx.pb[1][:, 0:512],
                                          c_fini[:, nt, chm * 128:(chm + 1) * 128],
                                          c_cwt[:, nt, half * 512:(half + 1) * 512],
                                          start=(nt == 0), stop=(nt == 7))
                    ctx.TE.inc(i)
                ctx.ten(_)

                def _(vector, chm=chm, half=half):
                    ctx.TE.wait(vector)
                    hs = slice(half * 512, (half + 1) * 512)
                    vector.tensor_scalar_mul(chat16[:, chm, hs], ctx.pb[0][:, 0:512], 2.0)
                    i = vector.tensor_scalar_mul(chat16[:, 4 + chm, hs],
                                                 ctx.pb[1][:, 0:512], -2.0)
                    ctx.VE.inc(i)
                ctx.vec(_)

        def _(vector):
            vector.tensor_scalar_mul(chat16[0:1, 0, :], chat16[0:1, 0, :], 0.5)
            ctx.VE.inc(vector.tensor_scalar_mul(chat16[0:1, 4, :], chat16[0:1, 4, :], 0.5))
        ctx.vec(_)

        for half in range(2):
            def _(tensor, half=half):
                ctx.VE.wait(tensor)
                i = None
                for nt in range(8):
                    i = tensor.matmul(ctx.pb[2][0:1, 0:512],
                                      c_finr[:, nt, 512:513],
                                      c_cwt[:, nt, half * 512:(half + 1) * 512],
                                      start=(nt == 0), stop=(nt == 7))
                ctx.TE.inc(i)
            ctx.ten(_)

            def _(vector, half=half):
                ctx.TE.wait(vector)
                ctx.VE.inc(vector.tensor_copy(
                    chat16[0:1, 8, half * 512:(half + 1) * 512], ctx.pb[2][0:1, 0:512]))
            ctx.vec(_)
    if dbg is not None:
        def _(gpsimd):
            ctx.VE.wait(gpsimd)
            ctx.GP.inc(gpsimd.dma_start(out=dbg["lamb"], in_=lamb.ap().rearrange("p a b -> p (a b)")), 16)
            ctx.GP.inc(gpsimd.dma_start(out=dbg["omega"], in_=omdump[:]), 16)
            ctx.GP.inc(gpsimd.dma_start(out=dbg["P0"], in_=Pr[:, 0, :]), 16)
            ctx.GP.inc(gpsimd.dma_start(out=dbg["P0i"], in_=Pi[:, 0, :]), 16)
            ctx.GP.inc(gpsimd.dma_start(out=dbg["qc"], in_=Qc.ap().rearrange("p a b c -> p (a b c)")), 16)
            ctx.GP.inc(gpsimd.dma_start(out=dbg["bhat0"], in_=bhat16[:, 0, :]), 16)
            ctx.GP.inc(gpsimd.dma_start(out=dbg["chat0"], in_=chat16[:, 0, :]), 16)
            ctx.GP.inc(gpsimd.dma_start(out=dbg["chat4"], in_=chat16[:, 4, :]), 16)
            ctx.GP.inc(gpsimd.dma_start(out=dbg["chat8"], in_=chat16[0:1, 8, :]), 16)
        ctx.gps(_)

    xb = ctx.sb("ssm_x", [128, DT, SCH], F32)
    ln16 = ctx.sb("ssm_ln16", [128, DT, SCH], BF16)
    scr = make_ln_scratch(ctx, "ssm")
    hch16 = ctx.sb("ssm_h16", [128, 9, SCH], BF16)
    A2b = ctx.sb("ssm_A", [128, 2, 2, SCH], F32)
    S2b = ctx.sb("ssm_S", [128, 2, 2, SCH], F32)
    gtmp = ctx.sb("ssm_gtmp", [128, 2, SCH], F32)
    bp2 = ctx.sb("ssm_bp", [128, 2, SCH], F32)
    tmpw = ctx.sb("ssm_tmpw", [128, SCH], F32)
    carry = ctx.sb("ssm_carry", [128, 2, NF], F32)
    cw_rel = {}
    pm_rel = {}
    ssm_ln_rel = {0: 0}
    car512 = ctx.sb("ssm_car512", [1, 2], F32)
    h512 = ctx.sb("ssm_h512", [1, SCH], F32)
    xb_release = {}

    for b in range(BC):
        def _(vector, b=b):
            if b > 0:
                ctx.TE.wait(vector)
            vector.memset(carry[:], 0.0)
            ctx.VE.inc(vector.memset(car512[:], 0.0))
        ctx.vec(_)

        for cc in range(NSC):
            c = b * NSC + cc

            def _(sync, c=c):
                if c == 0:
                    ctx.VE.wait(sync)
                    ctx.TE.wait(sync)
                else:
                    ctx.VE.wait(sync, xb_release[c - 1])
                    ctx.GP.wait(sync)
                ctx.LD.inc(sync.dma_start(out=xb[:], in_=x_dram_chunk(x_in, c)), 16)
            ctx.sync(_)
            ld_at = ctx.LD.n

            def _(vector, ld_at=ld_at):
                vector.wait_ge(ctx.LD.h, ld_at)
            ctx.vec(_)

            emit_ln(ctx, sh, xb, ln16, scr, ncols=SCH)
            ssm_ln_rel[0] = ctx.VE.n

            def _(tensor):
                ctx.VE.wait(tensor)
                i = None
                for dt_ in range(DT):
                    i = tensor.matmul(ctx.pb[0][0:1, 0:SCH],
                                      bhat16[:, dt_, 1024:1025],
                                      ln16[:, dt_, :],
                                      start=(dt_ == 0), stop=(dt_ == DT - 1))
                ctx.TE.inc(i)
            ctx.ten(_)

            def _(vector, cc=cc):
                ctx.TE.wait(vector)
                ctx.vs(vector, vector.tensor_tensor_scan(
                    h512[:], sh.ones_col_f32[0:1, 0:1].broadcast_to((1, SCH)),
                    ctx.pb[0][0:1, 0:SCH], car512[0:1, 0:1], ALU.mult, ALU.add))
                vector.tensor_copy(car512[0:1, 0:1], h512[0:1, SCH - 1:SCH])
                ctx.VE.inc(vector.tensor_copy(hch16[0:1, 8, :], h512[:]))
            ctx.vec(_)
            f512_rel = ctx.VE.n

            scan_rel = {}
            bte_at = {}
            for fm in range(NF):
                def _(tensor, fm=fm):
                    if fm == 0:
                        ctx.VE.wait(tensor, f512_rel)
                    if cw_rel.get(fm) is not None:
                        ctx.VE.wait(tensor, max(cw_rel[fm], ssm_ln_rel[0]))
                    i = None
                    for dt_ in range(DT):
                        tensor.matmul(ctx.pb[2 * fm][:, 0:SCH],
                                      bhat16[:, dt_, fm * 128:(fm + 1) * 128],
                                      ln16[:, dt_, :],
                                      start=(dt_ == 0), stop=(dt_ == DT - 1))
                    for dt_ in range(DT):
                        i = tensor.matmul(ctx.pb[2 * fm + 1][:, 0:SCH],
                                          bhat16[:, dt_, 512 + fm * 128:512 + (fm + 1) * 128],
                                          ln16[:, dt_, :],
                                          start=(dt_ == 0), stop=(dt_ == DT - 1))
                    ctx.TE.inc(i)
                ctx.ten(_)
                bte_at[fm] = ctx.TE.n

                def _(vector, fm=fm, cc=cc):
                    vector.wait_ge(ctx.TE.h, bte_at[fm])
                    kk = fm - 2 if fm >= 2 else fm + 2
                    if kk in pm_rel:
                        vector.wait_ge(ctx.GP.h, pm_rel[kk])
                    A2 = A2b[:, fm % 2, :, :]
                    S2 = S2b[:, fm % 2, :, :]
                    br = ctx.pb[2 * fm][:, 0:SCH]
                    bi = ctx.pb[2 * fm + 1][:, 0:SCH]
                    qr = Qc[:, 0, fm, cc:cc + 1]
                    qi = Qc[:, 1, fm, cc:cc + 1]
                    vector.tensor_scalar_mul(tmpw[:], Pi[:, fm, :], qi)
                    vector.scalar_tensor_tensor(A2[:, 0, :], Pr[:, fm, :], qr,
                                                tmpw[:], ALU.mult, ALU.subtract)
                    vector.tensor_scalar_mul(tmpw[:], Pr[:, fm, :], qi)
                    vector.scalar_tensor_tensor(A2[:, 1, :], Pi[:, fm, :], qr,
                                                tmpw[:], ALU.mult, ALU.add)
                    vector.tensor_mul(bp2[:, 0, :], A2[:, 0, :], br)
                    vector.tensor_mul(tmpw[:], A2[:, 1, :], bi)
                    vector.tensor_tensor(bp2[:, 0, :], bp2[:, 0, :], tmpw[:],
                                         ALU.subtract)
                    vector.tensor_mul(bp2[:, 1, :], A2[:, 0, :], bi)
                    vector.tensor_mul(tmpw[:], A2[:, 1, :], br)
                    vector.tensor_tensor(bp2[:, 1, :], bp2[:, 1, :], tmpw[:], ALU.add)
                    ones_b = sh.ones_col_f32[:, 0:1].broadcast_to((128, SCH))
                    vector.tensor_tensor_scan(S2[:, 0, :], ones_b, bp2[:, 0, :],
                                              carry[:, 0, fm:fm + 1], ALU.mult, ALU.add)
                    ctx.vs(vector, vector.tensor_tensor_scan(
                        S2[:, 1, :], ones_b, bp2[:, 1, :],
                        carry[:, 1, fm:fm + 1], ALU.mult, ALU.add))
                    vector.tensor_copy(carry[:, 0, fm:fm + 1], S2[:, 0, SCH - 1:SCH])
                    i = vector.tensor_copy(carry[:, 1, fm:fm + 1], S2[:, 1, SCH - 1:SCH])
                    ctx.VE.inc(i)
                ctx.vec(_)
                scan_rel[fm] = ctx.VE.n

                def _(gpsimd, fm=fm):
                    gpsimd.wait_ge(ctx.VE.h, scan_rel[fm])
                    A2 = A2b[:, fm % 2, :, :]
                    S2 = S2b[:, fm % 2, :, :]
                    gpsimd.tensor_mul(gtmp[:, 0, :], A2[:, 0, :], S2[:, 0, :])
                    gpsimd.tensor_mul(gtmp[:, 1, :], A2[:, 1, :], S2[:, 1, :])
                    gpsimd.tensor_tensor(hch16[:, fm, :], gtmp[:, 0, :],
                                         gtmp[:, 1, :], ALU.add)
                    gpsimd.tensor_mul(gtmp[:, 0, :], A2[:, 0, :], S2[:, 1, :])
                    gpsimd.tensor_mul(gtmp[:, 1, :], A2[:, 1, :], S2[:, 0, :])
                    ctx.GP.inc(gpsimd.tensor_tensor(
                        hch16[:, 4 + fm, :], gtmp[:, 0, :], gtmp[:, 1, :],
                        ALU.subtract), 1)
                ctx.gps(_)
                pm_rel[fm] = ctx.GP.n

            for w in range(NF):
                def _(tensor, w=w):
                    ctx.VE.wait(tensor, scan_rel[3])
                    ctx.GP.wait(tensor, pm_rel[3])
                    i = None
                    for m in range(2):
                        md = w * 2 + m
                        for chm in range(8):
                            tensor.matmul(ctx.pb[2 * w + m][:, 0:SCH],
                                          chat16[:, chm, md * 128:(md + 1) * 128],
                                          hch16[:, chm, :],
                                          start=(chm == 0), stop=False)
                        i = tensor.matmul(ctx.pb[2 * w + m][:, 0:SCH],
                                          chat16[0:1, 8, md * 128:(md + 1) * 128],
                                          hch16[0:1, 8, :],
                                          start=False, stop=True)
                    ctx.TE.inc(i)
                ctx.ten(_)
                cte_at = ctx.TE.n

                def _(vector, w=w, c=c, cte_at=cte_at):
                    vector.wait_ge(ctx.TE.h, cte_at)
                    i = None
                    for m in range(2):
                        md = w * 2 + m
                        vector.scalar_tensor_tensor(
                            tmpw[:], ln16[:, md, :], Dcol[:, md:md + 1],
                            xb[:, md, :], ALU.mult, ALU.add)
                        i = vector.tensor_tensor(xb[:, md, :],
                                                 ctx.pb[2 * w + m][:, 0:SCH],
                                                 tmpw[:], ALU.add)
                    ctx.VE.inc(i)
                ctx.vec(_)
                cw_rel[w] = ctx.VE.n


            xb_release[c] = ctx.VE.n

            def _(gpsimd, c=c):
                ctx.VE.wait(gpsimd)
                ctx.GP.inc(gpsimd.dma_start(out=x_dram_chunk(x_out, c), in_=xb[:]), 16)
            ctx.gps(_)

    def _(sync):
        ctx.GP.wait(sync)
    ctx.sync(_)


def make_ssm_consts():
    N = 1024
    f = np.arange(513)
    n = np.arange(1024)
    ang = 2.0 * np.pi * np.outer(n, f) / N
    fret = np.cos(ang).astype(np.float32)
    fimt = (-np.sin(ang)).astype(np.float32)
    finvre = (np.cos(ang) / N).astype(np.float32)
    finvim = (np.sin(ang) / N).astype(np.float32)
    k = np.arange(1, 513)
    msint = (-2.0 * np.sin(2.0 * np.pi * np.outer(k, f) / N)).astype(np.float32)
    msint[511, :] = 0.0
    return dict(fret=fret, fimt=fimt, finvre=finvre, finvim=finvim, msint=msint)


def build_l0attn_program():
    nc = bass.Bass("TRN2", target_bir_lowering=False, debug=False)
    x_in = nc.dram_tensor("x", [D, TOK], F32, kind="ExternalInput").ap()
    bw = nc.dram_tensor("bw", [D, D], F32, kind="ExternalInput").ap()
    cwt = nc.dram_tensor("cwt", [D, D], F32, kind="ExternalInput").ap()
    a512 = nc.dram_tensor("a512", [128, 4], F32, kind="ExternalInput").ap()
    dvec = nc.dram_tensor("dvec", [128, 8], F32, kind="ExternalInput").ap()
    fret = nc.dram_tensor("fret", [D, 513], F32, kind="ExternalInput").ap()
    fimt = nc.dram_tensor("fimt", [D, 513], F32, kind="ExternalInput").ap()
    finvre = nc.dram_tensor("finvre", [D, 513], F32, kind="ExternalInput").ap()
    finvim = nc.dram_tensor("finvim", [D, 513], F32, kind="ExternalInput").ap()
    msint = nc.dram_tensor("msint", [512, 513], F32, kind="ExternalInput").ap()
    x_out = nc.dram_tensor("y", [D, TOK], F32, kind="ExternalOutput").ap()
    with contextlib.ExitStack() as es:
        ctx = make_ctx(es, nc)
        sh = Shared(ctx)
        emit_l0attn(ctx, sh, dict(x_in=x_in, x_out=x_out, bw=bw, cwt=cwt,
                                  a512=a512, dvec=dvec, fret=fret, fimt=fimt,
                                  finvre=finvre, finvim=finvim, msint=msint))
    return nc



def emit_l1attn(ctx, sh, aps, dbg=None):
    nc = ctx.nc
    x_in, x_out = aps["x_in"], aps["x_out"]

    ln16 = ctx.sb("wkv_ln16", [128, DT, SEQ], BF16)
    yf16 = ctx.sb("wkv_yf", [128, DT, SEQ], BF16)
    yb16 = ctx.sb("wkv_yb", [128, DT, SEQ], BF16)
    lam2 = ctx.sb("wkv_lam", [128, 2, DT], F32)
    ucol2 = ctx.sb("wkv_u", [128, 2, DT], F32)
    carN = ctx.sb("wkv_carN", [128, DT], F32)
    carD = ctx.sb("wkv_carD", [128, DT], F32)
    ek2 = ctx.sb("wkv_ek", [128, 2, CH], F32)
    eu2 = ctx.sb("wkv_eu", [128, 2, CH], F32)
    ekv = ctx.sb("wkv_ekv", [128, CH], F32)
    euv = ctx.sb("wkv_euv", [128, CH], F32)
    num = ctx.sb("wkv_num", [128, CH + 1], F32)
    den = ctx.sb("wkv_den", [128, CH + 1], F32)
    nsum2 = ctx.sb("wkv_nsum", [128, 2, CH], F32)
    dsum2 = ctx.sb("wkv_dsum", [128, 2, CH], F32)
    yv = ctx.sb("wkv_y", [128, CH], F32)
    sig = ctx.sb("wkv_sig", [128, 4, CH], F32)
    wkv_xc = ctx.sb("wkv_xc", [128, DT, CH], F32)
    minus1 = ctx.sb("wkv_m1", [128, 1], F32)
    pend = [None]
    rev0 = (slice(None), slice(None, None, -1))

    jgp_rel = {}

    def emit_y(p):
        j, dr, tc, se_at, J = p["j"], p["dr"], p["tc"], p["se_at"], p["J"]

        def _(gpsimd):
            gpsimd.wait_ge(ctx.SE.h, se_at)
            gpsimd.tensor_mul(yv[:], nsum2[:, j % 2, :], dsum2[:, j % 2, :])
            ytile = (yf16 if dr == 0 else yb16)[:, j, tc:tc + CH]
            if dr == 0:
                i = gpsimd.tensor_mul(ytile, yv[:], sig[:, j % 4, :])
            else:
                i = gpsimd.tensor_mul(ytile, yv[rev0], sig[:, j % 4, :])
            ctx.GP.inc(i, 1)
        ctx.gps(_)
        jgp_rel[J] = ctx.GP.n
    scr = {
        "x16": yb16[:, :, 0:CH],
        "xsq": yb16[:, :, CH:2 * CH],
        "srow": ctx.sb("wkv_srow", [1, CH], F32),
        "rstd": ctx.sb("wkv_rstd", [1, CH], F32),
        "ab16": ctx.sb("wkv_ab16", [1, 2 * CH], BF16),
    }

    def _(sync):
        ctx.LD.inc(sync.dma_start(out=ucol2[:, 0, :], in_=aps["tf"]), 16)
        ctx.LD.inc(sync.dma_start(out=ucol2[:, 1, :], in_=aps["tfr"]), 16)
        ctx.LD.inc(sync.dma_start(out=lam2[:, 0, :], in_=aps["td"]), 16)
        ctx.LD.inc(sync.dma_start(out=lam2[:, 1, :], in_=aps["tdr"]), 16)
    ctx.sync(_)

    def _(scalar):
        ctx.LD.wait(scalar)
        ctx.SE.inc(scalar.activation(lam2[:, :, :], lam2[:, :, :], AF.Exp))
    ctx.sca(_)

    def _(vector):
        ctx.SE.wait(vector)
        vector.memset(minus1[:], -1.0)
        ctx.VE.inc(vector.tensor_scalar_mul(lam2[:, :, :], lam2[:, :, :], -1.0))
        ctx.VE.wait(vector)
    ctx.vec(_)

    def _(scalar):
        ctx.VE.wait(scalar)
        ctx.SE.inc(scalar.activation(lam2[:, :, :], lam2[:, :, :], AF.Exp))
    ctx.sca(_)

    xb_release = {}
    for b in range(BC):
        for dr in range(2):
            with contextlib.ExitStack() as hes:
                rkvh = hes.enter_context(
                    nc.sbuf_tensor(f"rkvh_{b}_{dr}", [128, DT, 3072], BF16))
                stg = hes.enter_context(
                    nc.sbuf_tensor(f"rkvstg_{b}_{dr}", [128, DT, 128], F32))
                for sl in range(24):
                    def _(sync, sl=sl, b=b, dr=dr):
                        ctx.VE.wait(sync)
                        if sl == 0:
                            ctx.TE.wait(sync)
                            ctx.GP.wait(sync)
                        ctx.LD.inc(sync.dma_start(
                            out=stg[:],
                            in_=aps["rkvt"].rearrange("(dt p) h -> p dt h", p=128)[
                                :, :, dr * 3072 + sl * 128:dr * 3072 + (sl + 1) * 128]), 16)
                    ctx.sync(_)

                    def _(vector, sl=sl):
                        ctx.LD.wait(vector)
                        ctx.VE.inc(vector.tensor_copy(
                            rkvh[:, :, sl * 128:(sl + 1) * 128], stg[:]))
                    ctx.vec(_)

                def _(vector):
                    vector.memset(carN[:], 0.0)
                    ctx.VE.inc(vector.memset(carD[:], 0.0))
                ctx.vec(_)

                lam_j = lam2[:, dr, :]
                u_j = ucol2[:, dr, :]
                jse_rel = {}
                jve_rel = {}
                ln_rel = {0: 0}
                for ci in range(NSC):
                    c = (NSC - 1 - ci) if dr else ci
                    tc = c * CH
                    gc = b * NSC + c

                    if dr == 0:
                        def _(sync, gc=gc):
                            if gc > 0:
                                ctx.VE.wait(sync, xb_release.get(gc - 1, 0))
                            ctx.LD.inc(sync.dma_start(
                                out=wkv_xc[:], in_=x_dram_chunk(x_in, gc)), 16)
                        ctx.sync(_)
                        ld_at = ctx.LD.n

                        def _(vector, ld_at=ld_at):
                            vector.wait_ge(ctx.LD.h, ld_at)
                        ctx.vec(_)
                        emit_ln(ctx, sh, wkv_xc, ln16[:, :, tc:tc + CH], scr)
                        xb_release[gc] = ctx.VE.n
                        ln_rel[0] = ctx.VE.n

                    for j in range(DT):
                        b0 = 3 * (j % 2)
                        pr, pk, pv = ctx.pb[b0], ctx.pb[b0 + 1], ctx.pb[b0 + 2]
                        J = ci * DT + j

                        def _(tensor, j=j, pr=pr, pk=pk, pv=pv, tc=tc, J=J):
                            if J >= 2:
                                ctx.SE.wait(tensor, jse_rel[J - 2])
                                ctx.VE.wait(tensor, max(jve_rel[J - 2], ln_rel[0]))
                            else:
                                ctx.SE.wait(tensor)
                                ctx.VE.wait(tensor)
                            for dt_ in range(DT):
                                tensor.matmul(pr[:, 0:CH],
                                              rkvh[:, dt_, j * 128:(j + 1) * 128],
                                              ln16[:, dt_, tc:tc + CH],
                                              start=(dt_ == 0), stop=(dt_ == DT - 1))
                            for dt_ in range(DT):
                                tensor.matmul(pk[:, 0:CH],
                                              rkvh[:, dt_, 1024 + j * 128:1024 + (j + 1) * 128],
                                              ln16[:, dt_, tc:tc + CH],
                                              start=(dt_ == 0), stop=(dt_ == DT - 1))
                            i = None
                            for dt_ in range(DT):
                                i = tensor.matmul(pv[:, 0:CH],
                                                  rkvh[:, dt_, 2048 + j * 128:2048 + (j + 1) * 128],
                                                  ln16[:, dt_, tc:tc + CH],
                                                  start=(dt_ == 0), stop=(dt_ == DT - 1))
                            ctx.TE.inc(i)
                        ctx.ten(_)

                        def _(scalar, j=j, pr=pr, pk=pk, J=J):
                            ctx.TE.wait(scalar)
                            if J - 4 in jgp_rel:
                                scalar.wait_ge(ctx.GP.h, jgp_rel[J - 4])
                            scalar.activation(sig[:, j % 4, :], pr[:, 0:CH], AF.Sigmoid)
                            scalar.activation(eu2[:, j % 2, :], pk[:, 0:CH], AF.Exp,
                                              bias=u_j[:, j:j + 1])
                            ctx.SE.inc(scalar.activation(ek2[:, j % 2, :],
                                                         pk[:, 0:CH], AF.Exp))
                        ctx.sca(_)
                        jse_rel[J] = ctx.SE.n

                        if pend[0] is not None:
                            emit_y(pend[0])
                            pend[0] = None

                        rev = (slice(None), slice(None, None, -1))

                        def _(vector, j=j, pv=pv, dr=dr, J=J):
                            ctx.SE.wait(vector)
                            if J - 2 in jgp_rel:
                                vector.wait_ge(ctx.GP.h, jgp_rel[J - 2])
                            eu = eu2[:, j % 2, :]
                            ek = ek2[:, j % 2, :]
                            nsum = nsum2[:, j % 2, :]
                            dsum = dsum2[:, j % 2, :]
                            vector.tensor_mul(ekv[:], ek, pv[:, 0:CH])
                            ctx.vs(vector, vector.tensor_mul(euv[:], eu, pv[:, 0:CH]))
                            vector.tensor_copy(num[:, 0:1], carN[:, j:j + 1])
                            vector.tensor_copy(den[:, 0:1], carD[:, j:j + 1])
                            lb = lam_j[:, j:j + 1].broadcast_to((128, CH))
                            if dr == 0:
                                vector.tensor_tensor_scan(
                                    num[:, 1:CH + 1], lb, ekv[:], carN[:, j:j + 1],
                                    ALU.mult, ALU.add)
                                ctx.vs(vector, vector.tensor_tensor_scan(
                                    den[:, 1:CH + 1], lb, ek, carD[:, j:j + 1],
                                    ALU.mult, ALU.add))
                                vector.tensor_tensor(nsum[:], num[:, 0:CH],
                                                     euv[:], ALU.add)
                                vector.tensor_tensor(dsum[:], den[:, 0:CH],
                                                     eu, ALU.add)
                                vector.tensor_copy(carN[:, j:j + 1], num[:, CH:CH + 1])
                                i = vector.tensor_copy(carD[:, j:j + 1], den[:, CH:CH + 1])
                            else:
                                vector.tensor_tensor_scan(
                                    num[:, 1:CH + 1], lb, ekv[rev], carN[:, j:j + 1],
                                    ALU.mult, ALU.add)
                                ctx.vs(vector, vector.tensor_tensor_scan(
                                    den[:, 1:CH + 1], lb, ek[rev], carD[:, j:j + 1],
                                    ALU.mult, ALU.add))
                                vector.tensor_tensor(nsum[:], num[:, 0:CH],
                                                     euv[:, CH - 1::-1], ALU.add)
                                vector.tensor_tensor(dsum[:], den[:, 0:CH],
                                                     eu[:, CH - 1::-1], ALU.add)
                                vector.tensor_copy(carN[:, j:j + 1], num[:, CH:CH + 1])
                                i = vector.tensor_copy(carD[:, j:j + 1], den[:, CH:CH + 1])
                            ctx.VE.inc(i)
                        ctx.vec(_)
                        jve_rel[J] = ctx.VE.n

                        def _(scalar, j=j, J=J):
                            ctx.VE.wait(scalar)
                            if J - 2 in jgp_rel:
                                scalar.wait_ge(ctx.GP.h, jgp_rel[J - 2])
                            ds = dsum2[:, j % 2, :]
                            scalar.activation(ds, ds, AF.Ln)
                            ctx.SE.inc(scalar.activation(ds, ds, AF.Exp,
                                                         scale=minus1[:, 0:1]))
                        ctx.sca(_)
                        pend[0] = dict(j=j, dr=dr, tc=tc, se_at=ctx.SE.n, J=J)


                if pend[0] is not None:
                    emit_y(pend[0])
                    pend[0] = None

        with contextlib.ExitStack() as oes:
            ow16 = oes.enter_context(nc.sbuf_tensor(f"ow16_{b}", [128, 16, D], BF16))
            ostg = oes.enter_context(nc.sbuf_tensor(f"ostg_{b}", [128, 16, 128], F32))
            xo = wkv_xc
            for sl in range(8):
                def _(sync, sl=sl):
                    ctx.VE.wait(sync)
                    if sl == 0:
                        ctx.TE.wait(sync)
                        ctx.GP.wait(sync)
                    ctx.LD.inc(sync.dma_start(
                        out=ostg[:],
                        in_=aps["outwt"].rearrange("(kt p) d -> p kt d", p=128)[
                            :, :, sl * 128:(sl + 1) * 128]), 16)
                ctx.sync(_)

                def _(vector, sl=sl):
                    ctx.LD.wait(vector)
                    ctx.VE.inc(vector.tensor_copy(
                        ow16[:, :, sl * 128:(sl + 1) * 128], ostg[:]))
                ctx.vec(_)

            for cc in range(NSC):
                gc = b * NSC + cc
                tc = cc * CH

                def _(sync, gc=gc):
                    ctx.VE.wait(sync)
                    ctx.GP.wait(sync)
                    ctx.LD.inc(sync.dma_start(out=xo[:], in_=x_dram_chunk(x_in, gc)), 16)
                ctx.sync(_)
                ld_at = ctx.LD.n

                def _(tensor, tc=tc, ld_at=ld_at):
                    ctx.VE.wait(tensor)
                    ctx.GP.wait(tensor)
                    i = None
                    for md in range(DT):
                        for kt in range(DT):
                            tensor.matmul(ctx.pb[md][:, 0:CH],
                                          ow16[:, kt, md * 128:(md + 1) * 128],
                                          yf16[:, kt, tc:tc + CH],
                                          start=(kt == 0), stop=False)
                        for kt in range(DT):
                            i = tensor.matmul(ctx.pb[md][:, 0:CH],
                                              ow16[:, 8 + kt, md * 128:(md + 1) * 128],
                                              yb16[:, kt, tc:tc + CH],
                                              start=False, stop=(kt == DT - 1))
                    ctx.TE.inc(i)
                ctx.ten(_)

                def _(vector, ld_at=ld_at):
                    vector.wait_ge(ctx.LD.h, ld_at)
                    ctx.TE.wait(vector)
                    i = None
                    for md in range(DT):
                        i = vector.tensor_tensor(xo[:, md, :], ctx.pb[md][:, 0:CH],
                                                 xo[:, md, :], ALU.add)
                    ctx.VE.inc(i)
                ctx.vec(_)

                def _(gpsimd, gc=gc):
                    ctx.VE.wait(gpsimd)
                    ctx.GP.inc(gpsimd.dma_start(out=x_dram_chunk(x_out, gc),
                                                in_=xo[:]), 16)
                ctx.gps(_)

    def _(sync):
        ctx.GP.wait(sync)
    ctx.sync(_)


def build_l1attn_program():
    nc = bass.Bass("TRN2", target_bir_lowering=False, debug=False)
    x_in = nc.dram_tensor("x", [D, TOK], F32, kind="ExternalInput").ap()
    rkvt = nc.dram_tensor("rkvt", [D, 6144], F32, kind="ExternalInput").ap()
    outwt = nc.dram_tensor("outwt", [2048, D], F32, kind="ExternalInput").ap()
    td = nc.dram_tensor("td", [128, 8], F32, kind="ExternalInput").ap()
    tf = nc.dram_tensor("tf", [128, 8], F32, kind="ExternalInput").ap()
    tdr = nc.dram_tensor("tdr", [128, 8], F32, kind="ExternalInput").ap()
    tfr = nc.dram_tensor("tfr", [128, 8], F32, kind="ExternalInput").ap()
    x_out = nc.dram_tensor("y", [D, TOK], F32, kind="ExternalOutput").ap()
    with contextlib.ExitStack() as es:
        ctx = make_ctx(es, nc)
        sh = Shared(ctx)
        emit_l1attn(ctx, sh, dict(x_in=x_in, x_out=x_out, rkvt=rkvt, outwt=outwt,
                                  td=td, tf=tf, tdr=tdr, tfr=tfr))
    return nc



MG = 1024
NMG = TOK // MG
INTER = 2048
NMP = 16


def emit_moe_preconvert(ctx, aps):
    nc = ctx.nc
    w1s, w2s = aps["w1s"], aps["w2s"]
    with contextlib.ExitStack() as pes:
        stg = [pes.enter_context(nc.sbuf_tensor(f"moepc_stg{i}", [128, 2, DT, 128], F32))
               for i in range(3)]
        cv = [pes.enter_context(nc.sbuf_tensor(f"moepc_cv{i}", [128, 2, DT, 128], BF16))
              for i in range(3)]
        stg2 = [pes.enter_context(nc.sbuf_tensor(f"moepc_stg2{i}", [128, 4, 1024], F32))
                for i in range(2)]
        cv2 = [pes.enter_context(nc.sbuf_tensor(f"moepc_cv2{i}", [128, 4, 1024], BF16))
               for i in range(2)]
        gp_rel = {}
        ve_relp = {}
        it = 0
        for e in range(8):
            for mp in range(NMP):
                s_, c_ = stg[it % 3], cv[it % 3]

                def _(sync, e=e, mp=mp, s_=s_, it=it):
                    if it >= 3:
                        ctx.GP.wait(sync, gp_rel[it - 3])
                        ctx.VE.wait(sync, ve_relp[it - 3])
                    src_ = aps["w1t"][e].rearrange("(dt p) c -> p dt c", p=128)
                    g = src_[:, :, mp * 128:(mp + 1) * 128].unsqueeze(1)
                    u = src_[:, :, INTER + mp * 128:INTER + (mp + 1) * 128].unsqueeze(1)
                    ctx.LD.inc(sync.dma_start(out=s_[:, 0:1, :, :], in_=g), 16)
                    ctx.LD.inc(sync.dma_start(out=s_[:, 1:2, :, :], in_=u), 16)
                ctx.sync(_)

                def _(vector, s_=s_, c_=c_):
                    ctx.LD.wait(vector)
                    ctx.VE.inc(vector.tensor_copy(c_[:], s_[:]))
                ctx.vec(_)
                ve_relp[it] = ctx.VE.n

                def _(gpsimd, e=e, mp=mp, c_=c_):
                    ctx.VE.wait(gpsimd)
                    ctx.GP.inc(gpsimd.dma_start(
                        out=w1s[e, mp],
                        in_=c_.ap().rearrange("p a b c -> p (a b c)")), 16)
                ctx.gps(_)
                gp_rel[it] = ctx.GP.n
                it += 1
        it2 = 0
        gp_rel2 = {}
        ve_rel2 = {}
        for e in range(8):
            for half in range(4):
                s_, c_ = stg2[it2 % 2], cv2[it2 % 2]

                def _(sync, e=e, half=half, s_=s_, it2=it2):
                    if it2 >= 2:
                        ctx.GP.wait(sync, gp_rel2[it2 - 2])
                        ctx.VE.wait(sync, ve_rel2[it2 - 2])
                    src_ = aps["w2t"][e].rearrange("(kt p) d -> p kt d", p=128)
                    ctx.LD.inc(sync.dma_start(
                        out=s_[:], in_=src_[:, half * 4:(half + 1) * 4, :]), 16)
                ctx.sync(_)

                def _(vector, s_=s_, c_=c_):
                    ctx.LD.wait(vector)
                    ctx.VE.inc(vector.tensor_copy(c_[:], s_[:]))
                ctx.vec(_)
                ve_rel2[it2] = ctx.VE.n

                def _(gpsimd, e=e, half=half, c_=c_):
                    ctx.VE.wait(gpsimd)
                    ctx.GP.inc(gpsimd.dma_start(
                        out=w2s[e, :, half * 4096:(half + 1) * 4096],
                        in_=c_.ap().rearrange("p a b -> p (a b)")), 16)
                ctx.gps(_)
                gp_rel2[it2] = ctx.GP.n
                it2 += 1


def emit_l1moe(ctx, sh, aps, scale):
    nc = ctx.nc
    x_in, x_out = aps["x_in"], aps["x_out"]

    acc = ctx.sb("moe_acc", [128, DT, MG], F32)
    Y16 = ctx.sb("moe_y", [128, NMP, MG], BF16)
    ln16 = ctx.sb("moe_ln", [128, DT, MG], BF16)
    w2sb2 = [ctx.sb(f"moe_w2_{i}", [128, 16, D], BF16) for i in range(2)]
    wsl = [ctx.sb(f"moe_w1sl{i}", [128, 2, DT, 128], BF16) for i in range(2)]
    xb = Y16.ap()[:, 0:8, :].bitcast(F32)
    cbrow = ctx.sb("moe_cbrow", [1, 8, MG], F32)
    cbbc = ctx.sb("moe_cbbc", [128, 2, CH], F32)
    comb = ctx.sb("moe_comb", [128, MG // 128, 8], F32)
    silu_sb = ctx.sb("moe_silu", [128, 4, CH], F32)
    tmpf = ctx.sb("moe_tmpf", [128, CH], F32)
    rw16 = ctx.sb("moe_rw16", [128, DT, 8], BF16)
    identf = ctx.sb("moe_ident", [128, 128], F32)
    rcol = ctx.sb("moe_rcol", [128, 4], F32)
    scr = {
        "x16": Y16[:, 8:16, 0:CH],
        "xsq": Y16[:, 8:16, CH:2 * CH],
        "srow": ctx.sb("moe_srow", [1, CH], F32),
        "rstd": ctx.sb("moe_rstd", [1, CH], F32),
        "ab16": ctx.sb("moe_ab16", [1, 2 * CH], BF16),
    }

    def _(sync):
        ctx.LD.inc(sync.dma_start(out=identf[:], in_=aps["ident"]), 16)
    ctx.sync(_)

    def _(sync):
        ctx.LD.inc(sync.dma_start(
            out=cbbc.ap().rearrange("p a b -> p (a b)")[:, 0:64].rearrange(
                "p (a b) -> p a b", a=DT),
            in_=aps["routert"].rearrange("(dt p) e -> p dt e", p=128)), 16)
    ctx.sync(_)

    def _(vector):
        ctx.LD.wait(vector)
        ctx.VE.inc(vector.tensor_copy(
            rw16[:], cbbc.ap().rearrange("p a b -> p (a b)")[:, 0:64].rearrange(
                "p (a b) -> p a b", a=DT)))
    ctx.vec(_)

    xb_release = {}
    ws_rel = {}
    w2_rel = {}
    qse_rel = {}
    qve_rel = {}
    moe_ln_rel = {0: 0}
    w2ve_rel = {}
    cb_rel = {0: 0}
    for g in range(NMG):
        t0 = g * MG
        for ch2 in range(2):
            gc = g * 2 + ch2

            def _(sync, gc=gc):
                if gc == 0:
                    ctx.VE.wait(sync)
                    ctx.TE.wait(sync)
                else:
                    ctx.VE.wait(sync, xb_release.get(gc - 1, 0))
                    ctx.GP.wait(sync)
                ctx.LD.inc(sync.dma_start(out=xb[:], in_=x_dram_chunk(x_in, gc)), 16)
            ctx.sync(_)
            ld_at = ctx.LD.n

            def _(vector, ld_at=ld_at):
                vector.wait_ge(ctx.LD.h, ld_at)
            ctx.vec(_)
            emit_ln(ctx, sh, xb, ln16[:, :, ch2 * CH:(ch2 + 1) * CH], scr)
            xb_release[gc] = ctx.VE.n

        def _(tensor):
            ctx.VE.wait(tensor)
            i = None
            for tt in range(MG // 128):
                for dt_ in range(DT):
                    i = tensor.matmul(ctx.pb[6][:, tt * 8:(tt + 1) * 8],
                                      ln16[:, dt_, tt * 128:(tt + 1) * 128],
                                      rw16[:, dt_, :],
                                      start=(dt_ == 0), stop=(dt_ == DT - 1))
            ctx.TE.inc(i)
        ctx.ten(_)

        r3 = lambda ap: ap.rearrange("p (a b) -> p a b", a=8)
        eg3 = r3(silu_sb[:, 0, 0:64])
        probs = r3(silu_sb[:, 0, 64:128])
        msk = r3(silu_sb[:, 0, 128:192])
        ge2 = r3(silu_sb[:, 0, 192:256])
        sr = r3(silu_sb[:, 1, 0:8])
        sr = silu_sb[:, 1, 0:8].rearrange("p (a b) -> p a b", b=1)
        m1r = silu_sb[:, 1, 8:16].rearrange("p (a b) -> p a b", b=1)
        m2r = silu_sb[:, 1, 16:24].rearrange("p (a b) -> p a b", b=1)

        def _(scalar):
            ctx.TE.wait(scalar)
            ctx.SE.inc(scalar.activation(
                silu_sb[:, 0, 0:64], ctx.pb[6][:, 0:64], AF.Exp))
        ctx.sca(_)

        def _(vector):
            ctx.SE.wait(vector)
            ctx.vs(vector, vector.reduce_sum(sr, eg3, mybir.AxisListType.X))
            ctx.vs(vector, vector.reciprocal(sr, sr))
            ctx.vs(vector, vector.tensor_tensor(
                probs, eg3, sr.broadcast_to((128, 8, 8)), ALU.mult))
            ctx.vs(vector, vector.reduce_max(m1r, probs, mybir.AxisListType.X))
            ctx.vs(vector, vector.tensor_tensor(
                msk, probs, m1r.broadcast_to((128, 8, 8)), ALU.is_ge))
            ctx.vs(vector, vector.scalar_tensor_tensor(
                msk, msk, -1e9, probs, ALU.mult, ALU.add))
            ctx.vs(vector, vector.reduce_max(m2r, msk, mybir.AxisListType.X))
            ctx.vs(vector, vector.tensor_tensor(
                ge2, probs, m2r.broadcast_to((128, 8, 8)), ALU.is_ge))
            ctx.VE.inc(vector.tensor_tensor(
                comb.ap().rearrange("p a b -> p a b"), probs, ge2, ALU.mult))
        ctx.vec(_)

        for e in range(8):
            for half in range(MG // 512):
                def _(tensor, e=e, half=half):
                    if e == 0 and half == 0:
                        ctx.VE.wait(tensor)
                    else:
                        ctx.VE.wait(tensor)
                    i = None
                    for q in range(4):
                        tt = half * 4 + q
                        i = tensor.transpose(ctx.pb[7][0:1, q * 128:(q + 1) * 128],
                                             comb[:, tt, e:e + 1], identf[:])
                    ctx.TE.inc(i)
                ctx.ten(_)

                def _(vector, e=e, half=half):
                    ctx.TE.wait(vector)
                    ctx.VE.inc(vector.tensor_copy(
                        cbrow[0:1, e, half * 512:(half + 1) * 512],
                        ctx.pb[7][0:1, 0:512]))
                ctx.vec(_)

        def _(vector):
            ctx.TE.wait(vector)
            ctx.VE.inc(vector.memset(acc[:], 0.0))
        ctx.vec(_)
        moe_ln_rel[0] = ctx.VE.n

        for e in range(8):
            w2sb = w2sb2[e % 2]
            E = g * 8 + e

            def _(scalar, e=e, g=g, w2sb=w2sb, E=E):
                if E - 2 in w2_rel:
                    scalar.wait_ge(ctx.TE.h, w2_rel[E - 2])
                else:
                    ctx.TE.wait(scalar)
                ctx.SD.inc(scalar.dma_start(
                    out=w2sb.ap().rearrange("p a b -> p (a b)"),
                    in_=aps["w2s"][e]), 16)
            ctx.sca(_)
            w2_at = ctx.SD.n

            def _(tensor, e=e, E=E):
                ve_t = moe_ln_rel[0]
                if E - 1 in w2ve_rel:
                    ve_t = max(ve_t, w2ve_rel[E - 1])
                    ctx.VE.wait(tensor, ve_t)
                else:
                    ctx.VE.wait(tensor)
                tensor.matmul(ctx.pb[6][:, 0:CH], sh.ones_row32[:],
                              cbrow[0:1, e, 0:CH], start=True, stop=True)
                ctx.TE.inc(tensor.matmul(ctx.pb[7][:, 0:CH], sh.ones_row32[:],
                                         cbrow[0:1, e, CH:2 * CH],
                                         start=True, stop=True))
            ctx.ten(_)

            def _(vector):
                ctx.TE.wait(vector)
                vector.tensor_copy(cbbc[:, 0, :], ctx.pb[6][:, 0:CH])
                ctx.VE.inc(vector.tensor_copy(cbbc[:, 1, :], ctx.pb[7][:, 0:CH]))
            ctx.vec(_)
            cb_rel[0] = ctx.VE.n

            te_at = {}
            for mp in range(NMP):
                ws = wsl[mp % 2]
                F = E * NMP + mp

                def _(sync, e=e, mp=mp, ws=ws, F=F):
                    if F - 2 in ws_rel:
                        ctx.TE.wait(sync, ws_rel[F - 2])
                    else:
                        ctx.TE.wait(sync)
                    ctx.LD.inc(sync.dma_start(
                        out=ws.ap().rearrange("p a b c -> p (a b c)"),
                        in_=aps["w1s"][e, mp]), 16)
                ctx.sync(_)
                sl_at = ctx.LD.n

                for ch2 in range(2):
                    q = mp * 2 + ch2
                    Q = E * 32 + q
                    bg, bu = ctx.pb[2 * (q % 4)], ctx.pb[2 * (q % 4) + 1]

                    def _(tensor, mp=mp, ch2=ch2, q=q, Q=Q, bg=bg, bu=bu, ws=ws,
                          sl_at=sl_at, e=e):
                        tensor.wait_ge(ctx.LD.h, sl_at)
                        if Q - 4 in qse_rel:
                            ctx.SE.wait(tensor, qse_rel[Q - 4])
                            ctx.VE.wait(tensor, max(qve_rel[Q - 4], moe_ln_rel[0],
                                                    cb_rel[0]))
                        else:
                            ctx.SE.wait(tensor)
                            ctx.VE.wait(tensor)
                        for k in range(DT):
                            tensor.matmul(bg[:, 0:CH], ws[:, 0, k, :],
                                          ln16[:, k, ch2 * CH:(ch2 + 1) * CH],
                                          start=(k == 0), stop=(k == DT - 1))
                        i = None
                        for k in range(DT):
                            i = tensor.matmul(bu[:, 0:CH], ws[:, 1, k, :],
                                              ln16[:, k, ch2 * CH:(ch2 + 1) * CH],
                                              start=(k == 0), stop=(k == DT - 1))
                        ctx.TE.inc(i)
                    ctx.ten(_)
                    te_at[q] = ctx.TE.n

                def _(scalar, mp=mp):
                    for ch2 in range(2):
                        q = mp * 2 + ch2
                        scalar.wait_ge(ctx.TE.h, te_at[q])
                        i = scalar.activation(silu_sb[:, q % 4, :],
                                              ctx.pb[2 * (q % 4)][:, 0:CH], AF.Silu)
                    ctx.SE.inc(i)
                ctx.sca(_)
                qse_rel[E * 32 + mp * 2] = qse_rel[E * 32 + mp * 2 + 1] = ctx.SE.n

                def _(vector, mp=mp):
                    ctx.SE.wait(vector)
                    i = None
                    for ch2 in range(2):
                        q = mp * 2 + ch2
                        bu = ctx.pb[2 * (q % 4) + 1]
                        vector.tensor_tensor(tmpf[:], bu[:, 0:CH],
                                             cbbc[:, ch2, :], ALU.mult)
                        i = vector.tensor_mul(
                            Y16[:, mp, ch2 * CH:(ch2 + 1) * CH],
                            tmpf[:], silu_sb[:, q % 4, :])
                    ctx.VE.inc(i)
                ctx.vec(_)
                qve_rel[E * 32 + mp * 2] = qve_rel[E * 32 + mp * 2 + 1] = ctx.VE.n
                ws_rel[F] = ctx.TE.n

            wv_rel = {}
            for ch2 in range(2):
                for wave in range(4):
                    r = ch2 * 4 + wave
                    bk = 4 + 2 * (r % 2)

                    def _(tensor, ch2=ch2, wave=wave, r=r, bk=bk, w2_at=w2_at,
                          w2sb=w2sb):
                        tensor.wait_ge(ctx.SD.h, w2_at)
                        if r >= 2:
                            ctx.VE.wait(tensor, wv_rel[r - 2])
                        else:
                            t = qve_rel[E * 32 + 31]
                            if E - 1 in w2ve_rel:
                                t = max(t, w2ve_rel[E - 1])
                                ctx.VE.wait(tensor, t)
                            else:
                                ctx.VE.wait(tensor)
                        i = None
                        for m in range(2):
                            md = wave * 2 + m
                            for kt in range(16):
                                i = tensor.matmul(
                                    ctx.pb[bk + m][:, 0:CH],
                                    w2sb[:, kt, md * 128:(md + 1) * 128],
                                    Y16[:, kt, ch2 * CH:(ch2 + 1) * CH],
                                    start=(kt == 0), stop=(kt == 15))
                        ctx.TE.inc(i)
                    ctx.ten(_)

                    def _(vector, ch2=ch2, wave=wave, bk=bk):
                        ctx.TE.wait(vector)
                        i = None
                        for m in range(2):
                            md = wave * 2 + m
                            sl_ = acc[:, md, ch2 * CH:(ch2 + 1) * CH]
                            i = vector.tensor_tensor(sl_, ctx.pb[bk + m][:, 0:CH],
                                                     sl_, ALU.add)
                        ctx.VE.inc(i)
                    ctx.vec(_)
                    wv_rel[r] = ctx.VE.n
            w2_rel[E] = ctx.TE.n
            w2ve_rel[E] = ctx.VE.n

        for ch2 in range(2):
            gc = g * 2 + ch2

            def _(sync, gc=gc):
                ctx.VE.wait(sync)
                ctx.GP.wait(sync)
                ctx.LD.inc(sync.dma_start(out=xb[:], in_=x_dram_chunk(x_in, gc)), 16)
            ctx.sync(_)
            ld_at = ctx.LD.n

            def _(vector, ch2=ch2, ld_at=ld_at):
                vector.wait_ge(ctx.LD.h, ld_at)
                i = None
                for md in range(DT):
                    i = vector.scalar_tensor_tensor(
                        xb[:, md, :], acc[:, md, ch2 * CH:(ch2 + 1) * CH], scale,
                        xb[:, md, :], ALU.mult, ALU.add)
                ctx.VE.inc(i)
            ctx.vec(_)

            def _(gpsimd, gc=gc):
                ctx.VE.wait(gpsimd)
                ctx.GP.inc(gpsimd.dma_start(out=x_dram_chunk(x_out, gc), in_=xb[:]), 16)
            ctx.gps(_)

    def _(sync):
        ctx.GP.wait(sync)
    ctx.sync(_)


def build_l1moe_program():
    nc = bass.Bass("TRN2", target_bir_lowering=False, debug=False)
    x_in = nc.dram_tensor("x", [D, TOK], F32, kind="ExternalInput").ap()
    w1t = nc.dram_tensor("w1t", [8, D, 2 * INTER], F32, kind="ExternalInput").ap()
    w2t = nc.dram_tensor("w2t", [8, INTER, D], F32, kind="ExternalInput").ap()
    routert = nc.dram_tensor("routert", [D, 8], F32, kind="ExternalInput").ap()
    ident = nc.dram_tensor("ident", [128, 128], F32, kind="ExternalInput").ap()
    x_out = nc.dram_tensor("y", [D, TOK], F32, kind="ExternalOutput").ap()
    w1s = nc.dram_tensor("w1s", [8, NMP, 128, 2048], BF16, kind="Internal").ap()
    w2s = nc.dram_tensor("w2s", [8, 128, 16384], BF16, kind="Internal").ap()
    with contextlib.ExitStack() as es:
        ctx = make_ctx(es, nc)
        sh = Shared(ctx)
        emit_moe_preconvert(ctx, dict(w1t=w1t, w2t=w2t, w1s=w1s, w2s=w2s))
        emit_l1moe(ctx, sh, dict(x_in=x_in, x_out=x_out, w1s=w1s, w2s=w2s,
                                 routert=routert, ident=ident), 0.5)
    return nc



def build_fused_program():
    nc = bass.Bass("TRN2", target_bir_lowering=False, debug=False)
    x_in = nc.dram_tensor("x", [D, TOK], F32, kind="ExternalInput").ap()
    wg = nc.dram_tensor("wg", [D, FFN_H], I8, kind="ExternalInput").ap()
    wu = nc.dram_tensor("wu", [D, FFN_H], I8, kind="ExternalInput").ap()
    wd = nc.dram_tensor("wd", [FFN_H, D], I8, kind="ExternalInput").ap()
    bw = nc.dram_tensor("bw", [D, D], F32, kind="ExternalInput").ap()
    cwt = nc.dram_tensor("cwt", [D, D], F32, kind="ExternalInput").ap()
    a512 = nc.dram_tensor("a512", [128, 4], F32, kind="ExternalInput").ap()
    dvec = nc.dram_tensor("dvec", [128, 8], F32, kind="ExternalInput").ap()
    fret = nc.dram_tensor("fret", [D, 513], F32, kind="ExternalInput").ap()
    fimt = nc.dram_tensor("fimt", [D, 513], F32, kind="ExternalInput").ap()
    finvre = nc.dram_tensor("finvre", [D, 513], F32, kind="ExternalInput").ap()
    finvim = nc.dram_tensor("finvim", [D, 513], F32, kind="ExternalInput").ap()
    msint = nc.dram_tensor("msint", [512, 513], F32, kind="ExternalInput").ap()
    rkvt = nc.dram_tensor("rkvt", [D, 6144], F32, kind="ExternalInput").ap()
    outwt = nc.dram_tensor("outwt", [2048, D], F32, kind="ExternalInput").ap()
    td = nc.dram_tensor("td", [128, 8], F32, kind="ExternalInput").ap()
    tf = nc.dram_tensor("tf", [128, 8], F32, kind="ExternalInput").ap()
    tdr = nc.dram_tensor("tdr", [128, 8], F32, kind="ExternalInput").ap()
    tfr = nc.dram_tensor("tfr", [128, 8], F32, kind="ExternalInput").ap()
    w1t = nc.dram_tensor("w1t", [8, D, 2 * INTER], F32, kind="ExternalInput").ap()
    w2t = nc.dram_tensor("w2t", [8, INTER, D], F32, kind="ExternalInput").ap()
    routert = nc.dram_tensor("routert", [D, 8], F32, kind="ExternalInput").ap()
    ident = nc.dram_tensor("ident", [128, 128], F32, kind="ExternalInput").ap()
    y_out = nc.dram_tensor("y", [D, TOK], F32, kind="ExternalOutput").ap()
    xa = nc.dram_tensor("xa", [D, TOK], F32, kind="Internal").ap()
    xb_ = nc.dram_tensor("xb", [D, TOK], F32, kind="Internal").ap()
    w1s = nc.dram_tensor("w1s", [8, NMP, 128, 2048], BF16, kind="Internal").ap()
    w2s = nc.dram_tensor("w2s", [8, 128, 16384], BF16, kind="Internal").ap()

    ssm = dict(bw=bw, cwt=cwt, a512=a512, dvec=dvec, fret=fret, fimt=fimt,
               finvre=finvre, finvim=finvim, msint=msint)
    wkv = dict(rkvt=rkvt, outwt=outwt, td=td, tf=tf, tdr=tdr, tfr=tfr)
    moe = dict(w1s=w1s, w2s=w2s, routert=routert, ident=ident)

    with contextlib.ExitStack() as es:
        ctx = make_ctx(es, nc)
        sh = Shared(ctx)

        def phase(emit, *args):
            with contextlib.ExitStack() as pes:
                ctx.cur_es = pes
                emit(*args)
            ctx.cur_es = None
            phase_barrier(ctx)

        phase(emit_moe_preconvert, ctx, dict(w1t=w1t, w2t=w2t, w1s=w1s, w2s=w2s))
        phase(emit_l0ffn, ctx, sh,
              dict(x_in=x_in, x_out=xa, wg=wg, wu=wu, wd=wd), 0.5)
        phase(emit_l0attn, ctx, sh, dict(x_in=xa, x_out=xb_, **ssm))
        phase(emit_l0ffn, ctx, sh,
              dict(x_in=xb_, x_out=xa, wg=wg, wu=wu, wd=wd), 0.5)
        phase(emit_l1moe, ctx, sh, dict(x_in=xa, x_out=xb_, **moe), 0.5)
        phase(emit_l1attn, ctx, sh, dict(x_in=xb_, x_out=xa, **wkv))
        phase(emit_l1moe, ctx, sh, dict(x_in=xa, x_out=y_out, **moe), 0.5)
    return nc



_PROG_CACHE = {}


def _get_prog(name):
    if name not in _PROG_CACHE:
        _PROG_CACHE[name] = {"l0ffn": build_l0ffn_program,
                             "l0attn": build_l0attn_program,
                             "l1attn": build_l1attn_program,
                             "l1moe": build_l1moe_program,
                             "fused": build_fused_program}[name]()
    return _PROG_CACHE[name]


def _col8(a):
    return np.ascontiguousarray(a.reshape(8, 128).T)


def kernel(L0_attn_u, L0_attn_a_params, L0_attn_B_w, L0_attn_C_w, L0_attn_D,
           L0_ffn_w_gate, L0_ffn_w_up, L0_ffn_w_down,
           L1_attn_rkv_w, L1_attn_out_w, L1_attn_time_decay, L1_attn_time_first,
           L1_attn_time_decay_rev, L1_attn_time_first_rev,
           L1_ffn_w1, L1_ffn_w2, L1_ffn_router_w):
    x = np.asarray(L0_attn_u)
    xT = np.ascontiguousarray(np.asarray(x).transpose(2, 0, 1))
    shards = [np.ascontiguousarray(xT[:, 4 * c:4 * (c + 1), :]).reshape(D, TOK)
              for c in range(NCORE)]

    wg = np.ascontiguousarray(_prep_int8(np.asarray(L0_ffn_w_gate)).T)
    wu = np.ascontiguousarray(_prep_int8(np.asarray(L0_ffn_w_up)).T)
    wd = np.ascontiguousarray(_prep_int8(np.asarray(L0_ffn_w_down)).T)
    cst = make_ssm_consts()
    ssm_w = dict(bw=np.ascontiguousarray(np.asarray(L0_attn_B_w)),
                 cwt=np.ascontiguousarray(np.asarray(L0_attn_C_w).T),
                 a512=_col8(np.asarray(L0_attn_a_params).reshape(4, 128).reshape(512))
                 if False else np.ascontiguousarray(
                     np.asarray(L0_attn_a_params).reshape(4, 128).T),
                 dvec=_col8(np.asarray(L0_attn_D)), **cst)
    wkv_w = dict(rkvt=np.ascontiguousarray(np.asarray(L1_attn_rkv_w).T),
                 outwt=np.ascontiguousarray(np.asarray(L1_attn_out_w).T),
                 td=_col8(np.asarray(L1_attn_time_decay)),
                 tf=_col8(np.asarray(L1_attn_time_first)),
                 tdr=_col8(np.asarray(L1_attn_time_decay_rev)),
                 tfr=_col8(np.asarray(L1_attn_time_first_rev)))
    moe_w = dict(w1t=np.ascontiguousarray(np.asarray(L1_ffn_w1).transpose(0, 2, 1)),
                 w2t=np.ascontiguousarray(np.asarray(L1_ffn_w2).transpose(0, 2, 1)),
                 routert=np.ascontiguousarray(np.asarray(L1_ffn_router_w).T),
                 ident=np.eye(128, dtype=np.float32))

    nc = _get_prog("fused")
    weights = {"wg": wg, "wu": wu, "wd": wd, **ssm_w, **wkv_w, **moe_w}
    in_maps = [{"x": shards[c], **weights} for c in range(NCORE)]
    res = run_bass_kernel_spmd(nc, in_maps, core_ids=list(range(NCORE)))
    ys = [res.results[c]["y"] for c in range(NCORE)]

    out = np.empty((32, SEQ, D), dtype=np.float32)
    for c in range(NCORE):
        out[4 * c:4 * (c + 1)] = ys[c].reshape(D, BC, SEQ).transpose(1, 2, 0)
    return out
